# version 16
# baseline (speedup 1.0000x reference)
import numpy as np
import ml_dtypes
from contextlib import ExitStack

import concourse.bass as bass
import concourse.mybir as mybir
from concourse.bass_utils import run_bass_kernel_spmd

F32 = mybir.dt.float32
BF16 = mybir.dt.bfloat16
AF = mybir.ActivationFunctionType
ALU = mybir.AluOpType
AX = mybir.AxisListType

T = 4096
D = 1024
NT = 32
NCH = 8
NH = 16
DH = 64
NG = 8
IA = 1024
IF = 2048
EA = 16
EF = 8

_DEV = False
_PHASES = (0, 1, 2, 4, 3, 5)
_LIM_P2 = NCH
_P4_NEXP = None
_P3_NEXP = None
_P2_PASSB = True
_P2_ROUTER = True
_P2_XASPILL = True
_P2_H2SPILL = True
_P2_BCAST = True
_LIM_P1G = NG
ACT_SET_LNEXP = 6

ENGINES = ("sync", "gpsimd", "vector", "scalar", "tensor")


class Sem:
    __slots__ = ("h", "n", "name")

    def __init__(self, h, name):
        self.h, self.n, self.name = h, 0, name


class Builder:
    def __init__(self, dev=False, phases=_PHASES):
        self.dev = dev
        self.phases = phases
        nc = self.nc = bass.Bass("TRN2", target_bir_lowering=False)
        self.es = ExitStack()
        self.ops = None

        def din(name, shape, dt=F32):
            return nc.dram_tensor(name, list(shape), dt, kind="ExternalInput").ap()

        self.x = din("x", (T, D))
        self.wqT = din("wqT", (D, D)); self.wkT = din("wkT", (D, D))
        self.wvT = din("wvT", (D, D)); self.woT = din("woT", (D, D))
        self.nw1 = din("nw1", (1, D)); self.nw2 = din("nw2", (1, D))
        self.router_wT = din("router_wT", (D, EA)); self.ebias = din("ebias", (1, EA))
        self.sgT = din("sgT", (D, IA)); self.suT = din("suT", (D, IA)); self.sdT = din("sdT", (IA, D))
        self.egT = din("egT", (EA, D, IA)); self.euT = din("euT", (EA, D, IA)); self.edT = din("edT", (EA, IA, D))
        self.gate_wT = din("gate_wT", (D, EF)); self.gbias = din("gbias", (1, EF))
        self.w1T = din("w1T", (EF, D, IF)); self.w3T = din("w3T", (EF, D, IF)); self.w2T = din("w2T", (EF, IF, D))
        self.sw1T = din("sw1T", (D, IF)); self.sw3T = din("sw3T", (D, IF)); self.sw2T = din("sw2T", (IF, D))
        self.cos_c = din("cos_c", (DH, T)); self.sin_c = din("sin_c", (DH, T))
        self.masks_c = din("masks_c", (4 * 128, 512))
        self.ones_f32 = din("ones_f32", (128, 128))

        self.out = nc.dram_tensor("out", [T, D], F32, kind="ExternalOutput").ap()
        sk = "ExternalOutput" if dev else "Internal"
        self.yz_d = nc.dram_tensor("yz_d", [128, NT * D], BF16, kind=sk).ap() if dev else None
        self.hb_d = nc.dram_tensor("hb_d", [128, 16 * 512], BF16, kind=sk).ap() if dev else None
        self.wa_d = nc.dram_tensor("wa_d", [128, 8 * 2048], BF16, kind=sk).ap() if dev else None
        self.xs_d = nc.dram_tensor("xs_d", [128, 8 * 512], BF16, kind=sk).ap() if dev else None
        self.gs_d = nc.dram_tensor("gs_d", [128, 512], BF16, kind=sk).ap() if dev else None
        self.sr_d = nc.dram_tensor("sr_d", [128, NT * D], BF16, kind=sk).ap() if dev else None
        self.xnT = nc.dram_tensor("xnT", [D, T], BF16, kind=sk).ap()
        self.xnrT = nc.dram_tensor("xnrT", [D, T], BF16, kind=sk).ap()
        self.ctxT = nc.dram_tensor("ctxT", [D, T], BF16, kind=sk).ap()
        self.h2nT = nc.dram_tensor("h2nT", [D, T], BF16, kind=sk).ap()
        self.xaT = nc.dram_tensor("xaT", [T, D], BF16, kind=sk).ap()

        self.sems = []
        for nm in ("SL", "SO", "SW", "SX", "ST", "SV", "SS", "SG"):
            s = Sem(self.es.enter_context(nc.semaphore(nm)), nm)
            setattr(self, nm, s)
            self.sems.append(s)

        sb = self.sb
        self.cols = sb("cols", (128, 112), F32)
        self.onesb = sb("onesb", (128, 128), BF16)
        self.onesf = sb("onesf", (128, 128), F32)
        self.ebias_r = sb("ebias_r", (128, EA), F32)
        self.gbias_r = sb("gbias_r", (128, EF), F32)
        self.cwa = sb("cwa", (128, NT * EA), F32)
        self.cwf = sb("cwf", (128, NT * EF), F32)
        self.pA = self.ps("pA", (128, 1024))
        self.pB = self.ps("pB", (128, 1024))
        self.p0 = self.ps("p0", (128, 512))
        self.p1 = self.ps("p1", (128, 512))
        self.p2 = self.ps("p2", (128, 512))
        self.p3 = self.ps("p3", (128, 512))

        self.C_NW1, self.C_NW1Q, self.C_EPS6, self.C_EPS5 = 0, 8, 16, 17
        self.C_NW2 = 60

    def sb(self, name, shape, dt):
        es = self.phase_es if getattr(self, "phase_es", None) is not None else self.es
        return es.enter_context(self.nc.sbuf_tensor(name, list(shape), dt))

    def sb_persist(self, name, shape, dt):
        return self.es.enter_context(self.nc.sbuf_tensor(name, list(shape), dt))

    def ps(self, name, shape):
        return self.es.enter_context(self.nc.psum_tensor(name, list(shape), F32))

    def begin_phase(self):
        self.phase_es = ExitStack()
        self._hwm = {}
        self.ops = {e: [] for e in ENGINES}
        for eng in ENGINES:
            for s in self.sems:
                if s.n > 0:
                    self.wait(eng, s, s.n)

    def end_phase(self):
        ops = self.ops
        with self.nc.Block() as block:
            @block.sync
            def _(e):
                for fn in ops["sync"]:
                    fn(e)

            @block.gpsimd
            def _(e):
                for fn in ops["gpsimd"]:
                    fn(e)

            @block.vector
            def _(e):
                for fn in ops["vector"]:
                    fn(e)

            @block.scalar
            def _(e):
                for fn in ops["scalar"]:
                    fn(e)

            @block.tensor
            def _(e):
                for fn in ops["tensor"]:
                    fn(e)
        self.ops = None
        self.phase_es.close()
        self.phase_es = None

    def emit(self, eng, fn):
        self.ops[eng].append(fn)

    def wait(self, eng, sem, val):
        if not val or val <= 0:
            return
        key = (eng, sem.name)
        if self._hwm.get(key, -1) >= val:
            return
        self._hwm[key] = val
        self.emit(eng, lambda e, s=sem.h, v=val: e.wait_ge(s, v))

    def dma(self, eng, sem, dst, src, **kw):
        sem.n += 16
        v = sem.n
        self.emit(eng, lambda e, d=dst, s=src, sm=sem.h, kw=kw: e.dma_start(d, s, **kw).then_inc(sm, 16))
        return v

    def dmat(self, dst, src):
        self.SX.n += 16
        v = self.SX.n
        self.emit("scalar", lambda e, d=dst, s=src: e.dma_start(d, s, transpose=True).then_inc(self.SX.h, 16))
        return v

    def op(self, eng, sem, f):
        if sem is None:
            self.emit(eng, f)
            return None
        sem.n += 1
        v = sem.n
        self.emit(eng, lambda e, f=f, s=sem.h: f(e).then_inc(s, 1))
        return v

    def drain(self, eng):
        self.emit(eng, lambda e: e.drain())

    def preload(self, set_id):
        pass

    def chunk_load(self, dstbuf, src, c, half_elems=512, eng="sync", sem=None):
        vr = None
        sem = sem or self.SL
        for h in range(2):
            vr = self.dma(eng, sem,
                          dstbuf[:, h * 4 * half_elems:(h + 1) * 4 * half_elems]
                          .rearrange("p (a n) -> p a n", n=half_elems),
                          src.rearrange("(a p) m -> p a m", p=128)[:, 4 * h:4 * h + 4,
                                                                  c * half_elems:(c + 1) * half_elems])
        return vr

    def build(self):
        self.startup()
        if 1 in self.phases:
            self.p1_attention()
        elif 0 in self.phases:
            self.begin_phase()
            self.p0_layernorm()
            self.end_phase()
        if 2 in self.phases:
            self.p2_wo_norm2()
        if 4 in self.phases:
            self.p4_ffn()
        if 3 in self.phases:
            self.p3_amoe()
        if 5 in self.phases:
            self.p5_combine()
        self.es.close()
        return self.nc

    def startup(self):
        self.begin_phase()
        b = self
        b.preload(ACT_SET_LNEXP)
        vof = b.dma("sync", b.SL, b.onesf[:, :], b.ones_f32)
        b.dma("sync", b.SL, b.cols[:, b.C_NW1:b.C_NW1 + 8], b.nw1.rearrange("o (a p) -> p (o a)", p=128),
              allow_slow_non_contiguous=True)
        b.dma("sync", b.SL, b.cols[:, b.C_NW2:b.C_NW2 + 8], b.nw2.rearrange("o (a p) -> p (o a)", p=128),
              allow_slow_non_contiguous=True)
        eb_row = b.sb("eb_row", (1, EA), F32)
        gb_row = b.sb("gb_row", (1, EF), F32)
        b.dma("sync", b.SL, eb_row[0:1, :], b.ebias)
        v = b.dma("sync", b.SL, gb_row[0:1, :], b.gbias)
        eb_hl = b.sb("eb_hl", (1, 2 * EA), BF16)
        gb_hl = b.sb("gb_hl", (1, 2 * EF), BF16)
        b.wait("vector", b.SL, v)
        b.wait("vector", b.SL, vof)
        b.emit("vector", lambda e: e.tensor_copy(b.onesb[:, :], b.onesf[:, :]))
        b.drain("vector")
        b.emit("vector", lambda e: e.tensor_copy(eb_hl[0:1, 0:EA], eb_row[0:1, :]))
        b.emit("vector", lambda e: e.tensor_copy(gb_hl[0:1, 0:EF], gb_row[0:1, :]))
        b.drain("vector")
        b.emit("vector", lambda e: e.tensor_tensor(eb_hl[0:1, EA:2 * EA], eb_row[0:1, :], eb_hl[0:1, 0:EA], op=ALU.subtract))
        vbl = b.op("vector", b.SV, lambda e: e.tensor_tensor(gb_hl[0:1, EF:2 * EF], gb_row[0:1, :], gb_hl[0:1, 0:EF], op=ALU.subtract))
        b.drain("vector")
        b.wait("tensor", b.SV, vbl)
        b.emit("tensor", lambda e: e.matmul(b.pA[:, 0:EA], b.onesb[0:1, 0:128], eb_hl[0:1, 0:EA], start=True, stop=False))
        b.emit("tensor", lambda e: e.matmul(b.pA[:, 0:EA], b.onesb[0:1, 0:128], eb_hl[0:1, EA:2 * EA], start=False, stop=True))
        b.emit("tensor", lambda e: e.matmul(b.pB[:, 0:EF], b.onesb[0:1, 0:128], gb_hl[0:1, 0:EF], start=True, stop=False))
        vbb = b.op("tensor", b.ST, lambda e: e.matmul(b.pB[:, 0:EF], b.onesb[0:1, 0:128], gb_hl[0:1, EF:2 * EF], start=False, stop=True))
        b.wait("scalar", b.ST, vbb)
        b.emit("scalar", lambda e: e.copy(b.ebias_r[:, :], b.pA[:, 0:EA]))
        b.op("scalar", b.SS, lambda e: e.copy(b.gbias_r[:, :], b.pB[:, 0:EF]))
        b.emit("vector", lambda e: e.memset(b.cols[:, b.C_EPS6:b.C_EPS6 + 1], 1e-6))
        b.emit("vector", lambda e: e.memset(b.cols[:, b.C_EPS5:b.C_EPS5 + 1], 1e-5))
        b.drain("vector")
        b.op("vector", b.SV, lambda e: e.tensor_scalar(
            b.cols[:, b.C_NW1Q:b.C_NW1Q + 8], b.cols[:, b.C_NW1:b.C_NW1 + 8], 0.125, None, op0=ALU.mult))
        self.end_phase()

    def p0_layernorm(self):
        b = self
        BT = 4
        xin = [b.sb("p0_x0", (128, BT * D), F32), b.sb("p0_x1", (128, BT * D), F32)]
        sqo = b.sb("p0_sqo", (128, D), F32)
        xn_tm = [b.sb("p0_n0", (128, D), BF16), b.sb("p0_n1", (128, D), BF16)]
        xr_tm = [b.sb("p0_r0", (128, D), BF16), b.sb("p0_r1", (128, D), BF16)]
        stg = [b.sb("p0_s0", (128, 2 * D), BF16), b.sb("p0_s1", (128, 2 * D), BF16)]
        c = b.cols
        CS, CQ, CM, CV, CR, CA, CX = 20, 28, 36, 44, 52, 68, 76
        last = {}
        spillv = self.p0_spillv = {}
        for blk in range(NT // BT):
            bi = blk % 2
            t0 = blk * BT
            if ("blk", bi) in last:
                b.wait("sync", b.SV, last[("blk", bi)])
                b.wait("sync", b.SS, last[("sq", bi)])
            vx = b.dma("sync", b.SL, xin[bi][:, :].rearrange("p (a n) -> p a n", n=D),
                       b.x.rearrange("(a p) n -> p a n", p=128)[:, t0:t0 + BT, :])
            b.wait("vector", b.SL, vx)
            b.wait("scalar", b.SL, vx)
            if "chain" in last:
                b.wait("vector", b.SS, last["chain_s"])
            for j in range(BT):
                b.emit("vector", lambda e, j=j, bi=bi: e.tensor_reduce(
                    c[:, CS + j:CS + j + 1], xin[bi][:, j * D:(j + 1) * D], AX.X, ALU.add))
            b.drain("vector")
            b.emit("vector", lambda e: e.tensor_scalar(c[:, CM:CM + BT], c[:, CS:CS + BT], 1.0 / D, None, op0=ALU.mult))
            b.drain("vector")
            vsq = None
            for j in range(BT):
                vsq = b.op("scalar", b.SS, lambda e, j=j, bi=bi: e.activation(
                    sqo[:, :], xin[bi][:, j * D:(j + 1) * D], AF.Square, accum_out=c[:, CQ + j:CQ + j + 1]))
            last[("sq", bi)] = vsq
            b.wait("vector", b.SS, vsq)
            b.emit("vector", lambda e: e.tensor_tensor(c[:, CA:CA + BT], c[:, CM:CM + BT], c[:, CM:CM + BT], op=ALU.mult))
            b.drain("vector")
            vvar = b.op("vector", b.SV, lambda e: e.scalar_tensor_tensor(
                c[:, CV:CV + BT], c[:, CQ:CQ + BT], 1.0 / D, c[:, CA:CA + BT], op0=ALU.mult, op1=ALU.subtract))
            b.drain("vector")
            b.wait("scalar", b.SV, vvar)
            b.emit("scalar", lambda e: e.activation(c[:, CA:CA + BT], c[:, CV:CV + BT], AF.Ln, bias=c[:, b.C_EPS5:b.C_EPS5 + 1], scale=1.0))
            b.drain("scalar")
            b.emit("scalar", lambda e: e.activation(c[:, CR:CR + BT], c[:, CA:CA + BT], AF.Exp, scale=-0.5))
            b.drain("scalar")
            vs2 = b.op("scalar", b.SS, lambda e: e.activation(c[:, CA:CA + BT], c[:, CR:CR + BT], AF.Square))
            b.wait("vector", b.SS, vs2)
            vv2 = b.op("vector", b.SV, lambda e: e.tensor_tensor(c[:, CA:CA + BT], c[:, CV:CV + BT], c[:, CA:CA + BT], op=ALU.mult))
            b.drain("vector")
            b.wait("scalar", b.SV, vv2)
            b.emit("scalar", lambda e: e.activation(c[:, CA:CA + BT], c[:, CA:CA + BT], AF.Ln, bias=c[:, b.C_EPS6:b.C_EPS6 + 1], scale=1.0))
            b.drain("scalar")
            vs3 = b.op("scalar", b.SS, lambda e: e.activation(c[:, CA:CA + BT], c[:, CA:CA + BT], AF.Exp, scale=-0.5))
            b.wait("vector", b.SS, vs3)
            vch = b.op("vector", b.SV, lambda e: e.tensor_tensor(c[:, CX:CX + BT], c[:, CR:CR + BT], c[:, CA:CA + BT], op=ALU.mult))
            b.drain("vector")
            last["chain"] = True
            last["chain_s"] = vs3
            vxn_last = None
            for j in range(BT):
                t = t0 + j
                pi = t % 2
                if ("tp", pi) in last:
                    b.wait("vector", b.SX, last[("tp", pi)])
                b.emit("vector", lambda e, j=j, bi=bi, pi=pi: e.tensor_scalar(
                    xn_tm[pi][:, :], xin[bi][:, j * D:(j + 1) * D],
                    c[:, CM + j:CM + j + 1], c[:, CR + j:CR + j + 1], op0=ALU.subtract, op1=ALU.mult))
                vxn = b.op("vector", b.SV, lambda e, j=j, bi=bi, pi=pi: e.tensor_scalar(
                    xr_tm[pi][:, :], xin[bi][:, j * D:(j + 1) * D],
                    c[:, CM + j:CM + j + 1], c[:, CX + j:CX + j + 1], op0=ALU.subtract, op1=ALU.mult))
                b.drain("vector")
                vxn_last = vxn
                if ("st", pi) in last:
                    b.wait("scalar", b.SO, last[("st", pi)])
                b.wait("scalar", b.SV, vxn)
                b.dmat(stg[pi][:, 0:D].rearrange("p (a n) -> p a n", n=128), xn_tm[pi][:, :])
                vtp = b.dmat(stg[pi][:, D:2 * D].rearrange("p (a n) -> p a n", n=128), xr_tm[pi][:, :])
                last[("tp", pi)] = vtp
                b.wait("sync", b.SX, vtp)
                for (dst, off) in ((b.xnT, 0), (b.xnrT, D)):
                    for h in range(2):
                        vsp = b.dma("sync", b.SO,
                                    dst.rearrange("(a p) m -> p a m", p=128)[:, 4 * h:4 * h + 4, t * 128:(t + 1) * 128],
                                    stg[pi][:, off + h * 512:off + (h + 1) * 512].rearrange("p (a n) -> p a n", n=128))
                last[("st", pi)] = vsp
                spillv[t] = vsp
            last[("blk", bi)] = vxn_last

    def p1_attention(self):
        b = self
        self.begin_phase()
        tabs = b.sb("p1_tabs", (DH, 2 * T), F32)
        do_p0 = 0 in self.phases
        masks = b.sb("p1_masks", (128, 4 * 512), BF16)
        masksf = b.sb("p1_masksf", (128, 4 * 512), F32)
        qg = b.sb("p1_qg", (128, T), BF16)
        kg = b.sb("p1_kg", (128, T), BF16)
        vg = b.sb("p1_vg", (128, NT * 130), BF16)
        vtg = b.sb("p1_vtg", (128, T), BF16)
        xs = [b.sb("p1_xs0", (128, 8 * 512), BF16), b.sb("p1_xs1", (128, 8 * 512), BF16)]
        wraw = b.sb("p1_wraw", (128, 3 * 1024), BF16)
        weff = b.sb("p1_weff", (128, 3 * 1024), BF16)
        pb = b.sb("p1_pb", (128, 4 * 512), BF16)
        t1s = b.sb("p1_t1", (128, 512), F32)
        t2s = b.sb("p1_t2", (128, 512), F32)
        rbs = b.sb("p1_rb", (64, 512), F32)
        rls = b.sb("p1_rl", (128, 2048), F32)
        rlhl = b.sb("p1_rlhl", (128, 2048), BF16)
        cst = b.sb("p1_cst", (128, 2 * 512), BF16)
        vstg = b.sb("p1_vstg", (128, 2 * 512), BF16)

        dma, emit, op, wait, drain = b.dma, b.emit, b.op, b.wait, b.drain
        b.preload(ACT_SET_LNEXP)
        dma("sync", b.SL, tabs[:, 0:T], b.cos_c)
        dma("sync", b.SL, tabs[:, T:2 * T], b.sin_c)
        v_tabs = dma("sync", b.SL, masksf[:, :].rearrange("p (a n) -> p a n", n=512),
                     b.masks_c.rearrange("(a p) n -> p a n", p=128))
        wait("vector", b.SL, v_tabs)
        emit("vector", lambda e: e.tensor_copy(masks[:, :], masksf[:, :]))
        drain("vector")
        emit("vector", lambda e: e.memset(
            vg[:, :].rearrange("p (t h n) -> p t h n", h=2, n=65)[:, :, :, 64:65], 1.0))
        drain("vector")
        vw0 = None
        for j, W in enumerate((b.wqT, b.wkT, b.wvT)):
            for h in range(2):
                vw0 = dma("gpsimd", b.SW,
                          wraw[:, j * 1024 + h * 512:j * 1024 + (h + 1) * 512].rearrange("p (a n) -> p a n", n=128),
                          W[:, 0:128].rearrange("(a p) n -> p a n", p=128)[:, 4 * h:4 * h + 4, :])
        if do_p0:
            self.p0_layernorm()
        spillv = getattr(self, "p0_spillv", {})

        prev = {}
        for g in range(_LIM_P1G):
            if g > 0:
                wait("gpsimd", b.ST, prev["qkv_done"])
                vw = None
                for j, W in enumerate((b.wqT, b.wkT, b.wvT)):
                    for h in range(2):
                        vw = dma("gpsimd", b.SW,
                                 wraw[:, j * 1024 + h * 512:j * 1024 + (h + 1) * 512].rearrange("p (a n) -> p a n", n=128),
                                 W[:, g * 128:(g + 1) * 128].rearrange("(a p) n -> p a n", p=128)[:, 4 * h:4 * h + 4, :])
            else:
                vw = vw0
            wait("gpsimd", b.SW, vw)
            vfold = None
            for j in range(3):
                ccol = b.C_NW1Q if j == 0 else b.C_NW1
                for dt in range(8):
                    vfold = b.op("gpsimd", b.SG, lambda e, j=j, dt=dt, ccol=ccol: e.tensor_scalar(
                        weff[:, j * 1024 + dt * 128:j * 1024 + (dt + 1) * 128],
                        wraw[:, j * 1024 + dt * 128:j * 1024 + (dt + 1) * 128],
                        b.cols[:, ccol + dt:ccol + dt + 1], None, op0=ALU.mult))
            if "attn_done" in prev:
                wait("vector", b.ST, prev["attn_done"])
                wait("scalar", b.ST, prev["attn_done"])
            ropeq = {}; ropek = {}; vsec = {}; vcpc = {}
            for c in range(NCH):
                ci = c % 2
                if ("xs", ci) in prev:
                    wait("gpsimd", b.ST, prev[("xs", ci)])
                if (4 * c + 3) in spillv:
                    wait("gpsimd", b.SO, spillv[4 * c + 3])
                vxs = b.chunk_load(xs[ci], b.xnrT, c, eng="gpsimd", sem=b.SW)
                wait("tensor", b.SW, vxs)
                wait("tensor", b.SG, vfold)
                if ("q_ps", ci) in prev:
                    wait("tensor", b.SV, prev[("q_ps", ci)])
                if ("k_ps", ci) in prev:
                    wait("tensor", b.SV, prev[("k_ps", ci)])
                if ("v_ps", ci) in prev:
                    wait("tensor", b.SS, prev[("v_ps", ci)])
                if g > 0 and c == 0:
                    wait("tensor", b.SS, b.SS.n)
                    wait("tensor", b.SV, b.SV.n)
                q_ps = (b.p0, b.p1)[ci]; k_ps = (b.p2, b.p3)[ci]
                v_ps = b.pA[:, ci * 512:(ci + 1) * 512]
                for dt in range(8):
                    emit("tensor", lambda e, dt=dt, q_ps=q_ps, ci=ci: e.matmul(
                        q_ps[:, :], weff[:, dt * 128:(dt + 1) * 128],
                        xs[ci][:, dt * 512:(dt + 1) * 512], start=(dt == 0), stop=(dt == 7)))
                for dt in range(8):
                    emit("tensor", lambda e, dt=dt, k_ps=k_ps, ci=ci: e.matmul(
                        k_ps[:, :], weff[:, 1024 + dt * 128:1024 + (dt + 1) * 128],
                        xs[ci][:, dt * 512:(dt + 1) * 512], start=(dt == 0), stop=(dt == 7)))
                vmm = None
                for dt in range(8):
                    vmm = op("tensor", b.ST, lambda e, dt=dt, v_ps=v_ps, ci=ci: e.matmul(
                        v_ps, weff[:, 2048 + dt * 128:2048 + (dt + 1) * 128],
                        xs[ci][:, dt * 512:(dt + 1) * 512], start=(dt == 0), stop=(dt == 7)))
                prev[("xs", ci)] = vmm
                prev["qkv_done"] = vmm
                wait("vector", b.ST, vmm)
                for which, sp, dst in (("q", q_ps, qg), ("k", k_ps, kg)):
                    emit("vector", lambda e, sp=sp, c=c: e.tensor_tensor(
                        t1s[0:64, :], sp[0:64, :], tabs[:, c * 512:(c + 1) * 512], op=ALU.mult))
                    emit("vector", lambda e, sp=sp, c=c: e.tensor_tensor(
                        t1s[64:128, :], sp[64:128, :], tabs[:, c * 512:(c + 1) * 512], op=ALU.mult))
                    emit("vector", lambda e, sp=sp, c=c: e.tensor_tensor(
                        t2s[0:32, :], sp[32:64, :], tabs[0:32, T + c * 512:T + c * 512 + 512], op=ALU.mult))
                    emit("vector", lambda e, sp=sp, c=c: e.tensor_tensor(
                        t2s[32:64, :], sp[0:32, :], tabs[32:64, T + c * 512:T + c * 512 + 512], op=ALU.mult))
                    emit("vector", lambda e, sp=sp, c=c: e.tensor_tensor(
                        t2s[64:96, :], sp[96:128, :], tabs[0:32, T + c * 512:T + c * 512 + 512], op=ALU.mult))
                    emit("vector", lambda e, sp=sp, c=c: e.tensor_tensor(
                        t2s[96:128, :], sp[64:96, :], tabs[32:64, T + c * 512:T + c * 512 + 512], op=ALU.mult))
                    drain("vector")
                    emit("vector", lambda e, d=dst, c=c: e.tensor_tensor(
                        d[0:32, c * 512:(c + 1) * 512], t1s[0:32, :], t2s[0:32, :], op=ALU.subtract))
                    emit("vector", lambda e, d=dst, c=c: e.tensor_tensor(
                        d[32:64, c * 512:(c + 1) * 512], t1s[32:64, :], t2s[32:64, :], op=ALU.add))
                    emit("vector", lambda e, d=dst, c=c: e.tensor_tensor(
                        d[64:96, c * 512:(c + 1) * 512], t1s[64:96, :], t2s[64:96, :], op=ALU.subtract))
                    vlast = op("vector", b.SV, lambda e, d=dst, c=c: e.tensor_tensor(
                        d[96:128, c * 512:(c + 1) * 512], t1s[96:128, :], t2s[96:128, :], op=ALU.add))
                    drain("vector")
                    prev[("q_ps" if which == "q" else "k_ps", ci)] = vlast
                    if which == "q":
                        ropeq[c] = vlast
                    else:
                        ropek[c] = vlast
                wait("scalar", b.ST, vmm)
                vse = op("scalar", b.SS, lambda e, v_ps=v_ps, c=c: e.copy(
                    vtg[:, c * 512:(c + 1) * 512], v_ps))
                prev[("v_ps", ci)] = vse
                vsec[c] = vse

                def v_transpose(vc):
                    si = vc % 2
                    drain("scalar")
                    if ("vstg", si) in prev:
                        wait("scalar", b.SV, prev[("vstg", si)])
                    vtp = b.dmat(vstg[:, si * 512:(si + 1) * 512].rearrange("p (a n) -> p a n", n=128),
                                 vtg[:, vc * 512:(vc + 1) * 512])
                    wait("vector", b.SX, vtp)
                    vcp = None
                    for hh in range(2):
                        vcp = op("vector", b.SV, lambda e, si=si, vc=vc, hh=hh: e.tensor_copy(
                            vg[:, :].rearrange("p (t h n) -> p t h n", h=2, n=65)[:, 4 * vc:4 * vc + 4, hh, 0:64],
                            vstg[:, si * 512:(si + 1) * 512].rearrange("p (a h n) -> p a h n", h=2, n=64)[:, :, hh, :]))
                    drain("vector")
                    prev[("vstg", si)] = vcp
                    vcpc[vc] = vcp

                if c > 0:
                    v_transpose(c - 1)
                if c == NCH - 1:
                    v_transpose(c)
            prev[("sbank", 0)] = (b.SV, ropeq[6])
            prev[("sbank", 1)] = (b.SV, ropeq[7])
            prev[("sbank", 2)] = (b.SV, ropek[6])
            prev[("ctxv", 0, 0)] = (b.SS, vsec[6])
            prev[("ctxv", 0, 1)] = (b.SS, vsec[7])
            sstep = prev.get("sstep", 0)
            pend = None

            def waitv(eng, entry):
                if isinstance(entry, tuple):
                    wait(eng, entry[0], entry[1])
                else:
                    wait(eng, b.SS, entry)

            def plan_evac_part2(pd):
                vctx = None
                for hh in range(2):
                    wait("tensor", b.SV, pd["vhl"][hh])
                    if "rb" in prev:
                        wait("tensor", b.SS, prev["rb"])
                    emit("tensor", lambda e, hh=hh: e.matmul(
                        b.p3[0:64, :], b.onesb[64:65, 0:64], rlhl[64:65, hh * 1024:hh * 1024 + 512],
                        start=True, stop=False))
                    vbc = op("tensor", b.ST, lambda e, hh=hh: e.matmul(
                        b.p3[0:64, :], b.onesb[64:65, 0:64], rlhl[64:65, hh * 1024 + 512:hh * 1024 + 1024],
                        start=False, stop=True))
                    wait("scalar", b.ST, vbc)
                    vrb = op("scalar", b.SS, lambda e: e.copy(rbs[:, :], b.p3[0:64, :]))
                    prev["rb"] = vrb
                    wait("vector", b.SS, vrb)
                    if ("cst", pd["par"]) in prev:
                        wait("vector", b.SO, prev[("cst", pd["par"])])
                    vctx = op("vector", b.SV, lambda e, hh=hh, pd=pd: e.tensor_tensor(
                        cst[hh * 64:(hh + 1) * 64, pd["par"] * 512:pd["par"] * 512 + 512],
                        pd["ctxb"][hh][0:64, :], rbs[:, :], op=ALU.mult))
                    drain("vector")
                    prev[("ctxv", hh, pd["par"])] = vctx
                wait("sync", b.SV, vctx)
                vsp = b.dma("sync", b.SO,
                            b.ctxT[g * 128:(g + 1) * 128, pd["qc"] * 512:(pd["qc"] + 1) * 512],
                            cst[:, pd["par"] * 512:pd["par"] * 512 + 512])
                prev[("cst", pd["par"])] = vsp

            for qc in range(NCH):
                nkt = 4 * qc + 4
                par = qc % 2
                ctxb = {0: b.pA[:, par * 512:(par + 1) * 512],
                        1: b.pB[:, par * 512:(par + 1) * 512]}
                vexp_q = [[None] * nkt for _ in range(2)]
                vpv_last = [None, None]
                for kt in range(nkt + 1):
                    if kt < nkt:
                        for hh in range(2):
                            j = sstep % 3
                            s_ps = (b.p0, b.p1, b.p2)[j]
                            if ("sbank", j) in prev:
                                waitv("tensor", prev[("sbank", j)])
                            wait("tensor", b.SV, ropeq[qc])
                            wait("tensor", b.SV, ropek[kt // 4])
                            vs_mm = op("tensor", b.ST, lambda e, hh=hh, kt=kt, qc=qc, s_ps=s_ps: e.matmul(
                                s_ps[:, :], kg[hh * 64:(hh + 1) * 64, kt * 128:(kt + 1) * 128],
                                qg[hh * 64:(hh + 1) * 64, qc * 512:(qc + 1) * 512],
                                start=True, stop=True, tile_position=(hh * 64, 0)))
                            a = kt - 4 * qc
                            pslot = sstep % 4
                            pbs = pb[:, pslot * 512:(pslot + 1) * 512]
                            if a >= 0:
                                wait("vector", b.ST, vs_mm)
                                vmask = op("vector", b.SV, lambda e, s_ps=s_ps, a=a: e.tensor_tensor(
                                    s_ps[:, :], s_ps[:, :], masks[:, a * 512:(a + 1) * 512], op=ALU.add))
                                wait("scalar", b.SV, vmask)
                            else:
                                wait("scalar", b.ST, vs_mm)
                            if ("pb", pslot) in prev:
                                wait("scalar", b.ST, prev[("pb", pslot)])
                            vexp = op("scalar", b.SS, lambda e, s_ps=s_ps, pbs=pbs: e.activation(
                                pbs, s_ps[:, :], AF.Exp))
                            prev[("sbank", j)] = vexp
                            vexp_q[hh][kt] = (vexp, pslot)
                            sstep += 1
                    if kt == 1 and pend is not None:
                        plan_evac_part2(pend)
                        pend = None
                    if kt >= 1:
                        ktp = kt - 1
                        for hh in range(2):
                            vexp, pslot = vexp_q[hh][ktp]
                            wait("tensor", b.SS, vexp)
                            wait("tensor", b.SV, vcpc[ktp // 4])
                            if ktp == 0 and ("ctxv", hh, par) in prev:
                                ent = prev[("ctxv", hh, par)]
                                if isinstance(ent, tuple):
                                    wait("tensor", ent[0], ent[1])
                                else:
                                    wait("tensor", b.SV, ent)
                            vpv = op("tensor", b.ST, lambda e, ktp=ktp, hh=hh, nkt=nkt, ctxb=ctxb, pslot=pslot: e.matmul(
                                ctxb[hh][0:65, :], vg[:, ktp * 130 + hh * 65:ktp * 130 + hh * 65 + 65],
                                pb[:, pslot * 512:(pslot + 1) * 512],
                                start=(ktp == 0), stop=(ktp == nkt - 1)))
                            prev[("pb", pslot)] = vpv
                            if ktp == nkt - 1:
                                vpv_last[hh] = vpv
                vhls = [None, None]
                for hh in range(2):
                    wait("scalar", b.ST, vpv_last[hh])
                    emit("scalar", lambda e, hh=hh, ctxb=ctxb: e.activation(
                        rls[64:65, hh * 1024:hh * 1024 + 512], ctxb[hh][64:65, :], AF.Ln))
                    drain("scalar")
                    vrl = op("scalar", b.SS, lambda e, hh=hh: e.activation(
                        rls[64:65, hh * 1024 + 512:hh * 1024 + 1024],
                        rls[64:65, hh * 1024:hh * 1024 + 512], AF.Exp, scale=-1.0))
                    wait("vector", b.SS, vrl)
                    emit("vector", lambda e, hh=hh: e.tensor_copy(
                        rlhl[64:65, hh * 1024:hh * 1024 + 512],
                        rls[64:65, hh * 1024 + 512:hh * 1024 + 1024]))
                    drain("vector")
                    vhl = op("vector", b.SV, lambda e, hh=hh: e.tensor_tensor(
                        rlhl[64:65, hh * 1024 + 512:hh * 1024 + 1024],
                        rls[64:65, hh * 1024 + 512:hh * 1024 + 1024],
                        rlhl[64:65, hh * 1024:hh * 1024 + 512], op=ALU.subtract))
                    drain("vector")
                    vhls[hh] = vhl
                pend = dict(qc=qc, par=par, ctxb=ctxb, vhl=vhls)
            plan_evac_part2(pend)
            pend = None
            prev["sstep"] = sstep
            prev["attn_done"] = b.ST.n
        self.end_phase()

    def p2_wo_norm2(self):
        b = self
        self.begin_phase()
        wo_sb = b.sb("p2_wo", (128, 8 * 1024), BF16)
        rw_sb = b.sb("p2_rw", (128, 8 * EA), BF16)
        ctx_s = [b.sb("p2_c0", (128, 8 * 512), BF16), b.sb("p2_c1", (128, 8 * 512), BF16)]
        xn_s = [b.sb("p2_x0", (128, 8 * 512), BF16), b.sb("p2_x1", (128, 8 * 512), BF16)]
        xa_scr = b.sb("p2_xa", (128, 8 * 512), BF16)
        h2_scr = b.sb("p2_h2", (128, 8 * 512), BF16)
        sqt = b.sb("p2_sq", (128, 512), BF16)
        r2row = b.sb("p2_r2", (1, T), F32)
        r2hl = b.sb("p2_r2hl", (1, 1024), BF16)
        tstg = b.sb("p2_ts", (128, 2 * 512), BF16)
        lgt = b.sb("p2_lg", (128, 4 * EA), F32)
        gts = b.sb("p2_gt", (128, 4 * EA), F32)
        mk1 = b.sb("p2_mk1", (128, 4 * EA), F32)
        mk2 = b.sb("p2_mk2", (128, 4 * EA), F32)
        mskd = b.sb("p2_msk", (128, 4 * EA), F32)
        c = b.cols
        m1c = c[:, 40:41]; m2c = c[:, 41:42]; nrm = c[:, 42:43]; inv = c[:, 43:44]

        dma, emit, op, wait, drain = b.dma, b.emit, b.op, b.wait, b.drain
        b.preload(ACT_SET_LNEXP)
        for h in range(2):
            dma("gpsimd", b.SW, wo_sb[:, h * 4096:(h + 1) * 4096].rearrange("p (a n) -> p a n", n=1024),
                b.woT.rearrange("(a p) n -> p a n", p=128)[:, 4 * h:4 * h + 4, :])
        vw = dma("gpsimd", b.SW, rw_sb[:, :].rearrange("p (a n) -> p a n", n=EA),
                 b.router_wT.rearrange("(a p) n -> p a n", p=128))
        prev = {}
        for cc in range(_LIM_P2):
            ci = cc % 2
            if ("ld", ci) in prev:
                wait("sync", b.ST, prev[("ld", ci)])
            b.chunk_load(ctx_s[ci], b.ctxT, cc)
            vld = b.chunk_load(xn_s[ci], b.xnT, cc)
            wait("tensor", b.SL, vld)
            wait("tensor", b.SW, vw)
            if cc >= 1:
                wait("vector", b.ST, prev["routers_done"])
                wait("vector", b.SX, prev["xa_tp_last"])
                wait("vector", b.SW, b.SW.n)
            vss = None
            for et in range(8):
                d_ps = (b.p0, b.p1)[et % 2]
                if ("d_ps", et % 2) in prev:
                    wait("tensor", b.SV, prev[("d_ps", et % 2)])
                vdm = None
                for dt in range(8):
                    vdm = op("tensor", b.ST, lambda e, dt=dt, et=et, d_ps=d_ps, ci=ci: e.matmul(
                        d_ps[:, :], wo_sb[:, dt * 1024 + et * 128:dt * 1024 + (et + 1) * 128],
                        ctx_s[ci][:, dt * 512:(dt + 1) * 512], start=(dt == 0), stop=(dt == 7)))
                wait("vector", b.ST, vdm)
                vxa = op("vector", b.SV, lambda e, et=et, d_ps=d_ps, ci=ci: e.tensor_tensor(
                    xa_scr[:, et * 512:(et + 1) * 512], d_ps[:, :],
                    xn_s[ci][:, et * 512:(et + 1) * 512], op=ALU.add))
                prev[("d_ps", et % 2)] = vxa
                drain("vector")
                vsq = op("vector", b.SV, lambda e, et=et: e.tensor_tensor(
                    sqt[:, :], xa_scr[:, et * 512:(et + 1) * 512],
                    xa_scr[:, et * 512:(et + 1) * 512], op=ALU.mult))
                drain("vector")
                wait("tensor", b.SV, vsq)
                if et == 0 and "r2_done" in prev:
                    wait("tensor", b.SS, prev["r2_done"])
                vss = op("tensor", b.ST, lambda e, et=et: e.matmul(
                    b.p2[0:1, :], b.onesb[:, 0:1], sqt[:, :], start=(et == 0), stop=(et == 7)))
            wait("scalar", b.ST, vss)
            emit("scalar", lambda e, cc=cc: e.activation(
                r2row[0:1, cc * 512:(cc + 1) * 512], b.p2[0:1, :], AF.Ln,
                bias=c[0:1, b.C_EPS6:b.C_EPS6 + 1], scale=1.0 / D))
            drain("scalar")
            vr2 = op("scalar", b.SS, lambda e, cc=cc: e.activation(
                r2row[0:1, cc * 512:(cc + 1) * 512], r2row[0:1, cc * 512:(cc + 1) * 512], AF.Exp, scale=-0.5))
            prev["r2_done"] = vr2
            if not _P2_PASSB:
                prev["routers_done"] = b.ST.n
                prev[("ld", ci)] = b.ST.n
                continue
            wait("vector", b.SS, vr2)
            emit("vector", lambda e, cc=cc: e.tensor_copy(
                r2hl[0:1, 0:512], r2row[0:1, cc * 512:(cc + 1) * 512]))
            drain("vector")
            vhl = op("vector", b.SV, lambda e, cc=cc: e.tensor_tensor(
                r2hl[0:1, 512:1024], r2row[0:1, cc * 512:(cc + 1) * 512], r2hl[0:1, 0:512], op=ALU.subtract))
            drain("vector")
            wait("tensor", b.SV, vhl)
            if "r2b" in prev:
                wait("tensor", b.SV, prev["r2b"])
            emit("tensor", lambda e: e.matmul(
                b.p3[:, :], b.onesb[0:1, 0:128], r2hl[0:1, 0:512], start=True, stop=False))
            vbc = op("tensor", b.ST, lambda e: e.matmul(
                b.p3[:, :], b.onesb[0:1, 0:128], r2hl[0:1, 512:1024], start=False, stop=True))
            wait("vector", b.ST, vbc)
            vh2_last = None
            vtp = None
            for et in range(8):
                if _P2_BCAST:
                    vh2 = op("vector", b.SV, lambda e, et=et: e.scalar_tensor_tensor(
                        h2_scr[:, et * 512:(et + 1) * 512], xa_scr[:, et * 512:(et + 1) * 512],
                        b.cols[:, b.C_NW2 + et:b.C_NW2 + et + 1], b.p3[:, :],
                        op0=ALU.mult, op1=ALU.mult))
                else:
                    vh2 = op("vector", b.SV, lambda e, et=et: e.tensor_copy(
                        h2_scr[:, et * 512:(et + 1) * 512], xa_scr[:, et * 512:(et + 1) * 512]))
                drain("vector")
                vh2_last = vh2
                if _P2_H2SPILL:
                    wait("gpsimd", b.SV, vh2)
                    dma("gpsimd", b.SW, b.h2nT[et * 128:(et + 1) * 128, cc * 512:(cc + 1) * 512],
                        h2_scr[:, et * 512:(et + 1) * 512])
                si = et % 2
                if _P2_XASPILL:
                    if ("tstg", si) in prev:
                        wait("scalar", b.SO, prev[("tstg", si)])
                    vtp = b.dmat(tstg[:, si * 512:(si + 1) * 512].rearrange("p (a n) -> p a n", n=128),
                                 xa_scr[:, et * 512:(et + 1) * 512])
                    wait("sync", b.SX, vtp)
                    xa_ap = bass.AP(b.xaT.tensor, (cc * 512) * D + et * 128,
                                    [[D, 128], [128 * D, 4], [1, 128]])
                    vso = b.dma("sync", b.SO, xa_ap,
                                tstg[:, si * 512:(si + 1) * 512].rearrange("p (a n) -> p a n", n=128))
                    prev[("tstg", si)] = vso
            prev["xa_tp_last"] = vtp if _P2_XASPILL else None
            prev["r2b"] = vh2_last
            if not _P2_ROUTER:
                prev["routers_done"] = b.ST.n
                prev[("ld", ci)] = b.ST.n
                continue
            wait("tensor", b.SV, vh2_last)
            c4 = b.cols
            M1, M2, NR, IV = 20, 24, 28, 32
            vlgs = []
            for tt in range(4):
                ttg = cc * 4 + tt
                rps = b.pB[:, 0:EA] if tt % 2 == 0 else b.pB[:, 512:512 + EA]
                if ("rtA", tt % 2) in prev:
                    wait("tensor", b.SV, prev[("rtA", tt % 2)])
                vrt = None
                for dt in range(8):
                    vrt = op("tensor", b.ST, lambda e, dt=dt, tt=tt, rps=rps: e.matmul(
                        rps, h2_scr[:, dt * 512 + tt * 128:dt * 512 + tt * 128 + 128],
                        rw_sb[:, dt * EA:(dt + 1) * EA], start=(dt == 0), stop=(dt == 7)))
                wait("vector", b.ST, vrt)
                if "sigA" in prev:
                    wait("vector", b.SS, prev["sigA"])
                vlg = op("vector", b.SV, lambda e, tt=tt, rps=rps: e.tensor_tensor(
                    lgt[:, tt * EA:(tt + 1) * EA], rps, b.ebias_r[:, :], op=ALU.add))
                prev[("rtA", tt % 2)] = vlg
                vlgs.append(vlg)
            drain("vector")
            wait("scalar", b.SV, vlgs[-1])
            vsg = op("scalar", b.SS, lambda e: e.activation(gts[:, :], lgt[:, :], AF.Sigmoid))
            prev["sigA"] = vsg
            wait("vector", b.SS, vsg)
            for tt in range(4):
                emit("vector", lambda e, tt=tt: e.tensor_reduce(
                    c4[:, M1 + tt:M1 + tt + 1], gts[:, tt * EA:(tt + 1) * EA], AX.X, ALU.max))
            drain("vector")
            for tt in range(4):
                emit("vector", lambda e, tt=tt: e.tensor_scalar(
                    mk1[:, tt * EA:(tt + 1) * EA], gts[:, tt * EA:(tt + 1) * EA],
                    c4[:, M1 + tt:M1 + tt + 1], None, op0=ALU.is_equal))
            drain("vector")
            for tt in range(4):
                emit("vector", lambda e, tt=tt: e.scalar_tensor_tensor(
                    mskd[:, tt * EA:(tt + 1) * EA], mk1[:, tt * EA:(tt + 1) * EA], -1e30,
                    gts[:, tt * EA:(tt + 1) * EA], op0=ALU.mult, op1=ALU.add))
            drain("vector")
            for tt in range(4):
                emit("vector", lambda e, tt=tt: e.tensor_reduce(
                    c4[:, M2 + tt:M2 + tt + 1], mskd[:, tt * EA:(tt + 1) * EA], AX.X, ALU.max))
            drain("vector")
            for tt in range(4):
                emit("vector", lambda e, tt=tt: e.tensor_scalar(
                    mk2[:, tt * EA:(tt + 1) * EA], mskd[:, tt * EA:(tt + 1) * EA],
                    c4[:, M2 + tt:M2 + tt + 1], None, op0=ALU.is_equal))
            emit("vector", lambda e: e.tensor_tensor(c4[:, NR:NR + 4], c4[:, M1:M1 + 4], c4[:, M2:M2 + 4], op=ALU.add))
            drain("vector")
            emit("vector", lambda e: e.tensor_tensor(mk1[:, :], mk1[:, :], mk2[:, :], op=ALU.add))
            emit("vector", lambda e: e.tensor_scalar(c4[:, NR:NR + 4], c4[:, NR:NR + 4], 1e-9, None, op0=ALU.add))
            drain("vector")
            emit("vector", lambda e: e.tensor_tensor(mk1[:, :], mk1[:, :], gts[:, :], op=ALU.mult))
            emit("vector", lambda e: e.reciprocal(c4[:, IV:IV + 4], c4[:, NR:NR + 4]))
            drain("vector")
            for tt in range(4):
                ttg = cc * 4 + tt
                op("vector", b.SV, lambda e, tt=tt, ttg=ttg: e.tensor_scalar(
                    b.cwa[:, ttg * EA:(ttg + 1) * EA], mk1[:, tt * EA:(tt + 1) * EA],
                    c4[:, IV + tt:IV + tt + 1], None, op0=ALU.mult))
            drain("vector")
            prev["routers_done"] = b.ST.n
            prev[("ld", ci)] = b.ST.n
        self.end_phase()

    def _moe(self, tag, n_exp, inner, stream_dram, wu1, wu3, wdn, su1, su3, sdn,
             acc, cwbuf, cw_w, overwrite_first):
        b = self
        nit = inner // 128
        wA, wB, wC = self.wsA, self.wsB, self.wsC
        xs, hb, gsc = self.ms_xs, self.ms_hb, self.ms_gs
        dma, emit, op, wait, drain = b.dma, b.emit, b.op, b.wait, b.drain
        prev = self.ms_prev

        def load_w(which, e_idx):
            if e_idx == -1:
                up1, up3, dn = su1, su3, sdn
            else:
                up1, up3, dn = wu1[e_idx], wu3[e_idx], wdn[e_idx]
            v = None
            if which == "A" or which == "B":
                W = up1 if which == "A" else up3
                slot = wA if which == "A" else wB
                for h in range(2):
                    v = dma("gpsimd", b.SW,
                            slot[:, h * 4 * inner:(h + 1) * 4 * inner].rearrange("p (a n) -> p a n", n=inner),
                            W.rearrange("(a p) n -> p a n", p=128)[:, 4 * h:4 * h + 4, :])
            else:
                nh = max(1, nit // 4)
                step = nit // nh
                for h in range(nh):
                    v = dma("gpsimd", b.SW,
                            wC[:, h * step * 1024:(h + 1) * step * 1024].rearrange("p (a n) -> p a n", n=1024),
                            dn.rearrange("(a p) n -> p a n", p=128)[:, step * h:step * (h + 1), :])
            return v

        exps = [-1] + list(range(n_exp))
        lim = _P4_NEXP if tag == "p4" else _P3_NEXP
        if lim is not None:
            exps = exps[:lim]
        wait("tensor", b.SV, b.SV.n)
        wait("tensor", b.SS, b.SS.n)
        vA = load_w("A", exps[0]); vB = load_w("B", exps[0]); vC = load_w("C", exps[0])
        for xi, eidx in enumerate(exps):
            for tc in range(NCH):
                ci = tc % 2
                if (tag, "xs", ci) in prev:
                    wait("sync", b.ST, prev[(tag, "xs", ci)])
                vxs = b.chunk_load(xs[ci], stream_dram, tc)
                wait("tensor", b.SL, vxs)
                if tc == 0:
                    wait("tensor", b.SW, vA)
                    wait("tensor", b.SW, vB)
                if (tag, "down_st") in prev:
                    wait("vector", b.ST, prev[(tag, "down_st")])
                for it in range(nit):
                    g_ps = (b.p0, b.p1)[it % 2]
                    u_ps = (b.p2, b.p3)[it % 2]
                    if (tag, "g_ps", it % 2) in prev:
                        wait("tensor", b.SS, prev[(tag, "g_ps", it % 2)])
                    vgm = None
                    for dt in range(8):
                        vgm = op("tensor", b.ST, lambda e, dt=dt, it=it, g_ps=g_ps, ci=ci: e.matmul(
                            g_ps[:, :], wA[:, dt * inner + it * 128:dt * inner + (it + 1) * 128],
                            xs[ci][:, dt * 512:(dt + 1) * 512], start=(dt == 0), stop=(dt == 7)))
                    if (tag, "u_ps", it % 2) in prev:
                        wait("tensor", b.SV, prev[(tag, "u_ps", it % 2)])
                    vum = None
                    for dt in range(8):
                        vum = op("tensor", b.ST, lambda e, dt=dt, it=it, u_ps=u_ps, ci=ci: e.matmul(
                            u_ps[:, :], wB[:, dt * inner + it * 128:dt * inner + (it + 1) * 128],
                            xs[ci][:, dt * 512:(dt + 1) * 512], start=(dt == 0), stop=(dt == 7)))
                    wait("scalar", b.ST, vgm)
                    gs = gsc[it % 2]
                    if (tag, "gs", it % 2) in prev:
                        wait("scalar", b.SV, prev[(tag, "gs", it % 2)])
                    vsi = op("scalar", b.SS, lambda e, g_ps=g_ps, gs=gs: e.activation(
                        gs[:, :], g_ps[:, :], AF.Silu))
                    prev[(tag, "g_ps", it % 2)] = vsi
                    wait("vector", b.ST, vum)
                    wait("vector", b.SS, vsi)
                    vh = op("vector", b.SV, lambda e, it=it, u_ps=u_ps, gs=gs: e.tensor_tensor(
                        hb[:, it * 512:(it + 1) * 512], gs[:, :], u_ps[:, :], op=ALU.mult))
                    drain("vector")
                    prev[(tag, "u_ps", it % 2)] = vh
                    prev[(tag, "gs", it % 2)] = vh
                prev[(tag, "xs", ci)] = b.ST.n
                if self.dev and tag == "p4" and xi == 0 and tc == 0:
                    vtap = b.SV.n
                    b.wait("sync", b.SV, vtap)
                    b.dma("sync", b.SO, self.hb_d, hb[:, 0:16 * 512])
                    b.dma("sync", b.SO, self.wa_d, wA[:, 0:8 * inner])
                    b.dma("sync", b.SO, self.xs_d, xs[ci][:, :])
                    b.dma("sync", b.SO, self.gs_d, gsc[0][:, :])
                if tc == NCH - 1 and xi + 1 < len(exps):
                    wait("gpsimd", b.ST, b.ST.n)
                    vA = load_w("A", exps[xi + 1])
                    vB = load_w("B", exps[xi + 1])
                if tc == 0:
                    wait("tensor", b.SW, vC)
                wait("tensor", b.SV, b.SV.n)
                for tt in range(4):
                    ttg = tc * 4 + tt
                    d_ps = (b.pA, b.pB)[tt % 2]
                    if (tag, "d_ps", tt % 2) in prev:
                        wait("tensor", b.SV, prev[(tag, "d_ps", tt % 2)])
                    vdm = None
                    for dch in range(2):
                        for it in range(nit):
                            vdm = op("tensor", b.ST, lambda e, tt=tt, dch=dch, it=it, d_ps=d_ps: e.matmul(
                                d_ps[:, dch * 512:(dch + 1) * 512],
                                hb[:, it * 512 + tt * 128:it * 512 + tt * 128 + 128],
                                wC[:, it * 1024 + dch * 512:it * 1024 + dch * 512 + 512],
                                start=(it == 0), stop=(it == nit - 1)))
                    wait("vector", b.ST, vdm)
                    accs = acc[:, ttg * D:(ttg + 1) * D]
                    if overwrite_first and xi == 0:
                        vac = op("vector", b.SV, lambda e, d_ps=d_ps, accs=accs: e.tensor_scalar(
                            accs, d_ps[:, :], 1.0, None, op0=ALU.mult))
                    elif eidx == -1:
                        vac = op("vector", b.SV, lambda e, d_ps=d_ps, accs=accs: e.scalar_tensor_tensor(
                            accs, d_ps[:, :], 1.0, accs, op0=ALU.mult, op1=ALU.add))
                    else:
                        cwcol = cwbuf[:, ttg * cw_w + eidx:ttg * cw_w + eidx + 1]
                        vac = op("vector", b.SV, lambda e, d_ps=d_ps, accs=accs, cwcol=cwcol: e.scalar_tensor_tensor(
                            accs, d_ps[:, :], cwcol, accs, op0=ALU.mult, op1=ALU.add))
                    drain("vector")
                    prev[(tag, "d_ps", tt % 2)] = vac
                prev[(tag, "down_st")] = b.ST.n
            if xi + 1 < len(exps):
                wait("gpsimd", b.ST, b.ST.n)
                vC = load_w("C", exps[xi + 1])

    def p4_ffn(self):
        b = self
        self.begin_phase()
        self.yz = b.sb_persist("yz", (128, NT * D), BF16)
        self.wsA = b.sb("p4_wA", (128, 8 * IF), BF16)
        self.wsB = b.sb("p4_wB", (128, 8 * IF), BF16)
        self.wsC = b.sb("p4_wC", (128, 16 * D), BF16)
        self.ms_xs = [b.sb("p4_xs0", (128, 8 * 512), BF16), b.sb("p4_xs1", (128, 8 * 512), BF16)]
        self.ms_hb = b.sb("p4_hb", (128, 16 * 512), BF16)
        self.ms_gs = [b.sb("p4_g0", (128, 512), BF16), b.sb("p4_g1", (128, 512), BF16)]
        self.ms_prev = {}
        gw_sb = b.sb("p4_gw", (128, 8 * EF), BF16)
        exs = b.sb("p4_ex", (128, EF), F32)
        sco = b.sb("p4_sc", (128, EF), F32)
        bsd = b.sb("p4_bs", (128, EF), F32)
        mk1 = b.sb("p4_m1", (128, EF), F32)
        mk2 = b.sb("p4_m2", (128, EF), F32)
        mskd = b.sb("p4_mk", (128, EF), F32)
        c = b.cols
        mxc = c[:, 50:51]; nmx = c[:, 51:52]; smc = c[:, 52:53]; rsc = c[:, 53:54]
        m1c = c[:, 54:55]; m2c = c[:, 55:56]

        dma, emit, op, wait, drain = b.dma, b.emit, b.op, b.wait, b.drain
        vgw = dma("gpsimd", b.SW, gw_sb[:, :].rearrange("p (a n) -> p a n", n=EF),
                  b.gate_wT.rearrange("(a p) n -> p a n", p=128))
        xs = self.ms_xs
        prevR = {}
        for cc in range(NCH):
            ci = cc % 2
            if ("xs", ci) in prevR:
                wait("sync", b.ST, prevR[("xs", ci)])
            vxs = b.chunk_load(xs[ci], b.xnT, cc)
            wait("tensor", b.SL, vxs)
            wait("tensor", b.SW, vgw)
            for tt in range(4):
                ttg = cc * 4 + tt
                if "pB" in prevR:
                    wait("tensor", b.SV, prevR["pBv"])
                    wait("tensor", b.SS, prevR["pBs"])
                vrt = None
                for dt in range(8):
                    vrt = op("tensor", b.ST, lambda e, dt=dt, tt=tt, ci=ci: e.matmul(
                        b.pB[:, 0:EF], xs[ci][:, dt * 512 + tt * 128:dt * 512 + tt * 128 + 128],
                        gw_sb[:, dt * EF:(dt + 1) * EF], start=(dt == 0), stop=(dt == 7)))
                wait("vector", b.ST, vrt)
                emit("vector", lambda e: e.tensor_reduce(mxc, b.pB[:, 0:EF], AX.X, ALU.max))
                drain("vector")
                vnm = op("vector", b.SV, lambda e: e.tensor_scalar(nmx, mxc, -1.0, None, op0=ALU.mult))
                prevR["pB"] = True
                prevR["pBv"] = vnm
                wait("scalar", b.SV, vnm)
                vex = op("scalar", b.SS, lambda e: e.activation(
                    exs[:, :], b.pB[:, 0:EF], AF.Exp, bias=nmx, scale=1.0))
                prevR["pBs"] = vex
                wait("vector", b.SS, vex)
                emit("vector", lambda e: e.tensor_reduce(smc, exs[:, :], AX.X, ALU.add))
                drain("vector")
                emit("vector", lambda e: e.reciprocal(rsc, smc))
                drain("vector")
                emit("vector", lambda e: e.tensor_scalar(sco[:, :], exs[:, :], rsc, None, op0=ALU.mult))
                drain("vector")
                emit("vector", lambda e: e.tensor_tensor(bsd[:, :], sco[:, :], b.gbias_r[:, :], op=ALU.add))
                drain("vector")
                emit("vector", lambda e: e.tensor_reduce(m1c, bsd[:, :], AX.X, ALU.max))
                drain("vector")
                emit("vector", lambda e: e.tensor_scalar(mk1[:, :], bsd[:, :], m1c, None, op0=ALU.is_equal))
                drain("vector")
                emit("vector", lambda e: e.scalar_tensor_tensor(
                    mskd[:, :], mk1[:, :], -1e30, bsd[:, :], op0=ALU.mult, op1=ALU.add))
                drain("vector")
                emit("vector", lambda e: e.tensor_reduce(m2c, mskd[:, :], AX.X, ALU.max))
                drain("vector")
                emit("vector", lambda e: e.tensor_scalar(mk2[:, :], mskd[:, :], m2c, None, op0=ALU.is_equal))
                drain("vector")
                emit("vector", lambda e: e.tensor_tensor(mk1[:, :], mk1[:, :], mk2[:, :], op=ALU.add))
                drain("vector")
                op("vector", b.SV, lambda e, ttg=ttg: e.tensor_tensor(
                    b.cwf[:, ttg * EF:(ttg + 1) * EF], mk1[:, :], sco[:, :], op=ALU.mult))
                drain("vector")
            prevR[("xs", ci)] = b.ST.n
        self.ms_prev[("p4", "xs", 0)] = b.ST.n
        self.ms_prev[("p4", "xs", 1)] = b.ST.n
        self._moe("p4", EF, IF, b.xnT,
                  [self.w1T[e] for e in range(EF)], [self.w3T[e] for e in range(EF)],
                  [self.w2T[e] for e in range(EF)],
                  self.sw1T, self.sw3T, self.sw2T, self.yz, b.cwf, EF, True)
        if self.dev:
            b.wait("sync", b.SV, b.SV.n)
            b.dma("sync", b.SO, self.yz_d, self.yz[:, :])
        self.end_phase()

    def p3_amoe(self):
        b = self
        self.begin_phase()
        self.sr = b.sb_persist("sr", (128, NT * D), BF16)
        self.wsA = b.sb("p3_wA", (128, 8 * IA), BF16)
        self.wsB = b.sb("p3_wB", (128, 8 * IA), BF16)
        self.wsC = b.sb("p3_wC", (128, 8 * D), BF16)
        self.ms_xs = [b.sb("p3_xs0", (128, 8 * 512), BF16), b.sb("p3_xs1", (128, 8 * 512), BF16)]
        self.ms_hb = b.sb("p3_hb", (128, 8 * 512), BF16)
        self.ms_gs = [b.sb("p3_g0", (128, 512), BF16), b.sb("p3_g1", (128, 512), BF16)]
        self.ms_prev = {}
        vin = None
        for q in range(8):
            vin = b.dma("sync", b.SL, self.sr[:, q * 4 * D:(q + 1) * 4 * D].rearrange("p (t n) -> p t n", n=D),
                        b.xaT.rearrange("(t p) n -> p t n", p=128)[:, 4 * q:4 * q + 4, :])
        b.wait("vector", b.SL, vin)
        self._moe("p3", EA, IA, b.h2nT,
                  [self.egT[e] for e in range(EA)], [self.euT[e] for e in range(EA)],
                  [self.edT[e] for e in range(EA)],
                  self.sgT, self.suT, self.sdT, self.sr, b.cwa, EA, False)
        if self.dev:
            b.wait("sync", b.SV, b.SV.n)
            b.dma("sync", b.SO, self.sr_d, self.sr[:, :])
        self.end_phase()

    def p5_combine(self):
        b = self
        self.begin_phase()
        xin = [b.sb("p5_x0", (128, D), F32), b.sb("p5_x1", (128, D), F32)]
        gsc = [b.sb("p5_g0", (128, D), F32), b.sb("p5_g1", (128, D), F32)]
        tsc = [b.sb("p5_t0", (128, D), F32), b.sb("p5_t1", (128, D), F32)]
        osb = [b.sb("p5_o0", (128, D), F32), b.sb("p5_o1", (128, D), F32)]
        dma, emit, op, wait, drain = b.dma, b.emit, b.op, b.wait, b.drain
        prev = {}
        for t in range(NT):
            pi = t % 2
            if ("vo", pi) in prev:
                wait("sync", b.SV, prev[("vo", pi)])
            vx = dma("sync", b.SL, xin[pi][:, :], b.x[t * 128:(t + 1) * 128, :])
            if ("vt", pi) in prev:
                wait("scalar", b.SV, prev[("vt", pi)])
            vg = op("scalar", b.SS, lambda e, t=t, pi=pi: e.activation(
                gsc[pi][:, :], self.yz[:, t * D:(t + 1) * D], AF.Silu))
            wait("vector", b.SS, vg)
            vt = op("vector", b.SV, lambda e, t=t, pi=pi: e.tensor_tensor(
                tsc[pi][:, :], self.sr[:, t * D:(t + 1) * D], gsc[pi][:, :], op=ALU.mult))
            drain("vector")
            prev[("vt", pi)] = vt
            wait("vector", b.SL, vx)
            if ("st", pi) in prev:
                wait("vector", b.SO, prev[("st", pi)])
            vo = op("vector", b.SV, lambda e, pi=pi: e.tensor_tensor(
                osb[pi][:, :], tsc[pi][:, :], xin[pi][:, :], op=ALU.add))
            drain("vector")
            prev[("vo", pi)] = vo
            wait("sync", b.SV, vo)
            vst = dma("sync", b.SO, b.out[t * 128:(t + 1) * 128, :], osb[pi][:, :])
            prev[("st", pi)] = vst
        b.wait("sync", b.SO, b.SO.n)
        self.end_phase()


def build_core(dev=False, phases=_PHASES):
    return Builder(dev=dev, phases=phases).build()



def _constants():
    pos = np.arange(T, dtype=np.float32)
    inv_freq = 1.0 / (10000.0 ** (np.arange(0, DH, 2, dtype=np.float32) / DH))
    freqs = np.concatenate([inv_freq, inv_freq])[:, None] * pos[None, :]
    cos_c = np.cos(freqs).astype(np.float32)
    sin_c = np.sin(freqs).astype(np.float32)
    masks = np.zeros((4, 128, 512), dtype=np.float32)
    for a in range(4):
        kk = a * 128 + np.arange(128)[:, None]
        qq = np.arange(512)[None, :]
        masks[a] = np.where(kk <= qq, 0.0, -1e9)
    masks_c = np.ascontiguousarray(masks.reshape(4 * 128, 512))
    ones = np.ones((128, 128), dtype=np.float32)
    return dict(cos_c=cos_c, sin_c=sin_c, masks_c=masks_c, ones_f32=ones)


def _tr(a):
    return np.ascontiguousarray(np.swapaxes(np.asarray(a), -1, -2))


def kernel(**inp):
    consts = _constants()
    w = dict(
        wqT=_tr(inp["L0_attn_wq"]), wkT=_tr(inp["L0_attn_wk"]),
        wvT=_tr(inp["L0_attn_wv"]), woT=_tr(inp["L0_attn_wo"]),
        nw1=np.ascontiguousarray(np.asarray(inp["L0_attn_norm1_w"])[None, :]),
        nw2=np.ascontiguousarray(np.asarray(inp["L0_attn_norm2_w"])[None, :]),
        router_wT=_tr(inp["L0_attn_router_w"]),
        ebias=np.ascontiguousarray(np.asarray(inp["L0_attn_expert_bias"])[None, :]),
        sgT=_tr(inp["L0_attn_sg"]), suT=_tr(inp["L0_attn_su"]), sdT=_tr(inp["L0_attn_sd"]),
        egT=_tr(inp["L0_attn_eg"]), euT=_tr(inp["L0_attn_eu"]), edT=_tr(inp["L0_attn_ed"]),
        gate_wT=_tr(inp["L0_ffn_gate_w"]),
        gbias=np.ascontiguousarray(np.asarray(inp["L0_ffn_gate_bias"])[None, :]),
        w1T=_tr(inp["L0_ffn_w1"]), w3T=_tr(inp["L0_ffn_w3"]), w2T=_tr(inp["L0_ffn_w2"]),
        sw1T=_tr(inp["L0_ffn_sw1"]), sw3T=_tr(inp["L0_ffn_sw3"]), sw2T=_tr(inp["L0_ffn_sw2"]),
    )
    xfull = np.asarray(inp["L0_attn_x"])
    nc = build_core(dev=_DEV)
    in_maps = []
    for cc in range(8):
        m = dict(x=np.ascontiguousarray(xfull[cc]))
        m.update(w)
        m.update(consts)
        in_maps.append(m)
    res = run_bass_kernel_spmd(nc, in_maps, core_ids=list(range(8)))
    return np.stack([np.asarray(r["out"]) for r in res.results], axis=0)


# version 19
# speedup vs baseline: 1.1062x; 1.1062x over previous
import numpy as np
import ml_dtypes
from contextlib import ExitStack

import concourse.bass as bass
import concourse.mybir as mybir
from concourse.bass_utils import run_bass_kernel_spmd

F32 = mybir.dt.float32
BF16 = mybir.dt.bfloat16
AF = mybir.ActivationFunctionType
ALU = mybir.AluOpType
AX = mybir.AxisListType

T = 4096
D = 1024
NT = 32
NCH = 8
NH = 16
DH = 64
NG = 8
IA = 1024
IF = 2048
EA = 16
EF = 8

_DEV = False
_PHASES = (0, 1, 2, 4, 3, 5)
_LIM_P2 = NCH
_P4_NEXP = None
_P3_NEXP = None
_P2_PASSB = True
_P2_ROUTER = True
_P2_XASPILL = True
_P2_H2SPILL = True
_P2_BCAST = True
_LIM_P1G = NG
ACT_SET_LNEXP = 6

ENGINES = ("sync", "gpsimd", "vector", "scalar", "tensor")


class Sem:
    __slots__ = ("h", "n", "name")

    def __init__(self, h, name):
        self.h, self.n, self.name = h, 0, name


class Builder:
    def __init__(self, dev=False, phases=_PHASES):
        self.dev = dev
        self.phases = phases
        nc = self.nc = bass.Bass("TRN2", target_bir_lowering=False)
        self.es = ExitStack()
        self.ops = None

        def din(name, shape, dt=F32):
            return nc.dram_tensor(name, list(shape), dt, kind="ExternalInput").ap()

        self.x = din("x", (T, D))
        self.wqT = din("wqT", (D, D)); self.wkT = din("wkT", (D, D))
        self.wvT = din("wvT", (D, D)); self.woT = din("woT", (D, D))
        self.nw1 = din("nw1", (1, D)); self.nw2 = din("nw2", (1, D))
        self.router_wT = din("router_wT", (D, EA)); self.ebias = din("ebias", (1, EA))
        self.sgT = din("sgT", (D, IA)); self.suT = din("suT", (D, IA)); self.sdT = din("sdT", (IA, D))
        self.egT = din("egT", (EA, D, IA)); self.euT = din("euT", (EA, D, IA)); self.edT = din("edT", (EA, IA, D))
        self.gate_wT = din("gate_wT", (D, EF)); self.gbias = din("gbias", (1, EF))
        self.w1T = din("w1T", (EF, D, IF)); self.w3T = din("w3T", (EF, D, IF)); self.w2T = din("w2T", (EF, IF, D))
        self.sw1T = din("sw1T", (D, IF)); self.sw3T = din("sw3T", (D, IF)); self.sw2T = din("sw2T", (IF, D))
        self.cos_c = din("cos_c", (DH, T)); self.sin_c = din("sin_c", (DH, T))
        self.masks_c = din("masks_c", (4 * 128, 512))
        self.ones_f32 = din("ones_f32", (128, 128))

        self.out = nc.dram_tensor("out", [T, D], F32, kind="ExternalOutput").ap()
        sk = "ExternalOutput" if dev else "Internal"
        self.yz_d = nc.dram_tensor("yz_d", [128, NT * D], BF16, kind=sk).ap() if dev else None
        self.hb_d = nc.dram_tensor("hb_d", [128, 16 * 512], BF16, kind=sk).ap() if dev else None
        self.wa_d = nc.dram_tensor("wa_d", [128, 8 * 2048], BF16, kind=sk).ap() if dev else None
        self.xs_d = nc.dram_tensor("xs_d", [128, 8 * 512], BF16, kind=sk).ap() if dev else None
        self.gs_d = nc.dram_tensor("gs_d", [128, 512], BF16, kind=sk).ap() if dev else None
        self.sr_d = nc.dram_tensor("sr_d", [128, NT * D], BF16, kind=sk).ap() if dev else None
        self.xnT = nc.dram_tensor("xnT", [D, T], BF16, kind=sk).ap()
        self.xnrT = nc.dram_tensor("xnrT", [D, T], BF16, kind=sk).ap()
        self.ctxT = nc.dram_tensor("ctxT", [D, T], BF16, kind=sk).ap()
        self.h2nT = nc.dram_tensor("h2nT", [D, T], BF16, kind=sk).ap()
        self.xaT = nc.dram_tensor("xaT", [T, D], BF16, kind=sk).ap()

        self.sems = []
        for nm in ("SL", "SO", "SW", "SX", "ST", "SV", "SS", "SG"):
            s = Sem(self.es.enter_context(nc.semaphore(nm)), nm)
            setattr(self, nm, s)
            self.sems.append(s)

        sb = self.sb
        self.cols = sb("cols", (128, 112), F32)
        self.onesb = sb("onesb", (128, 128), BF16)
        self.onesf = sb("onesf", (128, 128), F32)
        self.ebias_r = sb("ebias_r", (128, EA), F32)
        self.gbias_r = sb("gbias_r", (128, EF), F32)
        self.cwa = sb("cwa", (128, NT * EA), F32)
        self.cwf = sb("cwf", (128, NT * EF), F32)
        self.pA = self.ps("pA", (128, 1024))
        self.pB = self.ps("pB", (128, 1024))
        self.p0 = self.ps("p0", (128, 512))
        self.p1 = self.ps("p1", (128, 512))
        self.p2 = self.ps("p2", (128, 512))
        self.p3 = self.ps("p3", (128, 512))

        self.C_NW1, self.C_NW1Q, self.C_EPS6, self.C_EPS5 = 0, 8, 16, 17
        self.C_NW2 = 60

    def sb(self, name, shape, dt):
        es = self.phase_es if getattr(self, "phase_es", None) is not None else self.es
        return es.enter_context(self.nc.sbuf_tensor(name, list(shape), dt))

    def sb_persist(self, name, shape, dt):
        return self.es.enter_context(self.nc.sbuf_tensor(name, list(shape), dt))

    def ps(self, name, shape):
        return self.es.enter_context(self.nc.psum_tensor(name, list(shape), F32))

    def begin_phase(self):
        self.phase_es = ExitStack()
        self._hwm = {}
        self.ops = {e: [] for e in ENGINES}
        for eng in ENGINES:
            for s in self.sems:
                if s.n > 0:
                    self.wait(eng, s, s.n)

    def end_phase(self):
        ops = self.ops
        with self.nc.Block() as block:
            @block.sync
            def _(e):
                for fn in ops["sync"]:
                    fn(e)

            @block.gpsimd
            def _(e):
                for fn in ops["gpsimd"]:
                    fn(e)

            @block.vector
            def _(e):
                for fn in ops["vector"]:
                    fn(e)

            @block.scalar
            def _(e):
                for fn in ops["scalar"]:
                    fn(e)

            @block.tensor
            def _(e):
                for fn in ops["tensor"]:
                    fn(e)
        self.ops = None
        self.phase_es.close()
        self.phase_es = None

    def emit(self, eng, fn):
        self.ops[eng].append(fn)

    def wait(self, eng, sem, val):
        if not val or val <= 0:
            return
        key = (eng, sem.name)
        if self._hwm.get(key, -1) >= val:
            return
        self._hwm[key] = val
        self.emit(eng, lambda e, s=sem.h, v=val: e.wait_ge(s, v))

    def dma(self, eng, sem, dst, src, **kw):
        sem.n += 16
        v = sem.n
        self.emit(eng, lambda e, d=dst, s=src, sm=sem.h, kw=kw: e.dma_start(d, s, **kw).then_inc(sm, 16))
        return v

    def dmat(self, dst, src):
        self.SX.n += 16
        v = self.SX.n
        self.emit("scalar", lambda e, d=dst, s=src: e.dma_start(d, s, transpose=True).then_inc(self.SX.h, 16))
        return v

    def op(self, eng, sem, f):
        if sem is None:
            self.emit(eng, f)
            return None
        sem.n += 1
        v = sem.n
        self.emit(eng, lambda e, f=f, s=sem.h: f(e).then_inc(s, 1))
        return v

    def drain(self, eng):
        self.emit(eng, lambda e: e.drain())

    def preload(self, set_id):
        pass

    def chunk_load(self, dstbuf, src, c, half_elems=512):
        vr = None
        for h in range(2):
            vr = self.dma("sync", self.SL,
                          dstbuf[:, h * 4 * half_elems:(h + 1) * 4 * half_elems]
                          .rearrange("p (a n) -> p a n", n=half_elems),
                          src.rearrange("(a p) m -> p a m", p=128)[:, 4 * h:4 * h + 4,
                                                                  c * half_elems:(c + 1) * half_elems])
        return vr

    def build(self):
        self.startup()
        if 0 in self.phases:
            self.p0_layernorm()
        if 1 in self.phases:
            self.p1_attention()
        if 2 in self.phases:
            self.p2_wo_norm2()
        if 4 in self.phases:
            self.p4_ffn()
        if 3 in self.phases:
            self.p3_amoe()
        if 5 in self.phases:
            self.p5_combine()
        self.es.close()
        return self.nc

    def startup(self):
        self.begin_phase()
        b = self
        b.preload(ACT_SET_LNEXP)
        vof = b.dma("sync", b.SL, b.onesf[:, :], b.ones_f32)
        b.dma("sync", b.SL, b.cols[:, b.C_NW1:b.C_NW1 + 8], b.nw1.rearrange("o (a p) -> p (o a)", p=128),
              allow_slow_non_contiguous=True)
        b.dma("sync", b.SL, b.cols[:, b.C_NW2:b.C_NW2 + 8], b.nw2.rearrange("o (a p) -> p (o a)", p=128),
              allow_slow_non_contiguous=True)
        eb_row = b.sb("eb_row", (1, EA), F32)
        gb_row = b.sb("gb_row", (1, EF), F32)
        b.dma("sync", b.SL, eb_row[0:1, :], b.ebias)
        v = b.dma("sync", b.SL, gb_row[0:1, :], b.gbias)
        eb_hl = b.sb("eb_hl", (1, 2 * EA), BF16)
        gb_hl = b.sb("gb_hl", (1, 2 * EF), BF16)
        b.wait("vector", b.SL, v)
        b.wait("vector", b.SL, vof)
        b.emit("vector", lambda e: e.tensor_copy(b.onesb[:, :], b.onesf[:, :]))
        b.drain("vector")
        b.emit("vector", lambda e: e.tensor_copy(eb_hl[0:1, 0:EA], eb_row[0:1, :]))
        b.emit("vector", lambda e: e.tensor_copy(gb_hl[0:1, 0:EF], gb_row[0:1, :]))
        b.drain("vector")
        b.emit("vector", lambda e: e.tensor_tensor(eb_hl[0:1, EA:2 * EA], eb_row[0:1, :], eb_hl[0:1, 0:EA], op=ALU.subtract))
        vbl = b.op("vector", b.SV, lambda e: e.tensor_tensor(gb_hl[0:1, EF:2 * EF], gb_row[0:1, :], gb_hl[0:1, 0:EF], op=ALU.subtract))
        b.drain("vector")
        b.wait("tensor", b.SV, vbl)
        b.emit("tensor", lambda e: e.matmul(b.pA[:, 0:EA], b.onesb[0:1, 0:128], eb_hl[0:1, 0:EA], start=True, stop=False))
        b.emit("tensor", lambda e: e.matmul(b.pA[:, 0:EA], b.onesb[0:1, 0:128], eb_hl[0:1, EA:2 * EA], start=False, stop=True))
        b.emit("tensor", lambda e: e.matmul(b.pB[:, 0:EF], b.onesb[0:1, 0:128], gb_hl[0:1, 0:EF], start=True, stop=False))
        vbb = b.op("tensor", b.ST, lambda e: e.matmul(b.pB[:, 0:EF], b.onesb[0:1, 0:128], gb_hl[0:1, EF:2 * EF], start=False, stop=True))
        b.wait("scalar", b.ST, vbb)
        b.emit("scalar", lambda e: e.copy(b.ebias_r[:, :], b.pA[:, 0:EA]))
        b.op("scalar", b.SS, lambda e: e.copy(b.gbias_r[:, :], b.pB[:, 0:EF]))
        b.emit("vector", lambda e: e.memset(b.cols[:, b.C_EPS6:b.C_EPS6 + 1], 1e-6))
        b.emit("vector", lambda e: e.memset(b.cols[:, b.C_EPS5:b.C_EPS5 + 1], 1e-5))
        b.drain("vector")
        b.op("vector", b.SV, lambda e: e.tensor_scalar(
            b.cols[:, b.C_NW1Q:b.C_NW1Q + 8], b.cols[:, b.C_NW1:b.C_NW1 + 8], 0.125, None, op0=ALU.mult))
        self.end_phase()

    def p0_layernorm(self):
        b = self
        self.begin_phase()
        BT = 8
        xin = [b.sb("p0_x0", (128, BT * D), F32), b.sb("p0_x1", (128, BT * D), F32)]
        sqo = b.sb("p0_sqo", (128, D), F32)
        xn_tm = [b.sb("p0_n0", (128, D), BF16), b.sb("p0_n1", (128, D), BF16)]
        xr_tm = [b.sb("p0_r0", (128, D), BF16), b.sb("p0_r1", (128, D), BF16)]
        stg = [b.sb("p0_s0", (128, 2 * D), BF16), b.sb("p0_s1", (128, 2 * D), BF16)]
        c = b.cols
        CS, CQ, CM, CV, CR, CA, CX = 20, 28, 36, 44, 52, 68, 76
        last = {}
        for blk in range(NT // BT):
            bi = blk % 2
            t0 = blk * BT
            if ("blk", bi) in last:
                b.wait("sync", b.SV, last[("blk", bi)])
                b.wait("sync", b.SS, last[("sq", bi)])
            vx = b.dma("sync", b.SL, xin[bi][:, :].rearrange("p (a n) -> p a n", n=D),
                       b.x.rearrange("(a p) n -> p a n", p=128)[:, t0:t0 + BT, :])
            b.wait("vector", b.SL, vx)
            b.wait("scalar", b.SL, vx)
            if "chain" in last:
                b.wait("vector", b.SS, last["chain_s"])
            for j in range(BT):
                b.emit("vector", lambda e, j=j, bi=bi: e.tensor_reduce(
                    c[:, CS + j:CS + j + 1], xin[bi][:, j * D:(j + 1) * D], AX.X, ALU.add))
            b.drain("vector")
            b.emit("vector", lambda e: e.tensor_scalar(c[:, CM:CM + BT], c[:, CS:CS + BT], 1.0 / D, None, op0=ALU.mult))
            b.drain("vector")
            vsq = None
            for j in range(BT):
                vsq = b.op("scalar", b.SS, lambda e, j=j, bi=bi: e.activation(
                    sqo[:, :], xin[bi][:, j * D:(j + 1) * D], AF.Square, accum_out=c[:, CQ + j:CQ + j + 1]))
            last[("sq", bi)] = vsq
            b.wait("vector", b.SS, vsq)
            b.emit("vector", lambda e: e.tensor_tensor(c[:, CA:CA + BT], c[:, CM:CM + BT], c[:, CM:CM + BT], op=ALU.mult))
            b.drain("vector")
            vvar = b.op("vector", b.SV, lambda e: e.scalar_tensor_tensor(
                c[:, CV:CV + BT], c[:, CQ:CQ + BT], 1.0 / D, c[:, CA:CA + BT], op0=ALU.mult, op1=ALU.subtract))
            b.drain("vector")
            b.wait("scalar", b.SV, vvar)
            b.emit("scalar", lambda e: e.activation(c[:, CA:CA + BT], c[:, CV:CV + BT], AF.Ln, bias=c[:, b.C_EPS5:b.C_EPS5 + 1], scale=1.0))
            b.drain("scalar")
            b.emit("scalar", lambda e: e.activation(c[:, CR:CR + BT], c[:, CA:CA + BT], AF.Exp, scale=-0.5))
            b.drain("scalar")
            vs2 = b.op("scalar", b.SS, lambda e: e.activation(c[:, CA:CA + BT], c[:, CR:CR + BT], AF.Square))
            b.wait("vector", b.SS, vs2)
            vv2 = b.op("vector", b.SV, lambda e: e.tensor_tensor(c[:, CA:CA + BT], c[:, CV:CV + BT], c[:, CA:CA + BT], op=ALU.mult))
            b.drain("vector")
            b.wait("scalar", b.SV, vv2)
            b.emit("scalar", lambda e: e.activation(c[:, CA:CA + BT], c[:, CA:CA + BT], AF.Ln, bias=c[:, b.C_EPS6:b.C_EPS6 + 1], scale=1.0))
            b.drain("scalar")
            vs3 = b.op("scalar", b.SS, lambda e: e.activation(c[:, CA:CA + BT], c[:, CA:CA + BT], AF.Exp, scale=-0.5))
            b.wait("vector", b.SS, vs3)
            vch = b.op("vector", b.SV, lambda e: e.tensor_tensor(c[:, CX:CX + BT], c[:, CR:CR + BT], c[:, CA:CA + BT], op=ALU.mult))
            b.drain("vector")
            last["chain"] = True
            last["chain_s"] = vs3
            vxn_last = None
            for j in range(BT):
                t = t0 + j
                pi = t % 2
                if ("tp", pi) in last:
                    b.wait("vector", b.SX, last[("tp", pi)])
                b.emit("vector", lambda e, j=j, bi=bi, pi=pi: e.tensor_scalar(
                    xn_tm[pi][:, :], xin[bi][:, j * D:(j + 1) * D],
                    c[:, CM + j:CM + j + 1], c[:, CR + j:CR + j + 1], op0=ALU.subtract, op1=ALU.mult))
                vxn = b.op("vector", b.SV, lambda e, j=j, bi=bi, pi=pi: e.tensor_scalar(
                    xr_tm[pi][:, :], xin[bi][:, j * D:(j + 1) * D],
                    c[:, CM + j:CM + j + 1], c[:, CX + j:CX + j + 1], op0=ALU.subtract, op1=ALU.mult))
                b.drain("vector")
                vxn_last = vxn
                if ("st", pi) in last:
                    b.wait("scalar", b.SO, last[("st", pi)])
                b.wait("scalar", b.SV, vxn)
                b.dmat(stg[pi][:, 0:D].rearrange("p (a n) -> p a n", n=128), xn_tm[pi][:, :])
                vtp = b.dmat(stg[pi][:, D:2 * D].rearrange("p (a n) -> p a n", n=128), xr_tm[pi][:, :])
                last[("tp", pi)] = vtp
                b.wait("sync", b.SX, vtp)
                for (dst, off) in ((b.xnT, 0), (b.xnrT, D)):
                    for h in range(2):
                        vsp = b.dma("sync", b.SO,
                                    dst.rearrange("(a p) m -> p a m", p=128)[:, 4 * h:4 * h + 4, t * 128:(t + 1) * 128],
                                    stg[pi][:, off + h * 512:off + (h + 1) * 512].rearrange("p (a n) -> p a n", n=128))
                last[("st", pi)] = vsp
            last[("blk", bi)] = vxn_last
        self.end_phase()

    def p1_attention(self):
        b = self
        self.begin_phase()
        tabs = b.sb("p1_tabs", (DH, 2 * T), F32)
        masks = b.sb("p1_masks", (128, 4 * 512), BF16)
        masksf = b.sb("p1_masksf", (128, 4 * 512), F32)
        qg = b.sb("p1_qg", (128, T), BF16)
        kg = b.sb("p1_kg", (128, T), BF16)
        vg = b.sb("p1_vg", (128, NT * 130), BF16)
        vtg = b.sb("p1_vtg", (128, T), BF16)
        xs = [b.sb("p1_xs0", (128, 8 * 512), BF16), b.sb("p1_xs1", (128, 8 * 512), BF16)]
        wraw = b.sb("p1_wraw", (128, 3 * 1024), BF16)
        weff = b.sb("p1_weff", (128, 3 * 1024), BF16)
        pb = b.sb("p1_pb", (128, 4 * 512), BF16)
        t1s = b.sb("p1_t1", (128, 512), F32)
        t2s = b.sb("p1_t2", (128, 512), F32)
        rbs = b.sb("p1_rb", (64, 512), F32)
        rls = b.sb("p1_rl", (128, 2048), F32)
        rlhl = b.sb("p1_rlhl", (128, 2048), BF16)
        cst = b.sb("p1_cst", (128, 2 * 512), BF16)
        vstg = b.sb("p1_vstg", (128, 2 * 512), BF16)

        dma, emit, op, wait, drain = b.dma, b.emit, b.op, b.wait, b.drain
        b.preload(ACT_SET_LNEXP)
        dma("sync", b.SL, tabs[:, 0:T], b.cos_c)
        dma("sync", b.SL, tabs[:, T:2 * T], b.sin_c)
        v_tabs = dma("sync", b.SL, masksf[:, :].rearrange("p (a n) -> p a n", n=512),
                     b.masks_c.rearrange("(a p) n -> p a n", p=128))
        wait("vector", b.SL, v_tabs)
        emit("vector", lambda e: e.tensor_copy(masks[:, :], masksf[:, :]))
        drain("vector")
        emit("vector", lambda e: e.memset(
            vg[:, :].rearrange("p (t h n) -> p t h n", h=2, n=65)[:, :, :, 64:65], 1.0))
        drain("vector")

        prev = {}
        for g in range(_LIM_P1G):
            if g > 0:
                wait("gpsimd", b.ST, prev["qkv_done"])
            vw = None
            for j, W in enumerate((b.wqT, b.wkT, b.wvT)):
                for h in range(2):
                    vw = dma("gpsimd", b.SW,
                             wraw[:, j * 1024 + h * 512:j * 1024 + (h + 1) * 512].rearrange("p (a n) -> p a n", n=128),
                             W[:, g * 128:(g + 1) * 128].rearrange("(a p) n -> p a n", p=128)[:, 4 * h:4 * h + 4, :])
            wait("gpsimd", b.SW, vw)
            vfold = None
            for j in range(3):
                ccol = b.C_NW1Q if j == 0 else b.C_NW1
                for dt in range(8):
                    vfold = b.op("gpsimd", b.SG, lambda e, j=j, dt=dt, ccol=ccol: e.tensor_scalar(
                        weff[:, j * 1024 + dt * 128:j * 1024 + (dt + 1) * 128],
                        wraw[:, j * 1024 + dt * 128:j * 1024 + (dt + 1) * 128],
                        b.cols[:, ccol + dt:ccol + dt + 1], None, op0=ALU.mult))
            if "attn_done" in prev:
                wait("vector", b.ST, prev["attn_done"])
                wait("scalar", b.ST, prev["attn_done"])
            ropeq = {}; ropek = {}; vsec = {}; vcpc = {}
            for c in range(NCH):
                ci = c % 2
                if ("xs", ci) in prev:
                    wait("sync", b.ST, prev[("xs", ci)])
                vxs = b.chunk_load(xs[ci], b.xnrT, c)
                wait("tensor", b.SL, vxs)
                wait("tensor", b.SG, vfold)
                if ("q_ps", ci) in prev:
                    wait("tensor", b.SV, prev[("q_ps", ci)])
                if ("k_ps", ci) in prev:
                    wait("tensor", b.SV, prev[("k_ps", ci)])
                if ("v_ps", ci) in prev:
                    wait("tensor", b.SS, prev[("v_ps", ci)])
                if g > 0 and c == 0:
                    wait("tensor", b.SS, b.SS.n)
                    wait("tensor", b.SV, b.SV.n)
                q_ps = (b.p0, b.p1)[ci]; k_ps = (b.p2, b.p3)[ci]
                v_ps = b.pA[:, ci * 512:(ci + 1) * 512]
                for dt in range(8):
                    emit("tensor", lambda e, dt=dt, q_ps=q_ps, ci=ci: e.matmul(
                        q_ps[:, :], weff[:, dt * 128:(dt + 1) * 128],
                        xs[ci][:, dt * 512:(dt + 1) * 512], start=(dt == 0), stop=(dt == 7)))
                for dt in range(8):
                    emit("tensor", lambda e, dt=dt, k_ps=k_ps, ci=ci: e.matmul(
                        k_ps[:, :], weff[:, 1024 + dt * 128:1024 + (dt + 1) * 128],
                        xs[ci][:, dt * 512:(dt + 1) * 512], start=(dt == 0), stop=(dt == 7)))
                vmm = None
                for dt in range(8):
                    vmm = op("tensor", b.ST, lambda e, dt=dt, v_ps=v_ps, ci=ci: e.matmul(
                        v_ps, weff[:, 2048 + dt * 128:2048 + (dt + 1) * 128],
                        xs[ci][:, dt * 512:(dt + 1) * 512], start=(dt == 0), stop=(dt == 7)))
                prev[("xs", ci)] = vmm
                prev["qkv_done"] = vmm
                wait("vector", b.ST, vmm)
                for which, sp, dst in (("q", q_ps, qg), ("k", k_ps, kg)):
                    emit("vector", lambda e, sp=sp, c=c: e.tensor_tensor(
                        t1s[0:64, :], sp[0:64, :], tabs[:, c * 512:(c + 1) * 512], op=ALU.mult))
                    emit("vector", lambda e, sp=sp, c=c: e.tensor_tensor(
                        t1s[64:128, :], sp[64:128, :], tabs[:, c * 512:(c + 1) * 512], op=ALU.mult))
                    emit("vector", lambda e, sp=sp, c=c: e.tensor_tensor(
                        t2s[0:32, :], sp[32:64, :], tabs[0:32, T + c * 512:T + c * 512 + 512], op=ALU.mult))
                    emit("vector", lambda e, sp=sp, c=c: e.tensor_tensor(
                        t2s[32:64, :], sp[0:32, :], tabs[32:64, T + c * 512:T + c * 512 + 512], op=ALU.mult))
                    emit("vector", lambda e, sp=sp, c=c: e.tensor_tensor(
                        t2s[64:96, :], sp[96:128, :], tabs[0:32, T + c * 512:T + c * 512 + 512], op=ALU.mult))
                    emit("vector", lambda e, sp=sp, c=c: e.tensor_tensor(
                        t2s[96:128, :], sp[64:96, :], tabs[32:64, T + c * 512:T + c * 512 + 512], op=ALU.mult))
                    drain("vector")
                    emit("vector", lambda e, d=dst, c=c: e.tensor_tensor(
                        d[0:32, c * 512:(c + 1) * 512], t1s[0:32, :], t2s[0:32, :], op=ALU.subtract))
                    emit("vector", lambda e, d=dst, c=c: e.tensor_tensor(
                        d[32:64, c * 512:(c + 1) * 512], t1s[32:64, :], t2s[32:64, :], op=ALU.add))
                    emit("vector", lambda e, d=dst, c=c: e.tensor_tensor(
                        d[64:96, c * 512:(c + 1) * 512], t1s[64:96, :], t2s[64:96, :], op=ALU.subtract))
                    vlast = op("vector", b.SV, lambda e, d=dst, c=c: e.tensor_tensor(
                        d[96:128, c * 512:(c + 1) * 512], t1s[96:128, :], t2s[96:128, :], op=ALU.add))
                    drain("vector")
                    prev[("q_ps" if which == "q" else "k_ps", ci)] = vlast
                    if which == "q":
                        ropeq[c] = vlast
                    else:
                        ropek[c] = vlast
                wait("scalar", b.ST, vmm)
                vse = op("scalar", b.SS, lambda e, v_ps=v_ps, c=c: e.copy(
                    vtg[:, c * 512:(c + 1) * 512], v_ps))
                prev[("v_ps", ci)] = vse
                vsec[c] = vse

                def v_transpose(vc):
                    si = vc % 2
                    drain("scalar")
                    if ("vstg", si) in prev:
                        wait("scalar", b.SV, prev[("vstg", si)])
                    vtp = b.dmat(vstg[:, si * 512:(si + 1) * 512].rearrange("p (a n) -> p a n", n=128),
                                 vtg[:, vc * 512:(vc + 1) * 512])
                    wait("vector", b.SX, vtp)
                    vcp = None
                    for hh in range(2):
                        vcp = op("vector", b.SV, lambda e, si=si, vc=vc, hh=hh: e.tensor_copy(
                            vg[:, :].rearrange("p (t h n) -> p t h n", h=2, n=65)[:, 4 * vc:4 * vc + 4, hh, 0:64],
                            vstg[:, si * 512:(si + 1) * 512].rearrange("p (a h n) -> p a h n", h=2, n=64)[:, :, hh, :]))
                    drain("vector")
                    prev[("vstg", si)] = vcp
                    vcpc[vc] = vcp

                if c > 0:
                    v_transpose(c - 1)
                if c == NCH - 1:
                    v_transpose(c)
            prev[("sbank", 0)] = (b.SV, ropeq[6])
            prev[("sbank", 1)] = (b.SV, ropeq[7])
            prev[("sbank", 2)] = (b.SV, ropek[6])
            prev[("ctxv", 0, 0)] = (b.SS, vsec[6])
            prev[("ctxv", 0, 1)] = (b.SS, vsec[7])
            sstep = prev.get("sstep", 0)
            pend = None

            def waitv(eng, entry):
                if isinstance(entry, tuple):
                    wait(eng, entry[0], entry[1])
                else:
                    wait(eng, b.SS, entry)

            def plan_evac_part2(pd):
                vctx = None
                for hh in range(2):
                    wait("tensor", b.SV, pd["vhl"][hh])
                    if "rb" in prev:
                        wait("tensor", b.SS, prev["rb"])
                    emit("tensor", lambda e, hh=hh: e.matmul(
                        b.p3[0:64, :], b.onesb[64:65, 0:64], rlhl[64:65, hh * 1024:hh * 1024 + 512],
                        start=True, stop=False))
                    vbc = op("tensor", b.ST, lambda e, hh=hh: e.matmul(
                        b.p3[0:64, :], b.onesb[64:65, 0:64], rlhl[64:65, hh * 1024 + 512:hh * 1024 + 1024],
                        start=False, stop=True))
                    wait("scalar", b.ST, vbc)
                    vrb = op("scalar", b.SS, lambda e: e.copy(rbs[:, :], b.p3[0:64, :]))
                    prev["rb"] = vrb
                    wait("vector", b.SS, vrb)
                    if ("cst", pd["par"]) in prev:
                        wait("vector", b.SO, prev[("cst", pd["par"])])
                    vctx = op("vector", b.SV, lambda e, hh=hh, pd=pd: e.tensor_tensor(
                        cst[hh * 64:(hh + 1) * 64, pd["par"] * 512:pd["par"] * 512 + 512],
                        pd["ctxb"][hh][0:64, :], rbs[:, :], op=ALU.mult))
                    drain("vector")
                    prev[("ctxv", hh, pd["par"])] = vctx
                wait("sync", b.SV, vctx)
                vsp = b.dma("sync", b.SO,
                            b.ctxT[g * 128:(g + 1) * 128, pd["qc"] * 512:(pd["qc"] + 1) * 512],
                            cst[:, pd["par"] * 512:pd["par"] * 512 + 512])
                prev[("cst", pd["par"])] = vsp

            for qc in range(NCH):
                nkt = 4 * qc + 4
                par = qc % 2
                ctxb = {0: b.pA[:, par * 512:(par + 1) * 512],
                        1: b.pB[:, par * 512:(par + 1) * 512]}
                vexp_q = [[None] * nkt for _ in range(2)]
                vpv_last = [None, None]
                for kt in range(nkt + 1):
                    if kt < nkt:
                        for hh in range(2):
                            j = sstep % 3
                            s_ps = (b.p0, b.p1, b.p2)[j]
                            if ("sbank", j) in prev:
                                waitv("tensor", prev[("sbank", j)])
                            wait("tensor", b.SV, ropeq[qc])
                            wait("tensor", b.SV, ropek[kt // 4])
                            vs_mm = op("tensor", b.ST, lambda e, hh=hh, kt=kt, qc=qc, s_ps=s_ps: e.matmul(
                                s_ps[:, :], kg[hh * 64:(hh + 1) * 64, kt * 128:(kt + 1) * 128],
                                qg[hh * 64:(hh + 1) * 64, qc * 512:(qc + 1) * 512],
                                start=True, stop=True, tile_position=(hh * 64, 0)))
                            a = kt - 4 * qc
                            pslot = sstep % 4
                            pbs = pb[:, pslot * 512:(pslot + 1) * 512]
                            if a >= 0:
                                wait("vector", b.ST, vs_mm)
                                vmask = op("vector", b.SV, lambda e, s_ps=s_ps, a=a: e.tensor_tensor(
                                    s_ps[:, :], s_ps[:, :], masks[:, a * 512:(a + 1) * 512], op=ALU.add))
                                wait("scalar", b.SV, vmask)
                            else:
                                wait("scalar", b.ST, vs_mm)
                            if ("pb", pslot) in prev:
                                wait("scalar", b.ST, prev[("pb", pslot)])
                            vexp = op("scalar", b.SS, lambda e, s_ps=s_ps, pbs=pbs: e.activation(
                                pbs, s_ps[:, :], AF.Exp))
                            prev[("sbank", j)] = vexp
                            vexp_q[hh][kt] = (vexp, pslot)
                            sstep += 1
                    if kt == 1 and pend is not None:
                        plan_evac_part2(pend)
                        pend = None
                    if kt >= 1:
                        ktp = kt - 1
                        for hh in range(2):
                            vexp, pslot = vexp_q[hh][ktp]
                            wait("tensor", b.SS, vexp)
                            wait("tensor", b.SV, vcpc[ktp // 4])
                            if ktp == 0 and ("ctxv", hh, par) in prev:
                                ent = prev[("ctxv", hh, par)]
                                if isinstance(ent, tuple):
                                    wait("tensor", ent[0], ent[1])
                                else:
                                    wait("tensor", b.SV, ent)
                            vpv = op("tensor", b.ST, lambda e, ktp=ktp, hh=hh, nkt=nkt, ctxb=ctxb, pslot=pslot: e.matmul(
                                ctxb[hh][0:65, :], vg[:, ktp * 130 + hh * 65:ktp * 130 + hh * 65 + 65],
                                pb[:, pslot * 512:(pslot + 1) * 512],
                                start=(ktp == 0), stop=(ktp == nkt - 1)))
                            prev[("pb", pslot)] = vpv
                            if ktp == nkt - 1:
                                vpv_last[hh] = vpv
                vhls = [None, None]
                for hh in range(2):
                    wait("scalar", b.ST, vpv_last[hh])
                    emit("scalar", lambda e, hh=hh, ctxb=ctxb: e.activation(
                        rls[64:65, hh * 1024:hh * 1024 + 512], ctxb[hh][64:65, :], AF.Ln))
                    drain("scalar")
                    vrl = op("scalar", b.SS, lambda e, hh=hh: e.activation(
                        rls[64:65, hh * 1024 + 512:hh * 1024 + 1024],
                        rls[64:65, hh * 1024:hh * 1024 + 512], AF.Exp, scale=-1.0))
                    wait("vector", b.SS, vrl)
                    emit("vector", lambda e, hh=hh: e.tensor_copy(
                        rlhl[64:65, hh * 1024:hh * 1024 + 512],
                        rls[64:65, hh * 1024 + 512:hh * 1024 + 1024]))
                    drain("vector")
                    vhl = op("vector", b.SV, lambda e, hh=hh: e.tensor_tensor(
                        rlhl[64:65, hh * 1024 + 512:hh * 1024 + 1024],
                        rls[64:65, hh * 1024 + 512:hh * 1024 + 1024],
                        rlhl[64:65, hh * 1024:hh * 1024 + 512], op=ALU.subtract))
                    drain("vector")
                    vhls[hh] = vhl
                pend = dict(qc=qc, par=par, ctxb=ctxb, vhl=vhls)
            plan_evac_part2(pend)
            pend = None
            prev["sstep"] = sstep
            prev["attn_done"] = b.ST.n
        self.end_phase()

    def p2_wo_norm2(self):
        b = self
        self.begin_phase()
        wo_sb = b.sb("p2_wo", (128, 8 * 1024), BF16)
        rw_sb = b.sb("p2_rw", (128, 8 * EA), BF16)
        ctx_s = [b.sb("p2_c0", (128, 8 * 512), BF16), b.sb("p2_c1", (128, 8 * 512), BF16)]
        xn_s = [b.sb("p2_x0", (128, 8 * 512), BF16), b.sb("p2_x1", (128, 8 * 512), BF16)]
        xa_scr = [b.sb("p2_xa0", (128, 8 * 512), BF16), b.sb("p2_xa1", (128, 8 * 512), BF16)]
        h2_scr = [b.sb("p2_h20", (128, 8 * 512), BF16), b.sb("p2_h21", (128, 8 * 512), BF16)]
        sqt = [b.sb("p2_sq0", (128, 512), BF16), b.sb("p2_sq1", (128, 512), BF16)]
        r2row = b.sb("p2_r2", (1, T), F32)
        r2hl = b.sb("p2_r2hl", (1, 1024), BF16)
        tstg = b.sb("p2_ts", (128, 2 * 512), BF16)
        lgt = b.sb("p2_lg", (128, 4 * EA), F32)
        gts = b.sb("p2_gt", (128, 4 * EA), F32)
        mk1 = b.sb("p2_mk1", (128, 4 * EA), F32)
        mk2 = b.sb("p2_mk2", (128, 4 * EA), F32)
        mskd = b.sb("p2_msk", (128, 4 * EA), F32)
        c = b.cols

        dma, emit, op, wait, drain = b.dma, b.emit, b.op, b.wait, b.drain
        for h in range(2):
            dma("gpsimd", b.SW, wo_sb[:, h * 4096:(h + 1) * 4096].rearrange("p (a n) -> p a n", n=1024),
                b.woT.rearrange("(a p) n -> p a n", p=128)[:, 4 * h:4 * h + 4, :])
        vw = dma("gpsimd", b.SW, rw_sb[:, :].rearrange("p (a n) -> p a n", n=EA),
                 b.router_wT.rearrange("(a p) n -> p a n", p=128))
        prev = {}

        def plan_passB(cc):
            ci = cc % 2
            vr2 = prev[("r2", cc)]
            wait("vector", b.SS, vr2)
            emit("vector", lambda e, cc=cc: e.tensor_copy(
                r2hl[0:1, 0:512], r2row[0:1, cc * 512:(cc + 1) * 512]))
            drain("vector")
            vhl = op("vector", b.SV, lambda e, cc=cc: e.tensor_tensor(
                r2hl[0:1, 512:1024], r2row[0:1, cc * 512:(cc + 1) * 512], r2hl[0:1, 0:512], op=ALU.subtract))
            drain("vector")
            wait("tensor", b.SV, vhl)
            if "r2b" in prev:
                wait("tensor", b.SV, prev["r2b"])
            emit("tensor", lambda e: e.matmul(
                b.p3[:, :], b.onesb[0:1, 0:128], r2hl[0:1, 0:512], start=True, stop=False))
            vbc = op("tensor", b.ST, lambda e: e.matmul(
                b.p3[:, :], b.onesb[0:1, 0:128], r2hl[0:1, 512:1024], start=False, stop=True))
            wait("vector", b.ST, vbc)
            vh2_last = None
            vtp = None
            for et in range(8):
                vh2 = op("vector", b.SV, lambda e, et=et, ci=ci: e.scalar_tensor_tensor(
                    h2_scr[ci][:, et * 512:(et + 1) * 512], xa_scr[ci][:, et * 512:(et + 1) * 512],
                    b.cols[:, b.C_NW2 + et:b.C_NW2 + et + 1], b.p3[:, :],
                    op0=ALU.mult, op1=ALU.mult))
                drain("vector")
                vh2_last = vh2
                wait("gpsimd", b.SV, vh2)
                dma("gpsimd", b.SW, b.h2nT[et * 128:(et + 1) * 128, cc * 512:(cc + 1) * 512],
                    h2_scr[ci][:, et * 512:(et + 1) * 512])
                si = et % 2
                if ("tstg", si) in prev:
                    wait("scalar", b.SO, prev[("tstg", si)])
                vtp = b.dmat(tstg[:, si * 512:(si + 1) * 512].rearrange("p (a n) -> p a n", n=128),
                             xa_scr[ci][:, et * 512:(et + 1) * 512])
                wait("sync", b.SX, vtp)
                xa_ap = bass.AP(b.xaT.tensor, (cc * 512) * D + et * 128,
                                [[D, 128], [128 * D, 4], [1, 128]])
                vso = b.dma("sync", b.SO, xa_ap,
                            tstg[:, si * 512:(si + 1) * 512].rearrange("p (a n) -> p a n", n=128))
                prev[("tstg", si)] = vso
            prev[("xa_done", ci)] = (vh2_last, vtp)
            prev["r2b"] = vh2_last
            wait("tensor", b.SV, vh2_last)
            M1, M2, NR, IV = 20, 24, 28, 32
            vlgs = []
            for tt in range(4):
                rps = b.pB[:, 0:EA] if tt % 2 == 0 else b.pB[:, 512:512 + EA]
                if ("rtA", tt % 2) in prev:
                    wait("tensor", b.SV, prev[("rtA", tt % 2)])
                vrt = None
                for dt in range(8):
                    vrt = op("tensor", b.ST, lambda e, dt=dt, tt=tt, ci=ci, rps=rps: e.matmul(
                        rps, h2_scr[ci][:, dt * 512 + tt * 128:dt * 512 + tt * 128 + 128],
                        rw_sb[:, dt * EA:(dt + 1) * EA], start=(dt == 0), stop=(dt == 7)))
                wait("vector", b.ST, vrt)
                if "sigA" in prev:
                    wait("vector", b.SS, prev["sigA"])
                vlg = op("vector", b.SV, lambda e, tt=tt, rps=rps: e.tensor_tensor(
                    lgt[:, tt * EA:(tt + 1) * EA], rps, b.ebias_r[:, :], op=ALU.add))
                prev[("rtA", tt % 2)] = vlg
                vlgs.append(vlg)
            drain("vector")
            wait("scalar", b.SV, vlgs[-1])
            vsg = op("scalar", b.SS, lambda e: e.activation(gts[:, :], lgt[:, :], AF.Sigmoid))
            prev["sigA"] = vsg
            wait("vector", b.SS, vsg)
            for tt in range(4):
                emit("vector", lambda e, tt=tt: e.tensor_reduce(
                    c[:, M1 + tt:M1 + tt + 1], gts[:, tt * EA:(tt + 1) * EA], AX.X, ALU.max))
            drain("vector")
            for tt in range(4):
                emit("vector", lambda e, tt=tt: e.tensor_scalar(
                    mk1[:, tt * EA:(tt + 1) * EA], gts[:, tt * EA:(tt + 1) * EA],
                    c[:, M1 + tt:M1 + tt + 1], None, op0=ALU.is_equal))
            drain("vector")
            for tt in range(4):
                emit("vector", lambda e, tt=tt: e.scalar_tensor_tensor(
                    mskd[:, tt * EA:(tt + 1) * EA], mk1[:, tt * EA:(tt + 1) * EA], -1e30,
                    gts[:, tt * EA:(tt + 1) * EA], op0=ALU.mult, op1=ALU.add))
            drain("vector")
            for tt in range(4):
                emit("vector", lambda e, tt=tt: e.tensor_reduce(
                    c[:, M2 + tt:M2 + tt + 1], mskd[:, tt * EA:(tt + 1) * EA], AX.X, ALU.max))
            drain("vector")
            for tt in range(4):
                emit("vector", lambda e, tt=tt: e.tensor_scalar(
                    mk2[:, tt * EA:(tt + 1) * EA], mskd[:, tt * EA:(tt + 1) * EA],
                    c[:, M2 + tt:M2 + tt + 1], None, op0=ALU.is_equal))
            emit("vector", lambda e: e.tensor_tensor(c[:, NR:NR + 4], c[:, M1:M1 + 4], c[:, M2:M2 + 4], op=ALU.add))
            drain("vector")
            emit("vector", lambda e: e.tensor_tensor(mk1[:, :], mk1[:, :], mk2[:, :], op=ALU.add))
            emit("vector", lambda e: e.tensor_scalar(c[:, NR:NR + 4], c[:, NR:NR + 4], 1e-9, None, op0=ALU.add))
            drain("vector")
            emit("vector", lambda e: e.tensor_tensor(mk1[:, :], mk1[:, :], gts[:, :], op=ALU.mult))
            emit("vector", lambda e: e.reciprocal(c[:, IV:IV + 4], c[:, NR:NR + 4]))
            drain("vector")
            for tt in range(4):
                ttg = cc * 4 + tt
                op("vector", b.SV, lambda e, tt=tt, ttg=ttg: e.tensor_scalar(
                    b.cwa[:, ttg * EA:(ttg + 1) * EA], mk1[:, tt * EA:(tt + 1) * EA],
                    c[:, IV + tt:IV + tt + 1], None, op0=ALU.mult))
            drain("vector")
            prev[("routers_done", ci)] = b.ST.n

        for cc in range(NCH):
            ci = cc % 2
            if ("ld", ci) in prev:
                wait("sync", b.ST, prev[("ld", ci)][0])
                wait("sync", b.SV, prev[("ld", ci)][1])
            b.chunk_load(ctx_s[ci], b.ctxT, cc)
            vld = b.chunk_load(xn_s[ci], b.xnT, cc)
            wait("tensor", b.SL, vld)
            wait("tensor", b.SW, vw)
            if ("xa_done", ci) in prev:
                wait("vector", b.ST, prev[("routers_done", ci)])
                wait("vector", b.SX, prev[("xa_done", ci)][1])
                wait("vector", b.SW, b.SW.n)
            vss = None
            vxa_last = None
            sq_pend = None
            for et in range(8):
                d_ps = (b.p0, b.p1)[et % 2]
                if ("d_ps", et % 2) in prev:
                    wait("tensor", b.SV, prev[("d_ps", et % 2)])
                vdm = None
                for dt in range(8):
                    vdm = op("tensor", b.ST, lambda e, dt=dt, et=et, d_ps=d_ps, ci=ci: e.matmul(
                        d_ps[:, :], wo_sb[:, dt * 1024 + et * 128:dt * 1024 + (et + 1) * 128],
                        ctx_s[ci][:, dt * 512:(dt + 1) * 512], start=(dt == 0), stop=(dt == 7)))
                if sq_pend is not None:
                    pet, pv = sq_pend
                    wait("tensor", b.SV, pv)
                    if pet == 0 and ("r2", cc - 1) in prev:
                        wait("tensor", b.SS, prev[("r2", cc - 1)])
                    vss = op("tensor", b.ST, lambda e, pet=pet: e.matmul(
                        b.p2[0:1, :], b.onesb[:, 0:1], sqt[pet % 2][:, :], start=(pet == 0), stop=False))
                    prev[("ssq", pet % 2)] = vss
                wait("vector", b.ST, vdm)
                vxa = op("vector", b.SV, lambda e, et=et, d_ps=d_ps, ci=ci: e.tensor_tensor(
                    xa_scr[ci][:, et * 512:(et + 1) * 512], d_ps[:, :],
                    xn_s[ci][:, et * 512:(et + 1) * 512], op=ALU.add))
                prev[("d_ps", et % 2)] = vxa
                vxa_last = vxa
                drain("vector")
                if ("ssq", et % 2) in prev:
                    wait("vector", b.ST, prev[("ssq", et % 2)])
                vsq = op("vector", b.SV, lambda e, et=et, ci=ci: e.tensor_tensor(
                    sqt[et % 2][:, :], xa_scr[ci][:, et * 512:(et + 1) * 512],
                    xa_scr[ci][:, et * 512:(et + 1) * 512], op=ALU.mult))
                drain("vector")
                sq_pend = (et, vsq)
            pet, pv = sq_pend
            wait("tensor", b.SV, pv)
            vss = op("tensor", b.ST, lambda e, pet=pet: e.matmul(
                b.p2[0:1, :], b.onesb[:, 0:1], sqt[pet % 2][:, :], start=False, stop=True))
            prev[("ssq", pet % 2)] = vss
            wait("scalar", b.ST, vss)
            emit("scalar", lambda e, cc=cc: e.activation(
                r2row[0:1, cc * 512:(cc + 1) * 512], b.p2[0:1, :], AF.Ln,
                bias=c[0:1, b.C_EPS6:b.C_EPS6 + 1], scale=1.0 / D))
            drain("scalar")
            vr2 = op("scalar", b.SS, lambda e, cc=cc: e.activation(
                r2row[0:1, cc * 512:(cc + 1) * 512], r2row[0:1, cc * 512:(cc + 1) * 512], AF.Exp, scale=-0.5))
            prev[("r2", cc)] = vr2
            prev[("ld", ci)] = (b.ST.n, vxa_last)
            if cc > 0:
                plan_passB(cc - 1)
        plan_passB(NCH - 1)
        self.end_phase()

    def _moe(self, tag, n_exp, inner, stream_dram, wu1, wu3, wdn, su1, su3, sdn,
             acc, cwbuf, cw_w, overwrite_first):
        b = self
        nit = inner // 128
        wA, wB, wC = self.wsA, self.wsB, self.wsC
        xs, hb, gsc = self.ms_xs, self.ms_hb, self.ms_gs
        dma, emit, op, wait, drain = b.dma, b.emit, b.op, b.wait, b.drain
        prev = self.ms_prev

        def load_w(which, e_idx):
            if e_idx == -1:
                up1, up3, dn = su1, su3, sdn
            else:
                up1, up3, dn = wu1[e_idx], wu3[e_idx], wdn[e_idx]
            v = None
            if which == "A" or which == "B":
                W = up1 if which == "A" else up3
                slot = wA if which == "A" else wB
                for h in range(2):
                    v = dma("gpsimd", b.SW,
                            slot[:, h * 4 * inner:(h + 1) * 4 * inner].rearrange("p (a n) -> p a n", n=inner),
                            W.rearrange("(a p) n -> p a n", p=128)[:, 4 * h:4 * h + 4, :])
            else:
                nh = max(1, nit // 4)
                step = nit // nh
                for h in range(nh):
                    v = dma("gpsimd", b.SW,
                            wC[:, h * step * 1024:(h + 1) * step * 1024].rearrange("p (a n) -> p a n", n=1024),
                            dn.rearrange("(a p) n -> p a n", p=128)[:, step * h:step * (h + 1), :])
            return v

        exps = [-1] + list(range(n_exp))
        lim = _P4_NEXP if tag == "p4" else _P3_NEXP
        if lim is not None:
            exps = exps[:lim]
        wait("tensor", b.SV, b.SV.n)
        wait("tensor", b.SS, b.SS.n)
        vA = load_w("A", exps[0]); vB = load_w("B", exps[0]); vC = load_w("C", exps[0])
        for xi, eidx in enumerate(exps):
            for tc in range(NCH):
                ci = tc % 2
                if (tag, "xs", ci) in prev:
                    wait("sync", b.ST, prev[(tag, "xs", ci)])
                vxs = b.chunk_load(xs[ci], stream_dram, tc)
                wait("tensor", b.SL, vxs)
                if tc == 0:
                    wait("tensor", b.SW, vA)
                    wait("tensor", b.SW, vB)
                if (tag, "down_st") in prev:
                    wait("vector", b.ST, prev[(tag, "down_st")])
                for it in range(nit):
                    g_ps = (b.p0, b.p1)[it % 2]
                    u_ps = (b.p2, b.p3)[it % 2]
                    if (tag, "g_ps", it % 2) in prev:
                        wait("tensor", b.SS, prev[(tag, "g_ps", it % 2)])
                    vgm = None
                    for dt in range(8):
                        vgm = op("tensor", b.ST, lambda e, dt=dt, it=it, g_ps=g_ps, ci=ci: e.matmul(
                            g_ps[:, :], wA[:, dt * inner + it * 128:dt * inner + (it + 1) * 128],
                            xs[ci][:, dt * 512:(dt + 1) * 512], start=(dt == 0), stop=(dt == 7)))
                    if (tag, "u_ps", it % 2) in prev:
                        wait("tensor", b.SV, prev[(tag, "u_ps", it % 2)])
                    vum = None
                    for dt in range(8):
                        vum = op("tensor", b.ST, lambda e, dt=dt, it=it, u_ps=u_ps, ci=ci: e.matmul(
                            u_ps[:, :], wB[:, dt * inner + it * 128:dt * inner + (it + 1) * 128],
                            xs[ci][:, dt * 512:(dt + 1) * 512], start=(dt == 0), stop=(dt == 7)))
                    wait("scalar", b.ST, vgm)
                    gs = gsc[it % 2]
                    if (tag, "gs", it % 2) in prev:
                        wait("scalar", b.SV, prev[(tag, "gs", it % 2)])
                    vsi = op("scalar", b.SS, lambda e, g_ps=g_ps, gs=gs: e.activation(
                        gs[:, :], g_ps[:, :], AF.Silu))
                    prev[(tag, "g_ps", it % 2)] = vsi
                    wait("vector", b.ST, vum)
                    wait("vector", b.SS, vsi)
                    vh = op("vector", b.SV, lambda e, it=it, u_ps=u_ps, gs=gs: e.tensor_tensor(
                        hb[:, it * 512:(it + 1) * 512], gs[:, :], u_ps[:, :], op=ALU.mult))
                    drain("vector")
                    prev[(tag, "u_ps", it % 2)] = vh
                    prev[(tag, "gs", it % 2)] = vh
                prev[(tag, "xs", ci)] = b.ST.n
                if self.dev and tag == "p4" and xi == 0 and tc == 0:
                    vtap = b.SV.n
                    b.wait("sync", b.SV, vtap)
                    b.dma("sync", b.SO, self.hb_d, hb[:, 0:16 * 512])
                    b.dma("sync", b.SO, self.wa_d, wA[:, 0:8 * inner])
                    b.dma("sync", b.SO, self.xs_d, xs[ci][:, :])
                    b.dma("sync", b.SO, self.gs_d, gsc[0][:, :])
                if tc == NCH - 1 and xi + 1 < len(exps):
                    wait("gpsimd", b.ST, b.ST.n)
                    vA = load_w("A", exps[xi + 1])
                    vB = load_w("B", exps[xi + 1])
                if tc == 0:
                    wait("tensor", b.SW, vC)
                wait("tensor", b.SV, b.SV.n)
                for tt in range(4):
                    ttg = tc * 4 + tt
                    d_ps = (b.pA, b.pB)[tt % 2]
                    if (tag, "d_ps", tt % 2) in prev:
                        wait("tensor", b.SV, prev[(tag, "d_ps", tt % 2)])
                    vdm = None
                    for dch in range(2):
                        for it in range(nit):
                            vdm = op("tensor", b.ST, lambda e, tt=tt, dch=dch, it=it, d_ps=d_ps: e.matmul(
                                d_ps[:, dch * 512:(dch + 1) * 512],
                                hb[:, it * 512 + tt * 128:it * 512 + tt * 128 + 128],
                                wC[:, it * 1024 + dch * 512:it * 1024 + dch * 512 + 512],
                                start=(it == 0), stop=(it == nit - 1)))
                    wait("vector", b.ST, vdm)
                    accs = acc[:, ttg * D:(ttg + 1) * D]
                    if overwrite_first and xi == 0:
                        vac = op("vector", b.SV, lambda e, d_ps=d_ps, accs=accs: e.tensor_scalar(
                            accs, d_ps[:, :], 1.0, None, op0=ALU.mult))
                    elif eidx == -1:
                        vac = op("vector", b.SV, lambda e, d_ps=d_ps, accs=accs: e.scalar_tensor_tensor(
                            accs, d_ps[:, :], 1.0, accs, op0=ALU.mult, op1=ALU.add))
                    else:
                        cwcol = cwbuf[:, ttg * cw_w + eidx:ttg * cw_w + eidx + 1]
                        vac = op("vector", b.SV, lambda e, d_ps=d_ps, accs=accs, cwcol=cwcol: e.scalar_tensor_tensor(
                            accs, d_ps[:, :], cwcol, accs, op0=ALU.mult, op1=ALU.add))
                    drain("vector")
                    prev[(tag, "d_ps", tt % 2)] = vac
                prev[(tag, "down_st")] = b.ST.n
            if xi + 1 < len(exps):
                wait("gpsimd", b.ST, b.ST.n)
                vC = load_w("C", exps[xi + 1])

    def p4_ffn(self):
        b = self
        self.begin_phase()
        self.yz = b.sb_persist("yz", (128, NT * D), BF16)
        self.wsA = b.sb("p4_wA", (128, 8 * IF), BF16)
        self.wsB = b.sb("p4_wB", (128, 8 * IF), BF16)
        self.wsC = b.sb("p4_wC", (128, 16 * D), BF16)
        self.ms_xs = [b.sb("p4_xs0", (128, 8 * 512), BF16), b.sb("p4_xs1", (128, 8 * 512), BF16)]
        self.ms_hb = b.sb("p4_hb", (128, 16 * 512), BF16)
        self.ms_gs = [b.sb("p4_g0", (128, 512), BF16), b.sb("p4_g1", (128, 512), BF16)]
        self.ms_prev = {}
        gw_sb = b.sb("p4_gw", (128, 8 * EF), BF16)
        exs = b.sb("p4_ex", (128, EF), F32)
        sco = b.sb("p4_sc", (128, EF), F32)
        bsd = b.sb("p4_bs", (128, EF), F32)
        mk1 = b.sb("p4_m1", (128, EF), F32)
        mk2 = b.sb("p4_m2", (128, EF), F32)
        mskd = b.sb("p4_mk", (128, EF), F32)
        c = b.cols
        mxc = c[:, 50:51]; nmx = c[:, 51:52]; smc = c[:, 52:53]; rsc = c[:, 53:54]
        m1c = c[:, 54:55]; m2c = c[:, 55:56]

        dma, emit, op, wait, drain = b.dma, b.emit, b.op, b.wait, b.drain
        vgw = dma("gpsimd", b.SW, gw_sb[:, :].rearrange("p (a n) -> p a n", n=EF),
                  b.gate_wT.rearrange("(a p) n -> p a n", p=128))
        xs = self.ms_xs
        prevR = {}
        for cc in range(NCH):
            ci = cc % 2
            if ("xs", ci) in prevR:
                wait("sync", b.ST, prevR[("xs", ci)])
            vxs = b.chunk_load(xs[ci], b.xnT, cc)
            wait("tensor", b.SL, vxs)
            wait("tensor", b.SW, vgw)
            for tt in range(4):
                ttg = cc * 4 + tt
                if "pB" in prevR:
                    wait("tensor", b.SV, prevR["pBv"])
                    wait("tensor", b.SS, prevR["pBs"])
                vrt = None
                for dt in range(8):
                    vrt = op("tensor", b.ST, lambda e, dt=dt, tt=tt, ci=ci: e.matmul(
                        b.pB[:, 0:EF], xs[ci][:, dt * 512 + tt * 128:dt * 512 + tt * 128 + 128],
                        gw_sb[:, dt * EF:(dt + 1) * EF], start=(dt == 0), stop=(dt == 7)))
                wait("vector", b.ST, vrt)
                emit("vector", lambda e: e.tensor_reduce(mxc, b.pB[:, 0:EF], AX.X, ALU.max))
                drain("vector")
                vnm = op("vector", b.SV, lambda e: e.tensor_scalar(nmx, mxc, -1.0, None, op0=ALU.mult))
                prevR["pB"] = True
                prevR["pBv"] = vnm
                wait("scalar", b.SV, vnm)
                vex = op("scalar", b.SS, lambda e: e.activation(
                    exs[:, :], b.pB[:, 0:EF], AF.Exp, bias=nmx, scale=1.0))
                prevR["pBs"] = vex
                wait("vector", b.SS, vex)
                emit("vector", lambda e: e.tensor_reduce(smc, exs[:, :], AX.X, ALU.add))
                drain("vector")
                emit("vector", lambda e: e.reciprocal(rsc, smc))
                drain("vector")
                emit("vector", lambda e: e.tensor_scalar(sco[:, :], exs[:, :], rsc, None, op0=ALU.mult))
                drain("vector")
                emit("vector", lambda e: e.tensor_tensor(bsd[:, :], sco[:, :], b.gbias_r[:, :], op=ALU.add))
                drain("vector")
                emit("vector", lambda e: e.tensor_reduce(m1c, bsd[:, :], AX.X, ALU.max))
                drain("vector")
                emit("vector", lambda e: e.tensor_scalar(mk1[:, :], bsd[:, :], m1c, None, op0=ALU.is_equal))
                drain("vector")
                emit("vector", lambda e: e.scalar_tensor_tensor(
                    mskd[:, :], mk1[:, :], -1e30, bsd[:, :], op0=ALU.mult, op1=ALU.add))
                drain("vector")
                emit("vector", lambda e: e.tensor_reduce(m2c, mskd[:, :], AX.X, ALU.max))
                drain("vector")
                emit("vector", lambda e: e.tensor_scalar(mk2[:, :], mskd[:, :], m2c, None, op0=ALU.is_equal))
                drain("vector")
                emit("vector", lambda e: e.tensor_tensor(mk1[:, :], mk1[:, :], mk2[:, :], op=ALU.add))
                drain("vector")
                op("vector", b.SV, lambda e, ttg=ttg: e.tensor_tensor(
                    b.cwf[:, ttg * EF:(ttg + 1) * EF], mk1[:, :], sco[:, :], op=ALU.mult))
                drain("vector")
            prevR[("xs", ci)] = b.ST.n
        self.ms_prev[("p4", "xs", 0)] = b.ST.n
        self.ms_prev[("p4", "xs", 1)] = b.ST.n
        self._moe("p4", EF, IF, b.xnT,
                  [self.w1T[e] for e in range(EF)], [self.w3T[e] for e in range(EF)],
                  [self.w2T[e] for e in range(EF)],
                  self.sw1T, self.sw3T, self.sw2T, self.yz, b.cwf, EF, True)
        if self.dev:
            b.wait("sync", b.SV, b.SV.n)
            b.dma("sync", b.SO, self.yz_d, self.yz[:, :])
        self.end_phase()

    def p3_amoe(self):
        b = self
        self.begin_phase()
        self.sr = b.sb_persist("sr", (128, NT * D), BF16)
        self.wsA = b.sb("p3_wA", (128, 8 * IA), BF16)
        self.wsB = b.sb("p3_wB", (128, 8 * IA), BF16)
        self.wsC = b.sb("p3_wC", (128, 8 * D), BF16)
        self.ms_xs = [b.sb("p3_xs0", (128, 8 * 512), BF16), b.sb("p3_xs1", (128, 8 * 512), BF16)]
        self.ms_hb = b.sb("p3_hb", (128, 8 * 512), BF16)
        self.ms_gs = [b.sb("p3_g0", (128, 512), BF16), b.sb("p3_g1", (128, 512), BF16)]
        self.ms_prev = {}
        vin = None
        for q in range(8):
            vin = b.dma("sync", b.SL, self.sr[:, q * 4 * D:(q + 1) * 4 * D].rearrange("p (t n) -> p t n", n=D),
                        b.xaT.rearrange("(t p) n -> p t n", p=128)[:, 4 * q:4 * q + 4, :])
        b.wait("vector", b.SL, vin)
        self._moe("p3", EA, IA, b.h2nT,
                  [self.egT[e] for e in range(EA)], [self.euT[e] for e in range(EA)],
                  [self.edT[e] for e in range(EA)],
                  self.sgT, self.suT, self.sdT, self.sr, b.cwa, EA, False)
        if self.dev:
            b.wait("sync", b.SV, b.SV.n)
            b.dma("sync", b.SO, self.sr_d, self.sr[:, :])
        self.end_phase()

    def p5_combine(self):
        b = self
        self.begin_phase()
        xin = [b.sb("p5_x0", (128, D), F32), b.sb("p5_x1", (128, D), F32)]
        gsc = [b.sb("p5_g0", (128, D), F32), b.sb("p5_g1", (128, D), F32)]
        tsc = [b.sb("p5_t0", (128, D), F32), b.sb("p5_t1", (128, D), F32)]
        osb = [b.sb("p5_o0", (128, D), F32), b.sb("p5_o1", (128, D), F32)]
        dma, emit, op, wait, drain = b.dma, b.emit, b.op, b.wait, b.drain
        prev = {}
        for t in range(NT):
            pi = t % 2
            if ("vo", pi) in prev:
                wait("sync", b.SV, prev[("vo", pi)])
            vx = dma("sync", b.SL, xin[pi][:, :], b.x[t * 128:(t + 1) * 128, :])
            if ("vt", pi) in prev:
                wait("scalar", b.SV, prev[("vt", pi)])
            vg = op("scalar", b.SS, lambda e, t=t, pi=pi: e.activation(
                gsc[pi][:, :], self.yz[:, t * D:(t + 1) * D], AF.Silu))
            wait("vector", b.SS, vg)
            vt = op("vector", b.SV, lambda e, t=t, pi=pi: e.tensor_tensor(
                tsc[pi][:, :], self.sr[:, t * D:(t + 1) * D], gsc[pi][:, :], op=ALU.mult))
            drain("vector")
            prev[("vt", pi)] = vt
            wait("vector", b.SL, vx)
            if ("st", pi) in prev:
                wait("vector", b.SO, prev[("st", pi)])
            vo = op("vector", b.SV, lambda e, pi=pi: e.tensor_tensor(
                osb[pi][:, :], tsc[pi][:, :], xin[pi][:, :], op=ALU.add))
            drain("vector")
            prev[("vo", pi)] = vo
            wait("sync", b.SV, vo)
            vst = dma("sync", b.SO, b.out[t * 128:(t + 1) * 128, :], osb[pi][:, :])
            prev[("st", pi)] = vst
        b.wait("sync", b.SO, b.SO.n)
        self.end_phase()


def build_core(dev=False, phases=_PHASES):
    return Builder(dev=dev, phases=phases).build()



def _constants():
    pos = np.arange(T, dtype=np.float32)
    inv_freq = 1.0 / (10000.0 ** (np.arange(0, DH, 2, dtype=np.float32) / DH))
    freqs = np.concatenate([inv_freq, inv_freq])[:, None] * pos[None, :]
    cos_c = np.cos(freqs).astype(np.float32)
    sin_c = np.sin(freqs).astype(np.float32)
    masks = np.zeros((4, 128, 512), dtype=np.float32)
    for a in range(4):
        kk = a * 128 + np.arange(128)[:, None]
        qq = np.arange(512)[None, :]
        masks[a] = np.where(kk <= qq, 0.0, -1e9)
    masks_c = np.ascontiguousarray(masks.reshape(4 * 128, 512))
    ones = np.ones((128, 128), dtype=np.float32)
    return dict(cos_c=cos_c, sin_c=sin_c, masks_c=masks_c, ones_f32=ones)


def _tr(a):
    return np.ascontiguousarray(np.swapaxes(np.asarray(a), -1, -2))


def kernel(**inp):
    consts = _constants()
    w = dict(
        wqT=_tr(inp["L0_attn_wq"]), wkT=_tr(inp["L0_attn_wk"]),
        wvT=_tr(inp["L0_attn_wv"]), woT=_tr(inp["L0_attn_wo"]),
        nw1=np.ascontiguousarray(np.asarray(inp["L0_attn_norm1_w"])[None, :]),
        nw2=np.ascontiguousarray(np.asarray(inp["L0_attn_norm2_w"])[None, :]),
        router_wT=_tr(inp["L0_attn_router_w"]),
        ebias=np.ascontiguousarray(np.asarray(inp["L0_attn_expert_bias"])[None, :]),
        sgT=_tr(inp["L0_attn_sg"]), suT=_tr(inp["L0_attn_su"]), sdT=_tr(inp["L0_attn_sd"]),
        egT=_tr(inp["L0_attn_eg"]), euT=_tr(inp["L0_attn_eu"]), edT=_tr(inp["L0_attn_ed"]),
        gate_wT=_tr(inp["L0_ffn_gate_w"]),
        gbias=np.ascontiguousarray(np.asarray(inp["L0_ffn_gate_bias"])[None, :]),
        w1T=_tr(inp["L0_ffn_w1"]), w3T=_tr(inp["L0_ffn_w3"]), w2T=_tr(inp["L0_ffn_w2"]),
        sw1T=_tr(inp["L0_ffn_sw1"]), sw3T=_tr(inp["L0_ffn_sw3"]), sw2T=_tr(inp["L0_ffn_sw2"]),
    )
    xfull = np.asarray(inp["L0_attn_x"])
    nc = build_core(dev=_DEV)
    in_maps = []
    for cc in range(8):
        m = dict(x=np.ascontiguousarray(xfull[cc]))
        m.update(w)
        m.update(consts)
        in_maps.append(m)
    res = run_bass_kernel_spmd(nc, in_maps, core_ids=list(range(8)))
    return np.stack([np.asarray(r["out"]) for r in res.results], axis=0)


# version 23
# speedup vs baseline: 1.1173x; 1.0100x over previous
import numpy as np
import ml_dtypes
from contextlib import ExitStack

import concourse.bass as bass
import concourse.mybir as mybir
from concourse.bass_utils import run_bass_kernel_spmd

F32 = mybir.dt.float32
BF16 = mybir.dt.bfloat16
AF = mybir.ActivationFunctionType
ALU = mybir.AluOpType
AX = mybir.AxisListType

T = 4096
D = 1024
NT = 32
NCH = 8
NH = 16
DH = 64
NG = 8
IA = 1024
IF = 2048
EA = 16
EF = 8

_DEV = False
_PHASES = (0, 1, 2, 4, 3, 5)
_LIM_P2 = NCH
_P4_NEXP = None
_P3_NEXP = None
_P2_PASSB = True
_P2_ROUTER = True
_P2_XASPILL = True
_P2_H2SPILL = True
_P2_BCAST = True
_LIM_P1G = NG
ACT_SET_LNEXP = 6

ENGINES = ("sync", "gpsimd", "vector", "scalar", "tensor")


class Sem:
    __slots__ = ("h", "n", "name")

    def __init__(self, h, name):
        self.h, self.n, self.name = h, 0, name


class Builder:
    def __init__(self, dev=False, phases=_PHASES):
        self.dev = dev
        self.phases = phases
        nc = self.nc = bass.Bass("TRN2", target_bir_lowering=False)
        self.es = ExitStack()
        self.ops = None

        def din(name, shape, dt=F32):
            return nc.dram_tensor(name, list(shape), dt, kind="ExternalInput").ap()

        self.x = din("x", (T, D))
        self.wqT = din("wqT", (D, D)); self.wkT = din("wkT", (D, D))
        self.wvT = din("wvT", (D, D)); self.woT = din("woT", (D, D))
        self.nw1 = din("nw1", (1, D)); self.nw2 = din("nw2", (1, D))
        self.router_wT = din("router_wT", (D, EA)); self.ebias = din("ebias", (1, EA))
        self.sgT = din("sgT", (D, IA)); self.suT = din("suT", (D, IA)); self.sdT = din("sdT", (IA, D))
        self.egT = din("egT", (EA, D, IA)); self.euT = din("euT", (EA, D, IA)); self.edT = din("edT", (EA, IA, D))
        self.gate_wT = din("gate_wT", (D, EF)); self.gbias = din("gbias", (1, EF))
        self.w1T = din("w1T", (EF, D, IF)); self.w3T = din("w3T", (EF, D, IF)); self.w2T = din("w2T", (EF, IF, D))
        self.sw1T = din("sw1T", (D, IF)); self.sw3T = din("sw3T", (D, IF)); self.sw2T = din("sw2T", (IF, D))
        self.cos_c = din("cos_c", (DH, T)); self.sin_c = din("sin_c", (DH, T))
        self.masks_c = din("masks_c", (4 * 128, 512))
        self.ones_f32 = din("ones_f32", (128, 128))
        self.ident_c = din("ident_c", (128, 128))

        self.out = nc.dram_tensor("out", [T, D], F32, kind="ExternalOutput").ap()
        sk = "ExternalOutput" if dev else "Internal"
        self.yz_d = nc.dram_tensor("yz_d", [128, NT * D], BF16, kind=sk).ap() if dev else None
        self.hb_d = nc.dram_tensor("hb_d", [128, 16 * 512], BF16, kind=sk).ap() if dev else None
        self.wa_d = nc.dram_tensor("wa_d", [128, 8 * 2048], BF16, kind=sk).ap() if dev else None
        self.xs_d = nc.dram_tensor("xs_d", [128, 8 * 512], BF16, kind=sk).ap() if dev else None
        self.gs_d = nc.dram_tensor("gs_d", [128, 512], BF16, kind=sk).ap() if dev else None
        self.sr_d = nc.dram_tensor("sr_d", [128, NT * D], BF16, kind=sk).ap() if dev else None
        self.xnT = nc.dram_tensor("xnT", [D, T], BF16, kind=sk).ap()
        self.xnrT = nc.dram_tensor("xnrT", [D, T], BF16, kind=sk).ap()
        self.ctxT = nc.dram_tensor("ctxT", [D, T], BF16, kind=sk).ap()
        self.h2nT = nc.dram_tensor("h2nT", [D, T], BF16, kind=sk).ap()
        self.xaT = nc.dram_tensor("xaT", [T, D], BF16, kind=sk).ap()

        self.sems = []
        for nm in ("SL", "SO", "SW", "SX", "ST", "SV", "SS", "SG"):
            s = Sem(self.es.enter_context(nc.semaphore(nm)), nm)
            setattr(self, nm, s)
            self.sems.append(s)

        sb = self.sb
        self.cols = sb("cols", (128, 112), F32)
        self.onesb = sb("onesb", (128, 128), BF16)
        self.onesf = sb("onesf", (128, 128), F32)
        self.identb = sb("identb", (128, 128), BF16)
        self.ebias_r = sb("ebias_r", (128, EA), F32)
        self.gbias_r = sb("gbias_r", (128, EF), F32)
        self.cwa = sb("cwa", (128, NT * EA), F32)
        self.cwf = sb("cwf", (128, NT * EF), F32)
        self.pA = self.ps("pA", (128, 1024))
        self.pB = self.ps("pB", (128, 1024))
        self.p0 = self.ps("p0", (128, 512))
        self.p1 = self.ps("p1", (128, 512))
        self.p2 = self.ps("p2", (128, 512))
        self.p3 = self.ps("p3", (128, 512))

        self.C_NW1, self.C_NW1Q, self.C_EPS6, self.C_EPS5 = 0, 8, 16, 17
        self.C_NW2 = 60

    def sb(self, name, shape, dt):
        es = self.phase_es if getattr(self, "phase_es", None) is not None else self.es
        return es.enter_context(self.nc.sbuf_tensor(name, list(shape), dt))

    def sb_persist(self, name, shape, dt):
        return self.es.enter_context(self.nc.sbuf_tensor(name, list(shape), dt))

    def ps(self, name, shape):
        return self.es.enter_context(self.nc.psum_tensor(name, list(shape), F32))

    def begin_phase(self):
        self.phase_es = ExitStack()
        self._hwm = {}
        self.ops = {e: [] for e in ENGINES}
        for eng in ENGINES:
            for s in self.sems:
                if s.n > 0:
                    self.wait(eng, s, s.n)

    def end_phase(self):
        ops = self.ops
        with self.nc.Block() as block:
            @block.sync
            def _(e):
                for fn in ops["sync"]:
                    fn(e)

            @block.gpsimd
            def _(e):
                for fn in ops["gpsimd"]:
                    fn(e)

            @block.vector
            def _(e):
                for fn in ops["vector"]:
                    fn(e)

            @block.scalar
            def _(e):
                for fn in ops["scalar"]:
                    fn(e)

            @block.tensor
            def _(e):
                for fn in ops["tensor"]:
                    fn(e)
        self.ops = None
        self.phase_es.close()
        self.phase_es = None

    def emit(self, eng, fn):
        self.ops[eng].append(fn)

    def wait(self, eng, sem, val):
        if not val or val <= 0:
            return
        key = (eng, sem.name)
        if self._hwm.get(key, -1) >= val:
            return
        self._hwm[key] = val
        self.emit(eng, lambda e, s=sem.h, v=val: e.wait_ge(s, v))

    def dma(self, eng, sem, dst, src, **kw):
        sem.n += 16
        v = sem.n
        self.emit(eng, lambda e, d=dst, s=src, sm=sem.h, kw=kw: e.dma_start(d, s, **kw).then_inc(sm, 16))
        return v

    def dmat(self, dst, src):
        self.SX.n += 16
        v = self.SX.n
        self.emit("scalar", lambda e, d=dst, s=src: e.dma_start(d, s, transpose=True).then_inc(self.SX.h, 16))
        return v

    def op(self, eng, sem, f):
        if sem is None:
            self.emit(eng, f)
            return None
        sem.n += 1
        v = sem.n
        self.emit(eng, lambda e, f=f, s=sem.h: f(e).then_inc(s, 1))
        return v

    def drain(self, eng):
        self.emit(eng, lambda e: e.drain())

    def preload(self, set_id):
        pass

    def chunk_load(self, dstbuf, src, c, half_elems=512):
        vr = None
        for h in range(2):
            vr = self.dma("sync", self.SL,
                          dstbuf[:, h * 4 * half_elems:(h + 1) * 4 * half_elems]
                          .rearrange("p (a n) -> p a n", n=half_elems),
                          src.rearrange("(a p) m -> p a m", p=128)[:, 4 * h:4 * h + 4,
                                                                  c * half_elems:(c + 1) * half_elems])
        return vr

    def build(self):
        self.startup()
        if 0 in self.phases:
            self.p0_layernorm()
        if 1 in self.phases:
            self.p1_attention()
        if 2 in self.phases:
            self.p2_wo_norm2()
        if 4 in self.phases:
            self.p4_ffn()
        if 3 in self.phases:
            self.p3_amoe()
        if 5 in self.phases:
            self.p5_combine()
        self.es.close()
        return self.nc

    def startup(self):
        self.begin_phase()
        b = self
        b.preload(ACT_SET_LNEXP)
        vof = b.dma("sync", b.SL, b.onesf[:, :], b.ones_f32)
        b.dma("sync", b.SL, b.cols[:, b.C_NW1:b.C_NW1 + 8], b.nw1.rearrange("o (a p) -> p (o a)", p=128),
              allow_slow_non_contiguous=True)
        b.dma("sync", b.SL, b.cols[:, b.C_NW2:b.C_NW2 + 8], b.nw2.rearrange("o (a p) -> p (o a)", p=128),
              allow_slow_non_contiguous=True)
        eb_row = b.sb("eb_row", (1, EA), F32)
        gb_row = b.sb("gb_row", (1, EF), F32)
        b.dma("sync", b.SL, eb_row[0:1, :], b.ebias)
        v = b.dma("sync", b.SL, gb_row[0:1, :], b.gbias)
        eb_hl = b.sb("eb_hl", (1, 2 * EA), BF16)
        gb_hl = b.sb("gb_hl", (1, 2 * EF), BF16)
        identf = b.sb("identf", (128, 128), F32)
        vid = b.dma("sync", b.SL, identf[:, :], b.ident_c)
        b.wait("vector", b.SL, v)
        b.wait("vector", b.SL, vof)
        b.wait("vector", b.SL, vid)
        b.emit("vector", lambda e: e.tensor_copy(b.onesb[:, :], b.onesf[:, :]))
        b.drain("vector")
        b.emit("vector", lambda e: e.tensor_copy(b.identb[:, :], identf[:, :]))
        b.drain("vector")
        b.emit("vector", lambda e: e.tensor_copy(eb_hl[0:1, 0:EA], eb_row[0:1, :]))
        b.emit("vector", lambda e: e.tensor_copy(gb_hl[0:1, 0:EF], gb_row[0:1, :]))
        b.drain("vector")
        b.emit("vector", lambda e: e.tensor_tensor(eb_hl[0:1, EA:2 * EA], eb_row[0:1, :], eb_hl[0:1, 0:EA], op=ALU.subtract))
        vbl = b.op("vector", b.SV, lambda e: e.tensor_tensor(gb_hl[0:1, EF:2 * EF], gb_row[0:1, :], gb_hl[0:1, 0:EF], op=ALU.subtract))
        b.drain("vector")
        b.wait("tensor", b.SV, vbl)
        b.emit("tensor", lambda e: e.matmul(b.pA[:, 0:EA], b.onesb[0:1, 0:128], eb_hl[0:1, 0:EA], start=True, stop=False))
        b.emit("tensor", lambda e: e.matmul(b.pA[:, 0:EA], b.onesb[0:1, 0:128], eb_hl[0:1, EA:2 * EA], start=False, stop=True))
        b.emit("tensor", lambda e: e.matmul(b.pB[:, 0:EF], b.onesb[0:1, 0:128], gb_hl[0:1, 0:EF], start=True, stop=False))
        vbb = b.op("tensor", b.ST, lambda e: e.matmul(b.pB[:, 0:EF], b.onesb[0:1, 0:128], gb_hl[0:1, EF:2 * EF], start=False, stop=True))
        b.wait("scalar", b.ST, vbb)
        b.emit("scalar", lambda e: e.copy(b.ebias_r[:, :], b.pA[:, 0:EA]))
        b.op("scalar", b.SS, lambda e: e.copy(b.gbias_r[:, :], b.pB[:, 0:EF]))
        b.emit("vector", lambda e: e.memset(b.cols[:, b.C_EPS6:b.C_EPS6 + 1], 1e-6))
        b.emit("vector", lambda e: e.memset(b.cols[:, b.C_EPS5:b.C_EPS5 + 1], 1e-5))
        b.drain("vector")
        b.op("vector", b.SV, lambda e: e.tensor_scalar(
            b.cols[:, b.C_NW1Q:b.C_NW1Q + 8], b.cols[:, b.C_NW1:b.C_NW1 + 8], 0.125, None, op0=ALU.mult))
        self.end_phase()

    def p0_layernorm(self):
        b = self
        self.begin_phase()
        BT = 8
        xin = [b.sb("p0_x0", (128, BT * D), F32), b.sb("p0_x1", (128, BT * D), F32)]
        sqo = b.sb("p0_sqo", (128, D), F32)
        xn_tm = [b.sb("p0_n0", (128, D), BF16), b.sb("p0_n1", (128, D), BF16)]
        xr_tm = [b.sb("p0_r0", (128, D), BF16), b.sb("p0_r1", (128, D), BF16)]
        stg = [b.sb("p0_s0", (128, 2 * D), BF16), b.sb("p0_s1", (128, 2 * D), BF16)]
        c = b.cols
        CS, CQ, CM, CV, CR, CA, CX = 20, 28, 36, 44, 52, 68, 76
        last = {}
        for blk in range(NT // BT):
            bi = blk % 2
            t0 = blk * BT
            if ("blk", bi) in last:
                b.wait("sync", b.SV, last[("blk", bi)])
                b.wait("sync", b.SS, last[("sq", bi)])
            vx = b.dma("sync", b.SL, xin[bi][:, :].rearrange("p (a n) -> p a n", n=D),
                       b.x.rearrange("(a p) n -> p a n", p=128)[:, t0:t0 + BT, :])
            b.wait("vector", b.SL, vx)
            b.wait("scalar", b.SL, vx)
            if "chain" in last:
                b.wait("vector", b.SS, last["chain_s"])
            for j in range(BT):
                b.emit("vector", lambda e, j=j, bi=bi: e.tensor_reduce(
                    c[:, CS + j:CS + j + 1], xin[bi][:, j * D:(j + 1) * D], AX.X, ALU.add))
            b.drain("vector")
            b.emit("vector", lambda e: e.tensor_scalar(c[:, CM:CM + BT], c[:, CS:CS + BT], 1.0 / D, None, op0=ALU.mult))
            b.drain("vector")
            vsq = None
            for j in range(BT):
                vsq = b.op("scalar", b.SS, lambda e, j=j, bi=bi: e.activation(
                    sqo[:, :], xin[bi][:, j * D:(j + 1) * D], AF.Square, accum_out=c[:, CQ + j:CQ + j + 1]))
            last[("sq", bi)] = vsq
            b.wait("vector", b.SS, vsq)
            b.emit("vector", lambda e: e.tensor_tensor(c[:, CA:CA + BT], c[:, CM:CM + BT], c[:, CM:CM + BT], op=ALU.mult))
            b.drain("vector")
            vvar = b.op("vector", b.SV, lambda e: e.scalar_tensor_tensor(
                c[:, CV:CV + BT], c[:, CQ:CQ + BT], 1.0 / D, c[:, CA:CA + BT], op0=ALU.mult, op1=ALU.subtract))
            b.drain("vector")
            b.wait("scalar", b.SV, vvar)
            b.emit("scalar", lambda e: e.activation(c[:, CA:CA + BT], c[:, CV:CV + BT], AF.Ln, bias=c[:, b.C_EPS5:b.C_EPS5 + 1], scale=1.0))
            b.drain("scalar")
            b.emit("scalar", lambda e: e.activation(c[:, CR:CR + BT], c[:, CA:CA + BT], AF.Exp, scale=-0.5))
            b.drain("scalar")
            vs2 = b.op("scalar", b.SS, lambda e: e.activation(c[:, CA:CA + BT], c[:, CR:CR + BT], AF.Square))
            b.wait("vector", b.SS, vs2)
            vv2 = b.op("vector", b.SV, lambda e: e.tensor_tensor(c[:, CA:CA + BT], c[:, CV:CV + BT], c[:, CA:CA + BT], op=ALU.mult))
            b.drain("vector")
            b.wait("scalar", b.SV, vv2)
            b.emit("scalar", lambda e: e.activation(c[:, CA:CA + BT], c[:, CA:CA + BT], AF.Ln, bias=c[:, b.C_EPS6:b.C_EPS6 + 1], scale=1.0))
            b.drain("scalar")
            vs3 = b.op("scalar", b.SS, lambda e: e.activation(c[:, CA:CA + BT], c[:, CA:CA + BT], AF.Exp, scale=-0.5))
            b.wait("vector", b.SS, vs3)
            vch = b.op("vector", b.SV, lambda e: e.tensor_tensor(c[:, CX:CX + BT], c[:, CR:CR + BT], c[:, CA:CA + BT], op=ALU.mult))
            b.drain("vector")
            last["chain"] = True
            last["chain_s"] = vs3
            vxn_last = None
            for j in range(BT):
                t = t0 + j
                pi = t % 2
                if ("tp", pi) in last:
                    b.wait("vector", b.ST, last[("tp", pi)])
                b.emit("vector", lambda e, j=j, bi=bi, pi=pi: e.tensor_scalar(
                    xn_tm[pi][:, :], xin[bi][:, j * D:(j + 1) * D],
                    c[:, CM + j:CM + j + 1], c[:, CR + j:CR + j + 1], op0=ALU.subtract, op1=ALU.mult))
                vxn = b.op("vector", b.SV, lambda e, j=j, bi=bi, pi=pi: e.tensor_scalar(
                    xr_tm[pi][:, :], xin[bi][:, j * D:(j + 1) * D],
                    c[:, CM + j:CM + j + 1], c[:, CX + j:CX + j + 1], op0=ALU.subtract, op1=ALU.mult))
                b.drain("vector")
                vxn_last = vxn
                if "ev" in last:
                    b.wait("tensor", b.SS, last["ev"])
                b.wait("tensor", b.SV, vxn)
                vtp = None
                for k in range(8):
                    pd = (b.p0, b.p1)[k // 4]
                    vtp = b.op("tensor", b.ST, lambda e, k=k, pd=pd, pi=pi: e.matmul(
                        pd[:, (k % 4) * 128:(k % 4 + 1) * 128], xn_tm[pi][:, k * 128:(k + 1) * 128],
                        b.identb[:, :], start=True, stop=True))
                for k in range(8):
                    pd = (b.p2, b.p3)[k // 4]
                    vtp = b.op("tensor", b.ST, lambda e, k=k, pd=pd, pi=pi: e.matmul(
                        pd[:, (k % 4) * 128:(k % 4 + 1) * 128], xr_tm[pi][:, k * 128:(k + 1) * 128],
                        b.identb[:, :], start=True, stop=True))
                last[("tp", pi)] = vtp
                if ("st", pi) in last:
                    b.wait("scalar", b.SO, last[("st", pi)])
                b.wait("scalar", b.ST, vtp)
                vev = None
                for q, pd in enumerate((b.p0, b.p1, b.p2, b.p3)):
                    vev = b.op("scalar", b.SS, lambda e, q=q, pd=pd, pi=pi: e.copy(
                        stg[pi][:, q * 512:(q + 1) * 512], pd[:, :]))
                last["ev"] = vev
                b.wait("sync", b.SS, vev)
                for (dst, off) in ((b.xnT, 0), (b.xnrT, D)):
                    for h in range(2):
                        vsp = b.dma("sync", b.SO,
                                    dst.rearrange("(a p) m -> p a m", p=128)[:, 4 * h:4 * h + 4, t * 128:(t + 1) * 128],
                                    stg[pi][:, off + h * 512:off + (h + 1) * 512].rearrange("p (a n) -> p a n", n=128))
                last[("st", pi)] = vsp
            last[("blk", bi)] = vxn_last
        self.end_phase()

    def p1_attention(self):
        b = self
        self.begin_phase()
        tabs = b.sb("p1_tabs", (DH, 2 * T), F32)
        masks = b.sb("p1_masks", (128, 4 * 512), BF16)
        masksf = b.sb("p1_masksf", (128, 4 * 512), F32)
        qg = b.sb("p1_qg", (128, T), BF16)
        kg = b.sb("p1_kg", (128, T), BF16)
        vg = b.sb("p1_vg", (128, NT * 130), BF16)
        vtg = b.sb("p1_vtg", (128, T), BF16)
        xs = [b.sb("p1_xs0", (128, 8 * 512), BF16), b.sb("p1_xs1", (128, 8 * 512), BF16)]
        wraw = b.sb("p1_wraw", (128, 3 * 1024), BF16)
        weff = b.sb("p1_weff", (128, 3 * 1024), BF16)
        pb = b.sb("p1_pb", (128, 4 * 512), BF16)
        t1s = b.sb("p1_t1", (128, 512), F32)
        t2s = b.sb("p1_t2", (128, 512), F32)
        rbs = b.sb("p1_rb", (64, 512), F32)
        rls = b.sb("p1_rl", (128, 2048), F32)
        rlhl = b.sb("p1_rlhl", (128, 2048), BF16)
        cst = b.sb("p1_cst", (128, 2 * 512), BF16)
        vstg = b.sb("p1_vstg", (128, 2 * 512), BF16)

        dma, emit, op, wait, drain = b.dma, b.emit, b.op, b.wait, b.drain
        b.preload(ACT_SET_LNEXP)
        dma("sync", b.SL, tabs[:, 0:T], b.cos_c)
        dma("sync", b.SL, tabs[:, T:2 * T], b.sin_c)
        v_tabs = dma("sync", b.SL, masksf[:, :].rearrange("p (a n) -> p a n", n=512),
                     b.masks_c.rearrange("(a p) n -> p a n", p=128))
        wait("vector", b.SL, v_tabs)
        emit("vector", lambda e: e.tensor_copy(masks[:, :], masksf[:, :]))
        drain("vector")
        emit("vector", lambda e: e.memset(
            vg[:, :].rearrange("p (t h n) -> p t h n", h=2, n=65)[:, :, :, 64:65], 1.0))
        drain("vector")

        prev = {}
        for g in range(_LIM_P1G):
            if g > 0:
                wait("gpsimd", b.ST, prev["qkv_done"])
            vw = None
            for j, W in enumerate((b.wqT, b.wkT, b.wvT)):
                for h in range(2):
                    vw = dma("gpsimd", b.SW,
                             wraw[:, j * 1024 + h * 512:j * 1024 + (h + 1) * 512].rearrange("p (a n) -> p a n", n=128),
                             W[:, g * 128:(g + 1) * 128].rearrange("(a p) n -> p a n", p=128)[:, 4 * h:4 * h + 4, :])
            wait("gpsimd", b.SW, vw)
            vfold = None
            for j in range(3):
                ccol = b.C_NW1Q if j == 0 else b.C_NW1
                for dt in range(8):
                    vfold = b.op("gpsimd", b.SG, lambda e, j=j, dt=dt, ccol=ccol: e.tensor_scalar(
                        weff[:, j * 1024 + dt * 128:j * 1024 + (dt + 1) * 128],
                        wraw[:, j * 1024 + dt * 128:j * 1024 + (dt + 1) * 128],
                        b.cols[:, ccol + dt:ccol + dt + 1], None, op0=ALU.mult))
            if "attn_done" in prev:
                wait("vector", b.ST, prev["attn_done"])
                wait("scalar", b.ST, prev["attn_done"])
            ropeq = {}; ropek = {}; vsec = {}; vcpc = {}
            for c in range(NCH):
                ci = c % 2
                if ("xs", ci) in prev:
                    wait("sync", b.ST, prev[("xs", ci)])
                vxs = b.chunk_load(xs[ci], b.xnrT, c)
                wait("tensor", b.SL, vxs)
                wait("tensor", b.SG, vfold)
                if ("q_ps", ci) in prev:
                    wait("tensor", b.SV, prev[("q_ps", ci)])
                if ("k_ps", ci) in prev:
                    wait("tensor", b.SV, prev[("k_ps", ci)])
                if ("v_ps", ci) in prev:
                    wait("tensor", b.SS, prev[("v_ps", ci)])
                if g > 0 and c == 0:
                    wait("tensor", b.SS, b.SS.n)
                    wait("tensor", b.SV, b.SV.n)
                q_ps = (b.p0, b.p1)[ci]; k_ps = (b.p2, b.p3)[ci]
                v_ps = b.pA[:, ci * 512:(ci + 1) * 512]
                for dt in range(8):
                    emit("tensor", lambda e, dt=dt, q_ps=q_ps, ci=ci: e.matmul(
                        q_ps[:, :], weff[:, dt * 128:(dt + 1) * 128],
                        xs[ci][:, dt * 512:(dt + 1) * 512], start=(dt == 0), stop=(dt == 7)))
                for dt in range(8):
                    emit("tensor", lambda e, dt=dt, k_ps=k_ps, ci=ci: e.matmul(
                        k_ps[:, :], weff[:, 1024 + dt * 128:1024 + (dt + 1) * 128],
                        xs[ci][:, dt * 512:(dt + 1) * 512], start=(dt == 0), stop=(dt == 7)))
                vmm = None
                for dt in range(8):
                    vmm = op("tensor", b.ST, lambda e, dt=dt, v_ps=v_ps, ci=ci: e.matmul(
                        v_ps, weff[:, 2048 + dt * 128:2048 + (dt + 1) * 128],
                        xs[ci][:, dt * 512:(dt + 1) * 512], start=(dt == 0), stop=(dt == 7)))
                prev[("xs", ci)] = vmm
                prev["qkv_done"] = vmm
                wait("vector", b.ST, vmm)
                for which, sp, dst in (("q", q_ps, qg), ("k", k_ps, kg)):
                    emit("vector", lambda e, sp=sp, c=c: e.tensor_tensor(
                        t1s[0:64, :], sp[0:64, :], tabs[:, c * 512:(c + 1) * 512], op=ALU.mult))
                    emit("vector", lambda e, sp=sp, c=c: e.tensor_tensor(
                        t1s[64:128, :], sp[64:128, :], tabs[:, c * 512:(c + 1) * 512], op=ALU.mult))
                    emit("vector", lambda e, sp=sp, c=c: e.tensor_tensor(
                        t2s[0:32, :], sp[32:64, :], tabs[0:32, T + c * 512:T + c * 512 + 512], op=ALU.mult))
                    emit("vector", lambda e, sp=sp, c=c: e.tensor_tensor(
                        t2s[32:64, :], sp[0:32, :], tabs[32:64, T + c * 512:T + c * 512 + 512], op=ALU.mult))
                    emit("vector", lambda e, sp=sp, c=c: e.tensor_tensor(
                        t2s[64:96, :], sp[96:128, :], tabs[0:32, T + c * 512:T + c * 512 + 512], op=ALU.mult))
                    emit("vector", lambda e, sp=sp, c=c: e.tensor_tensor(
                        t2s[96:128, :], sp[64:96, :], tabs[32:64, T + c * 512:T + c * 512 + 512], op=ALU.mult))
                    drain("vector")
                    emit("vector", lambda e, d=dst, c=c: e.tensor_tensor(
                        d[0:32, c * 512:(c + 1) * 512], t1s[0:32, :], t2s[0:32, :], op=ALU.subtract))
                    emit("vector", lambda e, d=dst, c=c: e.tensor_tensor(
                        d[32:64, c * 512:(c + 1) * 512], t1s[32:64, :], t2s[32:64, :], op=ALU.add))
                    emit("vector", lambda e, d=dst, c=c: e.tensor_tensor(
                        d[64:96, c * 512:(c + 1) * 512], t1s[64:96, :], t2s[64:96, :], op=ALU.subtract))
                    vlast = op("vector", b.SV, lambda e, d=dst, c=c: e.tensor_tensor(
                        d[96:128, c * 512:(c + 1) * 512], t1s[96:128, :], t2s[96:128, :], op=ALU.add))
                    drain("vector")
                    prev[("q_ps" if which == "q" else "k_ps", ci)] = vlast
                    if which == "q":
                        ropeq[c] = vlast
                    else:
                        ropek[c] = vlast
                wait("scalar", b.ST, vmm)
                vse = op("scalar", b.SS, lambda e, v_ps=v_ps, c=c: e.copy(
                    vtg[:, c * 512:(c + 1) * 512], v_ps))
                prev[("v_ps", ci)] = vse
                vsec[c] = vse

                def v_transpose(vc):
                    si = vc % 2
                    drain("scalar")
                    if ("vstg", si) in prev:
                        wait("scalar", b.SV, prev[("vstg", si)])
                    vtp = b.dmat(vstg[:, si * 512:(si + 1) * 512].rearrange("p (a n) -> p a n", n=128),
                                 vtg[:, vc * 512:(vc + 1) * 512])
                    wait("vector", b.SX, vtp)
                    vcp = None
                    for hh in range(2):
                        vcp = op("vector", b.SV, lambda e, si=si, vc=vc, hh=hh: e.tensor_copy(
                            vg[:, :].rearrange("p (t h n) -> p t h n", h=2, n=65)[:, 4 * vc:4 * vc + 4, hh, 0:64],
                            vstg[:, si * 512:(si + 1) * 512].rearrange("p (a h n) -> p a h n", h=2, n=64)[:, :, hh, :]))
                    drain("vector")
                    prev[("vstg", si)] = vcp
                    vcpc[vc] = vcp

                if c > 0:
                    v_transpose(c - 1)
                if c == NCH - 1:
                    v_transpose(c)
            prev[("sbank", 0)] = (b.SV, ropeq[6])
            prev[("sbank", 1)] = (b.SV, ropeq[7])
            prev[("sbank", 2)] = (b.SV, ropek[6])
            prev[("ctxv", 0, 0)] = (b.SS, vsec[6])
            prev[("ctxv", 0, 1)] = (b.SS, vsec[7])
            sstep = prev.get("sstep", 0)
            pend = None

            def waitv(eng, entry):
                if isinstance(entry, tuple):
                    wait(eng, entry[0], entry[1])
                else:
                    wait(eng, b.SS, entry)

            def plan_evac_part2(pd):
                vctx = None
                for hh in range(2):
                    wait("tensor", b.SV, pd["vhl"][hh])
                    if "rb" in prev:
                        wait("tensor", b.SS, prev["rb"])
                    emit("tensor", lambda e, hh=hh: e.matmul(
                        b.p3[0:64, :], b.onesb[64:65, 0:64], rlhl[64:65, hh * 1024:hh * 1024 + 512],
                        start=True, stop=False))
                    vbc = op("tensor", b.ST, lambda e, hh=hh: e.matmul(
                        b.p3[0:64, :], b.onesb[64:65, 0:64], rlhl[64:65, hh * 1024 + 512:hh * 1024 + 1024],
                        start=False, stop=True))
                    wait("scalar", b.ST, vbc)
                    vrb = op("scalar", b.SS, lambda e: e.copy(rbs[:, :], b.p3[0:64, :]))
                    prev["rb"] = vrb
                    wait("vector", b.SS, vrb)
                    if ("cst", pd["par"]) in prev:
                        wait("vector", b.SO, prev[("cst", pd["par"])])
                    vctx = op("vector", b.SV, lambda e, hh=hh, pd=pd: e.tensor_tensor(
                        cst[hh * 64:(hh + 1) * 64, pd["par"] * 512:pd["par"] * 512 + 512],
                        pd["ctxb"][hh][0:64, :], rbs[:, :], op=ALU.mult))
                    drain("vector")
                    prev[("ctxv", hh, pd["par"])] = vctx
                wait("sync", b.SV, vctx)
                vsp = b.dma("sync", b.SO,
                            b.ctxT[g * 128:(g + 1) * 128, pd["qc"] * 512:(pd["qc"] + 1) * 512],
                            cst[:, pd["par"] * 512:pd["par"] * 512 + 512])
                prev[("cst", pd["par"])] = vsp

            for qc in range(NCH):
                nkt = 4 * qc + 4
                par = qc % 2
                ctxb = {0: b.pA[:, par * 512:(par + 1) * 512],
                        1: b.pB[:, par * 512:(par + 1) * 512]}
                vexp_q = [[None] * nkt for _ in range(2)]
                vpv_last = [None, None]
                for kt in range(nkt + 1):
                    if kt < nkt:
                        for hh in range(2):
                            j = sstep % 3
                            s_ps = (b.p0, b.p1, b.p2)[j]
                            if ("sbank", j) in prev:
                                waitv("tensor", prev[("sbank", j)])
                            wait("tensor", b.SV, ropeq[qc])
                            wait("tensor", b.SV, ropek[kt // 4])
                            vs_mm = op("tensor", b.ST, lambda e, hh=hh, kt=kt, qc=qc, s_ps=s_ps: e.matmul(
                                s_ps[:, :], kg[hh * 64:(hh + 1) * 64, kt * 128:(kt + 1) * 128],
                                qg[hh * 64:(hh + 1) * 64, qc * 512:(qc + 1) * 512],
                                start=True, stop=True, tile_position=(hh * 64, 0)))
                            a = kt - 4 * qc
                            pslot = sstep % 4
                            pbs = pb[:, pslot * 512:(pslot + 1) * 512]
                            if a >= 0:
                                wait("vector", b.ST, vs_mm)
                                vmask = op("vector", b.SV, lambda e, s_ps=s_ps, a=a: e.tensor_tensor(
                                    s_ps[:, :], s_ps[:, :], masks[:, a * 512:(a + 1) * 512], op=ALU.add))
                                wait("scalar", b.SV, vmask)
                            else:
                                wait("scalar", b.ST, vs_mm)
                            if ("pb", pslot) in prev:
                                wait("scalar", b.ST, prev[("pb", pslot)])
                            vexp = op("scalar", b.SS, lambda e, s_ps=s_ps, pbs=pbs: e.activation(
                                pbs, s_ps[:, :], AF.Exp))
                            prev[("sbank", j)] = vexp
                            vexp_q[hh][kt] = (vexp, pslot)
                            sstep += 1
                    if kt == 1 and pend is not None:
                        plan_evac_part2(pend)
                        pend = None
                    if kt >= 1:
                        ktp = kt - 1
                        for hh in range(2):
                            vexp, pslot = vexp_q[hh][ktp]
                            wait("tensor", b.SS, vexp)
                            wait("tensor", b.SV, vcpc[ktp // 4])
                            if ktp == 0 and ("ctxv", hh, par) in prev:
                                ent = prev[("ctxv", hh, par)]
                                if isinstance(ent, tuple):
                                    wait("tensor", ent[0], ent[1])
                                else:
                                    wait("tensor", b.SV, ent)
                            vpv = op("tensor", b.ST, lambda e, ktp=ktp, hh=hh, nkt=nkt, ctxb=ctxb, pslot=pslot: e.matmul(
                                ctxb[hh][0:65, :], vg[:, ktp * 130 + hh * 65:ktp * 130 + hh * 65 + 65],
                                pb[:, pslot * 512:(pslot + 1) * 512],
                                start=(ktp == 0), stop=(ktp == nkt - 1)))
                            prev[("pb", pslot)] = vpv
                            if ktp == nkt - 1:
                                vpv_last[hh] = vpv
                vhls = [None, None]
                for hh in range(2):
                    wait("scalar", b.ST, vpv_last[hh])
                    emit("scalar", lambda e, hh=hh, ctxb=ctxb: e.activation(
                        rls[64:65, hh * 1024:hh * 1024 + 512], ctxb[hh][64:65, :], AF.Ln))
                    drain("scalar")
                    vrl = op("scalar", b.SS, lambda e, hh=hh: e.activation(
                        rls[64:65, hh * 1024 + 512:hh * 1024 + 1024],
                        rls[64:65, hh * 1024:hh * 1024 + 512], AF.Exp, scale=-1.0))
                    wait("vector", b.SS, vrl)
                    emit("vector", lambda e, hh=hh: e.tensor_copy(
                        rlhl[64:65, hh * 1024:hh * 1024 + 512],
                        rls[64:65, hh * 1024 + 512:hh * 1024 + 1024]))
                    drain("vector")
                    vhl = op("vector", b.SV, lambda e, hh=hh: e.tensor_tensor(
                        rlhl[64:65, hh * 1024 + 512:hh * 1024 + 1024],
                        rls[64:65, hh * 1024 + 512:hh * 1024 + 1024],
                        rlhl[64:65, hh * 1024:hh * 1024 + 512], op=ALU.subtract))
                    drain("vector")
                    vhls[hh] = vhl
                pend = dict(qc=qc, par=par, ctxb=ctxb, vhl=vhls)
            plan_evac_part2(pend)
            pend = None
            prev["sstep"] = sstep
            prev["attn_done"] = b.ST.n
        self.end_phase()

    def p2_wo_norm2(self):
        b = self
        self.begin_phase()
        wo_sb = b.sb("p2_wo", (128, 8 * 1024), BF16)
        rw_sb = b.sb("p2_rw", (128, 8 * EA), BF16)
        ctx_s = [b.sb("p2_c0", (128, 8 * 512), BF16), b.sb("p2_c1", (128, 8 * 512), BF16)]
        xn_s = [b.sb("p2_x0", (128, 8 * 512), BF16), b.sb("p2_x1", (128, 8 * 512), BF16)]
        xa_scr = [b.sb("p2_xa0", (128, 8 * 512), BF16), b.sb("p2_xa1", (128, 8 * 512), BF16)]
        h2_scr = [b.sb("p2_h20", (128, 8 * 512), BF16), b.sb("p2_h21", (128, 8 * 512), BF16)]
        sqt = [b.sb("p2_sq0", (128, 512), BF16), b.sb("p2_sq1", (128, 512), BF16)]
        r2row = b.sb("p2_r2", (1, T), F32)
        r2hl = b.sb("p2_r2hl", (1, 1024), BF16)
        tstg = b.sb("p2_ts", (128, 2 * 512), BF16)
        lgt = b.sb("p2_lg", (128, 4 * EA), F32)
        gts = b.sb("p2_gt", (128, 4 * EA), F32)
        mk1 = b.sb("p2_mk1", (128, 4 * EA), F32)
        mk2 = b.sb("p2_mk2", (128, 4 * EA), F32)
        mskd = b.sb("p2_msk", (128, 4 * EA), F32)
        c = b.cols

        dma, emit, op, wait, drain = b.dma, b.emit, b.op, b.wait, b.drain
        for h in range(2):
            dma("gpsimd", b.SW, wo_sb[:, h * 4096:(h + 1) * 4096].rearrange("p (a n) -> p a n", n=1024),
                b.woT.rearrange("(a p) n -> p a n", p=128)[:, 4 * h:4 * h + 4, :])
        vw = dma("gpsimd", b.SW, rw_sb[:, :].rearrange("p (a n) -> p a n", n=EA),
                 b.router_wT.rearrange("(a p) n -> p a n", p=128))
        prev = {}

        def plan_passB(cc):
            ci = cc % 2
            vr2 = prev[("r2", cc)]
            wait("vector", b.SS, vr2)
            emit("vector", lambda e, cc=cc: e.tensor_copy(
                r2hl[0:1, 0:512], r2row[0:1, cc * 512:(cc + 1) * 512]))
            drain("vector")
            vhl = op("vector", b.SV, lambda e, cc=cc: e.tensor_tensor(
                r2hl[0:1, 512:1024], r2row[0:1, cc * 512:(cc + 1) * 512], r2hl[0:1, 0:512], op=ALU.subtract))
            drain("vector")
            wait("tensor", b.SV, vhl)
            if "r2b" in prev:
                wait("tensor", b.SV, prev["r2b"])
            emit("tensor", lambda e: e.matmul(
                b.p3[:, :], b.onesb[0:1, 0:128], r2hl[0:1, 0:512], start=True, stop=False))
            vbc = op("tensor", b.ST, lambda e: e.matmul(
                b.p3[:, :], b.onesb[0:1, 0:128], r2hl[0:1, 512:1024], start=False, stop=True))
            wait("vector", b.ST, vbc)
            vh2_last = None
            vtp = None
            for et in range(8):
                vh2 = op("vector", b.SV, lambda e, et=et, ci=ci: e.scalar_tensor_tensor(
                    h2_scr[ci][:, et * 512:(et + 1) * 512], xa_scr[ci][:, et * 512:(et + 1) * 512],
                    b.cols[:, b.C_NW2 + et:b.C_NW2 + et + 1], b.p3[:, :],
                    op0=ALU.mult, op1=ALU.mult))
                drain("vector")
                vh2_last = vh2
                wait("gpsimd", b.SV, vh2)
                dma("gpsimd", b.SW, b.h2nT[et * 128:(et + 1) * 128, cc * 512:(cc + 1) * 512],
                    h2_scr[ci][:, et * 512:(et + 1) * 512])
                si = et % 2
                if ("tstg", si) in prev:
                    wait("scalar", b.SO, prev[("tstg", si)])
                vtp = b.dmat(tstg[:, si * 512:(si + 1) * 512].rearrange("p (a n) -> p a n", n=128),
                             xa_scr[ci][:, et * 512:(et + 1) * 512])
                wait("sync", b.SX, vtp)
                xa_ap = bass.AP(b.xaT.tensor, (cc * 512) * D + et * 128,
                                [[D, 128], [128 * D, 4], [1, 128]])
                vso = b.dma("sync", b.SO, xa_ap,
                            tstg[:, si * 512:(si + 1) * 512].rearrange("p (a n) -> p a n", n=128))
                prev[("tstg", si)] = vso
            prev[("xa_done", ci)] = (vh2_last, vtp)
            prev["r2b"] = vh2_last
            wait("tensor", b.SV, vh2_last)
            M1, M2, NR, IV = 20, 24, 28, 32
            vlgs = []
            for tt in range(4):
                rps = b.pB[:, 0:EA] if tt % 2 == 0 else b.pB[:, 512:512 + EA]
                if ("rtA", tt % 2) in prev:
                    wait("tensor", b.SV, prev[("rtA", tt % 2)])
                vrt = None
                for dt in range(8):
                    vrt = op("tensor", b.ST, lambda e, dt=dt, tt=tt, ci=ci, rps=rps: e.matmul(
                        rps, h2_scr[ci][:, dt * 512 + tt * 128:dt * 512 + tt * 128 + 128],
                        rw_sb[:, dt * EA:(dt + 1) * EA], start=(dt == 0), stop=(dt == 7)))
                wait("vector", b.ST, vrt)
                if "sigA" in prev:
                    wait("vector", b.SS, prev["sigA"])
                vlg = op("vector", b.SV, lambda e, tt=tt, rps=rps: e.tensor_tensor(
                    lgt[:, tt * EA:(tt + 1) * EA], rps, b.ebias_r[:, :], op=ALU.add))
                prev[("rtA", tt % 2)] = vlg
                vlgs.append(vlg)
            drain("vector")
            wait("scalar", b.SV, vlgs[-1])
            vsg = op("scalar", b.SS, lambda e: e.activation(gts[:, :], lgt[:, :], AF.Sigmoid))
            prev["sigA"] = vsg
            wait("vector", b.SS, vsg)
            for tt in range(4):
                emit("vector", lambda e, tt=tt: e.tensor_reduce(
                    c[:, M1 + tt:M1 + tt + 1], gts[:, tt * EA:(tt + 1) * EA], AX.X, ALU.max))
            drain("vector")
            for tt in range(4):
                emit("vector", lambda e, tt=tt: e.tensor_scalar(
                    mk1[:, tt * EA:(tt + 1) * EA], gts[:, tt * EA:(tt + 1) * EA],
                    c[:, M1 + tt:M1 + tt + 1], None, op0=ALU.is_equal))
            drain("vector")
            for tt in range(4):
                emit("vector", lambda e, tt=tt: e.scalar_tensor_tensor(
                    mskd[:, tt * EA:(tt + 1) * EA], mk1[:, tt * EA:(tt + 1) * EA], -1e30,
                    gts[:, tt * EA:(tt + 1) * EA], op0=ALU.mult, op1=ALU.add))
            drain("vector")
            for tt in range(4):
                emit("vector", lambda e, tt=tt: e.tensor_reduce(
                    c[:, M2 + tt:M2 + tt + 1], mskd[:, tt * EA:(tt + 1) * EA], AX.X, ALU.max))
            drain("vector")
            for tt in range(4):
                emit("vector", lambda e, tt=tt: e.tensor_scalar(
                    mk2[:, tt * EA:(tt + 1) * EA], mskd[:, tt * EA:(tt + 1) * EA],
                    c[:, M2 + tt:M2 + tt + 1], None, op0=ALU.is_equal))
            emit("vector", lambda e: e.tensor_tensor(c[:, NR:NR + 4], c[:, M1:M1 + 4], c[:, M2:M2 + 4], op=ALU.add))
            drain("vector")
            emit("vector", lambda e: e.tensor_tensor(mk1[:, :], mk1[:, :], mk2[:, :], op=ALU.add))
            emit("vector", lambda e: e.tensor_scalar(c[:, NR:NR + 4], c[:, NR:NR + 4], 1e-9, None, op0=ALU.add))
            drain("vector")
            emit("vector", lambda e: e.tensor_tensor(mk1[:, :], mk1[:, :], gts[:, :], op=ALU.mult))
            emit("vector", lambda e: e.reciprocal(c[:, IV:IV + 4], c[:, NR:NR + 4]))
            drain("vector")
            for tt in range(4):
                ttg = cc * 4 + tt
                op("vector", b.SV, lambda e, tt=tt, ttg=ttg: e.tensor_scalar(
                    b.cwa[:, ttg * EA:(ttg + 1) * EA], mk1[:, tt * EA:(tt + 1) * EA],
                    c[:, IV + tt:IV + tt + 1], None, op0=ALU.mult))
            drain("vector")
            prev[("routers_done", ci)] = b.ST.n

        for cc in range(NCH):
            ci = cc % 2
            if ("ld", ci) in prev:
                wait("sync", b.ST, prev[("ld", ci)][0])
                wait("sync", b.SV, prev[("ld", ci)][1])
            b.chunk_load(ctx_s[ci], b.ctxT, cc)
            vld = b.chunk_load(xn_s[ci], b.xnT, cc)
            wait("tensor", b.SL, vld)
            wait("tensor", b.SW, vw)
            if ("xa_done", ci) in prev:
                wait("vector", b.ST, prev[("routers_done", ci)])
                wait("vector", b.SX, prev[("xa_done", ci)][1])
                wait("vector", b.SW, b.SW.n)
            vss = None
            vxa_last = None
            sq_pend = None
            for et in range(8):
                d_ps = (b.p0, b.p1)[et % 2]
                if ("d_ps", et % 2) in prev:
                    wait("tensor", b.SV, prev[("d_ps", et % 2)])
                vdm = None
                for dt in range(8):
                    vdm = op("tensor", b.ST, lambda e, dt=dt, et=et, d_ps=d_ps, ci=ci: e.matmul(
                        d_ps[:, :], wo_sb[:, dt * 1024 + et * 128:dt * 1024 + (et + 1) * 128],
                        ctx_s[ci][:, dt * 512:(dt + 1) * 512], start=(dt == 0), stop=(dt == 7)))
                if sq_pend is not None:
                    pet, pv = sq_pend
                    wait("tensor", b.SV, pv)
                    if pet == 0 and ("r2", cc - 1) in prev:
                        wait("tensor", b.SS, prev[("r2", cc - 1)])
                    vss = op("tensor", b.ST, lambda e, pet=pet: e.matmul(
                        b.p2[0:1, :], b.onesb[:, 0:1], sqt[pet % 2][:, :], start=(pet == 0), stop=False))
                    prev[("ssq", pet % 2)] = vss
                wait("vector", b.ST, vdm)
                vxa = op("vector", b.SV, lambda e, et=et, d_ps=d_ps, ci=ci: e.tensor_tensor(
                    xa_scr[ci][:, et * 512:(et + 1) * 512], d_ps[:, :],
                    xn_s[ci][:, et * 512:(et + 1) * 512], op=ALU.add))
                prev[("d_ps", et % 2)] = vxa
                vxa_last = vxa
                drain("vector")
                if ("ssq", et % 2) in prev:
                    wait("vector", b.ST, prev[("ssq", et % 2)])
                vsq = op("vector", b.SV, lambda e, et=et, ci=ci: e.tensor_tensor(
                    sqt[et % 2][:, :], xa_scr[ci][:, et * 512:(et + 1) * 512],
                    xa_scr[ci][:, et * 512:(et + 1) * 512], op=ALU.mult))
                drain("vector")
                sq_pend = (et, vsq)
            pet, pv = sq_pend
            wait("tensor", b.SV, pv)
            vss = op("tensor", b.ST, lambda e, pet=pet: e.matmul(
                b.p2[0:1, :], b.onesb[:, 0:1], sqt[pet % 2][:, :], start=False, stop=True))
            prev[("ssq", pet % 2)] = vss
            wait("scalar", b.ST, vss)
            emit("scalar", lambda e, cc=cc: e.activation(
                r2row[0:1, cc * 512:(cc + 1) * 512], b.p2[0:1, :], AF.Ln,
                bias=c[0:1, b.C_EPS6:b.C_EPS6 + 1], scale=1.0 / D))
            drain("scalar")
            vr2 = op("scalar", b.SS, lambda e, cc=cc: e.activation(
                r2row[0:1, cc * 512:(cc + 1) * 512], r2row[0:1, cc * 512:(cc + 1) * 512], AF.Exp, scale=-0.5))
            prev[("r2", cc)] = vr2
            prev[("ld", ci)] = (b.ST.n, vxa_last)
            if cc > 0:
                plan_passB(cc - 1)
        plan_passB(NCH - 1)
        self.end_phase()

    def _moe(self, tag, n_exp, inner, stream_dram, wu1, wu3, wdn, su1, su3, sdn,
             acc, cwbuf, cw_w, overwrite_first):
        b = self
        nit = inner // 128
        wA, wB, wC = self.wsA, self.wsB, self.wsC
        xs, hb, gsc = self.ms_xs, self.ms_hb, self.ms_gs
        dma, emit, op, wait, drain = b.dma, b.emit, b.op, b.wait, b.drain
        prev = self.ms_prev

        def load_w(which, e_idx):
            if e_idx == -1:
                up1, up3, dn = su1, su3, sdn
            else:
                up1, up3, dn = wu1[e_idx], wu3[e_idx], wdn[e_idx]
            v = None
            if which == "A" or which == "B":
                W = up1 if which == "A" else up3
                slot = wA if which == "A" else wB
                for h in range(2):
                    v = dma("gpsimd", b.SW,
                            slot[:, h * 4 * inner:(h + 1) * 4 * inner].rearrange("p (a n) -> p a n", n=inner),
                            W.rearrange("(a p) n -> p a n", p=128)[:, 4 * h:4 * h + 4, :])
            else:
                nh = max(1, nit // 4)
                step = nit // nh
                for h in range(nh):
                    v = dma("gpsimd", b.SW,
                            wC[:, h * step * 1024:(h + 1) * step * 1024].rearrange("p (a n) -> p a n", n=1024),
                            dn.rearrange("(a p) n -> p a n", p=128)[:, step * h:step * (h + 1), :])
            return v

        exps = [-1] + list(range(n_exp))
        lim = _P4_NEXP if tag == "p4" else _P3_NEXP
        if lim is not None:
            exps = exps[:lim]
        wait("tensor", b.SV, b.SV.n)
        wait("tensor", b.SS, b.SS.n)
        vA = load_w("A", exps[0]); vB = load_w("B", exps[0]); vC = load_w("C", exps[0])
        for xi, eidx in enumerate(exps):
            for tc in range(NCH):
                ci = tc % 2
                if (tag, "xs", ci) in prev:
                    wait("sync", b.ST, prev[(tag, "xs", ci)])
                vxs = b.chunk_load(xs[ci], stream_dram, tc)
                wait("tensor", b.SL, vxs)
                if tc == 0:
                    wait("tensor", b.SW, vA)
                    wait("tensor", b.SW, vB)
                if (tag, "down_st") in prev:
                    wait("vector", b.ST, prev[(tag, "down_st")])
                for it in range(nit):
                    g_ps = (b.p0, b.p1)[it % 2]
                    u_ps = (b.p2, b.p3)[it % 2]
                    if (tag, "g_ps", it % 2) in prev:
                        wait("tensor", b.SS, prev[(tag, "g_ps", it % 2)])
                    vgm = None
                    for dt in range(8):
                        vgm = op("tensor", b.ST, lambda e, dt=dt, it=it, g_ps=g_ps, ci=ci: e.matmul(
                            g_ps[:, :], wA[:, dt * inner + it * 128:dt * inner + (it + 1) * 128],
                            xs[ci][:, dt * 512:(dt + 1) * 512], start=(dt == 0), stop=(dt == 7)))
                    if (tag, "u_ps", it % 2) in prev:
                        wait("tensor", b.SV, prev[(tag, "u_ps", it % 2)])
                    vum = None
                    for dt in range(8):
                        vum = op("tensor", b.ST, lambda e, dt=dt, it=it, u_ps=u_ps, ci=ci: e.matmul(
                            u_ps[:, :], wB[:, dt * inner + it * 128:dt * inner + (it + 1) * 128],
                            xs[ci][:, dt * 512:(dt + 1) * 512], start=(dt == 0), stop=(dt == 7)))
                    wait("scalar", b.ST, vgm)
                    gs = gsc[it % 2]
                    if (tag, "gs", it % 2) in prev:
                        wait("scalar", b.SV, prev[(tag, "gs", it % 2)])
                    vsi = op("scalar", b.SS, lambda e, g_ps=g_ps, gs=gs: e.activation(
                        gs[:, :], g_ps[:, :], AF.Silu))
                    prev[(tag, "g_ps", it % 2)] = vsi
                    wait("vector", b.ST, vum)
                    wait("vector", b.SS, vsi)
                    vh = op("vector", b.SV, lambda e, it=it, u_ps=u_ps, gs=gs: e.tensor_tensor(
                        hb[:, it * 512:(it + 1) * 512], gs[:, :], u_ps[:, :], op=ALU.mult))
                    drain("vector")
                    prev[(tag, "u_ps", it % 2)] = vh
                    prev[(tag, "gs", it % 2)] = vh
                prev[(tag, "xs", ci)] = b.ST.n
                if self.dev and tag == "p4" and xi == 0 and tc == 0:
                    vtap = b.SV.n
                    b.wait("sync", b.SV, vtap)
                    b.dma("sync", b.SO, self.hb_d, hb[:, 0:16 * 512])
                    b.dma("sync", b.SO, self.wa_d, wA[:, 0:8 * inner])
                    b.dma("sync", b.SO, self.xs_d, xs[ci][:, :])
                    b.dma("sync", b.SO, self.gs_d, gsc[0][:, :])
                if tc == NCH - 1 and xi + 1 < len(exps):
                    wait("gpsimd", b.ST, b.ST.n)
                    vA = load_w("A", exps[xi + 1])
                    vB = load_w("B", exps[xi + 1])
                if tc == 0:
                    wait("tensor", b.SW, vC)
                wait("tensor", b.SV, b.SV.n)
                for tt in range(4):
                    ttg = tc * 4 + tt
                    d_ps = (b.pA, b.pB)[tt % 2]
                    if (tag, "d_ps", tt % 2) in prev:
                        wait("tensor", b.SV, prev[(tag, "d_ps", tt % 2)])
                    vdm = None
                    for dch in range(2):
                        for it in range(nit):
                            vdm = op("tensor", b.ST, lambda e, tt=tt, dch=dch, it=it, d_ps=d_ps: e.matmul(
                                d_ps[:, dch * 512:(dch + 1) * 512],
                                hb[:, it * 512 + tt * 128:it * 512 + tt * 128 + 128],
                                wC[:, it * 1024 + dch * 512:it * 1024 + dch * 512 + 512],
                                start=(it == 0), stop=(it == nit - 1)))
                    wait("vector", b.ST, vdm)
                    accs = acc[:, ttg * D:(ttg + 1) * D]
                    if overwrite_first and xi == 0:
                        vac = op("vector", b.SV, lambda e, d_ps=d_ps, accs=accs: e.tensor_scalar(
                            accs, d_ps[:, :], 1.0, None, op0=ALU.mult))
                    elif eidx == -1:
                        vac = op("vector", b.SV, lambda e, d_ps=d_ps, accs=accs: e.scalar_tensor_tensor(
                            accs, d_ps[:, :], 1.0, accs, op0=ALU.mult, op1=ALU.add))
                    else:
                        cwcol = cwbuf[:, ttg * cw_w + eidx:ttg * cw_w + eidx + 1]
                        vac = op("vector", b.SV, lambda e, d_ps=d_ps, accs=accs, cwcol=cwcol: e.scalar_tensor_tensor(
                            accs, d_ps[:, :], cwcol, accs, op0=ALU.mult, op1=ALU.add))
                    drain("vector")
                    prev[(tag, "d_ps", tt % 2)] = vac
                prev[(tag, "down_st")] = b.ST.n
            if xi + 1 < len(exps):
                wait("gpsimd", b.ST, b.ST.n)
                vC = load_w("C", exps[xi + 1])

    def p4_ffn(self):
        b = self
        self.begin_phase()
        self.yz = b.sb_persist("yz", (128, NT * D), BF16)
        self.wsA = b.sb("p4_wA", (128, 8 * IF), BF16)
        self.wsB = b.sb("p4_wB", (128, 8 * IF), BF16)
        self.wsC = b.sb("p4_wC", (128, 16 * D), BF16)
        self.ms_xs = [b.sb("p4_xs0", (128, 8 * 512), BF16), b.sb("p4_xs1", (128, 8 * 512), BF16)]
        self.ms_hb = b.sb("p4_hb", (128, 16 * 512), BF16)
        self.ms_gs = [b.sb("p4_g0", (128, 512), BF16), b.sb("p4_g1", (128, 512), BF16)]
        self.ms_prev = {}
        gw_sb = b.sb("p4_gw", (128, 8 * EF), BF16)
        exs = b.sb("p4_ex", (128, EF), F32)
        sco = b.sb("p4_sc", (128, EF), F32)
        bsd = b.sb("p4_bs", (128, EF), F32)
        mk1 = b.sb("p4_m1", (128, EF), F32)
        mk2 = b.sb("p4_m2", (128, EF), F32)
        mskd = b.sb("p4_mk", (128, EF), F32)
        c = b.cols
        mxc = c[:, 50:51]; nmx = c[:, 51:52]; smc = c[:, 52:53]; rsc = c[:, 53:54]
        m1c = c[:, 54:55]; m2c = c[:, 55:56]

        dma, emit, op, wait, drain = b.dma, b.emit, b.op, b.wait, b.drain
        vgw = dma("gpsimd", b.SW, gw_sb[:, :].rearrange("p (a n) -> p a n", n=EF),
                  b.gate_wT.rearrange("(a p) n -> p a n", p=128))
        xs = self.ms_xs
        prevR = {}
        for cc in range(NCH):
            ci = cc % 2
            if ("xs", ci) in prevR:
                wait("sync", b.ST, prevR[("xs", ci)])
            vxs = b.chunk_load(xs[ci], b.xnT, cc)
            wait("tensor", b.SL, vxs)
            wait("tensor", b.SW, vgw)
            for tt in range(4):
                ttg = cc * 4 + tt
                if "pB" in prevR:
                    wait("tensor", b.SV, prevR["pBv"])
                    wait("tensor", b.SS, prevR["pBs"])
                vrt = None
                for dt in range(8):
                    vrt = op("tensor", b.ST, lambda e, dt=dt, tt=tt, ci=ci: e.matmul(
                        b.pB[:, 0:EF], xs[ci][:, dt * 512 + tt * 128:dt * 512 + tt * 128 + 128],
                        gw_sb[:, dt * EF:(dt + 1) * EF], start=(dt == 0), stop=(dt == 7)))
                wait("vector", b.ST, vrt)
                emit("vector", lambda e: e.tensor_reduce(mxc, b.pB[:, 0:EF], AX.X, ALU.max))
                drain("vector")
                vnm = op("vector", b.SV, lambda e: e.tensor_scalar(nmx, mxc, -1.0, None, op0=ALU.mult))
                prevR["pB"] = True
                prevR["pBv"] = vnm
                wait("scalar", b.SV, vnm)
                vex = op("scalar", b.SS, lambda e: e.activation(
                    exs[:, :], b.pB[:, 0:EF], AF.Exp, bias=nmx, scale=1.0))
                prevR["pBs"] = vex
                wait("vector", b.SS, vex)
                emit("vector", lambda e: e.tensor_reduce(smc, exs[:, :], AX.X, ALU.add))
                drain("vector")
                emit("vector", lambda e: e.reciprocal(rsc, smc))
                drain("vector")
                emit("vector", lambda e: e.tensor_scalar(sco[:, :], exs[:, :], rsc, None, op0=ALU.mult))
                drain("vector")
                emit("vector", lambda e: e.tensor_tensor(bsd[:, :], sco[:, :], b.gbias_r[:, :], op=ALU.add))
                drain("vector")
                emit("vector", lambda e: e.tensor_reduce(m1c, bsd[:, :], AX.X, ALU.max))
                drain("vector")
                emit("vector", lambda e: e.tensor_scalar(mk1[:, :], bsd[:, :], m1c, None, op0=ALU.is_equal))
                drain("vector")
                emit("vector", lambda e: e.scalar_tensor_tensor(
                    mskd[:, :], mk1[:, :], -1e30, bsd[:, :], op0=ALU.mult, op1=ALU.add))
                drain("vector")
                emit("vector", lambda e: e.tensor_reduce(m2c, mskd[:, :], AX.X, ALU.max))
                drain("vector")
                emit("vector", lambda e: e.tensor_scalar(mk2[:, :], mskd[:, :], m2c, None, op0=ALU.is_equal))
                drain("vector")
                emit("vector", lambda e: e.tensor_tensor(mk1[:, :], mk1[:, :], mk2[:, :], op=ALU.add))
                drain("vector")
                op("vector", b.SV, lambda e, ttg=ttg: e.tensor_tensor(
                    b.cwf[:, ttg * EF:(ttg + 1) * EF], mk1[:, :], sco[:, :], op=ALU.mult))
                drain("vector")
            prevR[("xs", ci)] = b.ST.n
        self.ms_prev[("p4", "xs", 0)] = b.ST.n
        self.ms_prev[("p4", "xs", 1)] = b.ST.n
        self._moe("p4", EF, IF, b.xnT,
                  [self.w1T[e] for e in range(EF)], [self.w3T[e] for e in range(EF)],
                  [self.w2T[e] for e in range(EF)],
                  self.sw1T, self.sw3T, self.sw2T, self.yz, b.cwf, EF, True)
        if self.dev:
            b.wait("sync", b.SV, b.SV.n)
            b.dma("sync", b.SO, self.yz_d, self.yz[:, :])
        self.end_phase()

    def p3_amoe(self):
        b = self
        self.begin_phase()
        self.sr = b.sb_persist("sr", (128, NT * D), BF16)
        self.wsA = b.sb("p3_wA", (128, 8 * IA), BF16)
        self.wsB = b.sb("p3_wB", (128, 8 * IA), BF16)
        self.wsC = b.sb("p3_wC", (128, 8 * D), BF16)
        self.ms_xs = [b.sb("p3_xs0", (128, 8 * 512), BF16), b.sb("p3_xs1", (128, 8 * 512), BF16)]
        self.ms_hb = b.sb("p3_hb", (128, 8 * 512), BF16)
        self.ms_gs = [b.sb("p3_g0", (128, 512), BF16), b.sb("p3_g1", (128, 512), BF16)]
        self.ms_prev = {}
        vin = None
        for q in range(8):
            vin = b.dma("sync", b.SL, self.sr[:, q * 4 * D:(q + 1) * 4 * D].rearrange("p (t n) -> p t n", n=D),
                        b.xaT.rearrange("(t p) n -> p t n", p=128)[:, 4 * q:4 * q + 4, :])
        b.wait("vector", b.SL, vin)
        self._moe("p3", EA, IA, b.h2nT,
                  [self.egT[e] for e in range(EA)], [self.euT[e] for e in range(EA)],
                  [self.edT[e] for e in range(EA)],
                  self.sgT, self.suT, self.sdT, self.sr, b.cwa, EA, False)
        if self.dev:
            b.wait("sync", b.SV, b.SV.n)
            b.dma("sync", b.SO, self.sr_d, self.sr[:, :])
        self.end_phase()

    def p5_combine(self):
        b = self
        self.begin_phase()
        xin = [b.sb("p5_x0", (128, D), F32), b.sb("p5_x1", (128, D), F32)]
        gsc = [b.sb("p5_g0", (128, D), F32), b.sb("p5_g1", (128, D), F32)]
        tsc = [b.sb("p5_t0", (128, D), F32), b.sb("p5_t1", (128, D), F32)]
        osb = [b.sb("p5_o0", (128, D), F32), b.sb("p5_o1", (128, D), F32)]
        dma, emit, op, wait, drain = b.dma, b.emit, b.op, b.wait, b.drain
        prev = {}
        for t in range(NT):
            pi = t % 2
            if ("vo", pi) in prev:
                wait("sync", b.SV, prev[("vo", pi)])
            vx = dma("sync", b.SL, xin[pi][:, :], b.x[t * 128:(t + 1) * 128, :])
            if ("vt", pi) in prev:
                wait("scalar", b.SV, prev[("vt", pi)])
            vg = op("scalar", b.SS, lambda e, t=t, pi=pi: e.activation(
                gsc[pi][:, :], self.yz[:, t * D:(t + 1) * D], AF.Silu))
            wait("vector", b.SS, vg)
            vt = op("vector", b.SV, lambda e, t=t, pi=pi: e.tensor_tensor(
                tsc[pi][:, :], self.sr[:, t * D:(t + 1) * D], gsc[pi][:, :], op=ALU.mult))
            drain("vector")
            prev[("vt", pi)] = vt
            wait("vector", b.SL, vx)
            if ("st", pi) in prev:
                wait("vector", b.SO, prev[("st", pi)])
            vo = op("vector", b.SV, lambda e, pi=pi: e.tensor_tensor(
                osb[pi][:, :], tsc[pi][:, :], xin[pi][:, :], op=ALU.add))
            drain("vector")
            prev[("vo", pi)] = vo
            wait("sync", b.SV, vo)
            vst = dma("sync", b.SO, b.out[t * 128:(t + 1) * 128, :], osb[pi][:, :])
            prev[("st", pi)] = vst
        b.wait("sync", b.SO, b.SO.n)
        self.end_phase()


def build_core(dev=False, phases=_PHASES):
    return Builder(dev=dev, phases=phases).build()



def _constants():
    pos = np.arange(T, dtype=np.float32)
    inv_freq = 1.0 / (10000.0 ** (np.arange(0, DH, 2, dtype=np.float32) / DH))
    freqs = np.concatenate([inv_freq, inv_freq])[:, None] * pos[None, :]
    cos_c = np.cos(freqs).astype(np.float32)
    sin_c = np.sin(freqs).astype(np.float32)
    masks = np.zeros((4, 128, 512), dtype=np.float32)
    for a in range(4):
        kk = a * 128 + np.arange(128)[:, None]
        qq = np.arange(512)[None, :]
        masks[a] = np.where(kk <= qq, 0.0, -1e9)
    masks_c = np.ascontiguousarray(masks.reshape(4 * 128, 512))
    ones = np.ones((128, 128), dtype=np.float32)
    return dict(cos_c=cos_c, sin_c=sin_c, masks_c=masks_c, ones_f32=ones,
                ident_c=np.eye(128, dtype=np.float32))


def _tr(a):
    return np.ascontiguousarray(np.swapaxes(np.asarray(a), -1, -2))


def kernel(**inp):
    consts = _constants()
    w = dict(
        wqT=_tr(inp["L0_attn_wq"]), wkT=_tr(inp["L0_attn_wk"]),
        wvT=_tr(inp["L0_attn_wv"]), woT=_tr(inp["L0_attn_wo"]),
        nw1=np.ascontiguousarray(np.asarray(inp["L0_attn_norm1_w"])[None, :]),
        nw2=np.ascontiguousarray(np.asarray(inp["L0_attn_norm2_w"])[None, :]),
        router_wT=_tr(inp["L0_attn_router_w"]),
        ebias=np.ascontiguousarray(np.asarray(inp["L0_attn_expert_bias"])[None, :]),
        sgT=_tr(inp["L0_attn_sg"]), suT=_tr(inp["L0_attn_su"]), sdT=_tr(inp["L0_attn_sd"]),
        egT=_tr(inp["L0_attn_eg"]), euT=_tr(inp["L0_attn_eu"]), edT=_tr(inp["L0_attn_ed"]),
        gate_wT=_tr(inp["L0_ffn_gate_w"]),
        gbias=np.ascontiguousarray(np.asarray(inp["L0_ffn_gate_bias"])[None, :]),
        w1T=_tr(inp["L0_ffn_w1"]), w3T=_tr(inp["L0_ffn_w3"]), w2T=_tr(inp["L0_ffn_w2"]),
        sw1T=_tr(inp["L0_ffn_sw1"]), sw3T=_tr(inp["L0_ffn_sw3"]), sw2T=_tr(inp["L0_ffn_sw2"]),
    )
    xfull = np.asarray(inp["L0_attn_x"])
    nc = build_core(dev=_DEV)
    in_maps = []
    for cc in range(8):
        m = dict(x=np.ascontiguousarray(xfull[cc]))
        m.update(w)
        m.update(consts)
        in_maps.append(m)
    res = run_bass_kernel_spmd(nc, in_maps, core_ids=list(range(8)))
    return np.stack([np.asarray(r["out"]) for r in res.results], axis=0)


# version 25 (final)
# speedup vs baseline: 1.1813x; 1.0573x over previous
import numpy as np
import ml_dtypes
from contextlib import ExitStack

import concourse.bass as bass
import concourse.mybir as mybir
from concourse.bass_utils import run_bass_kernel_spmd

F32 = mybir.dt.float32
BF16 = mybir.dt.bfloat16
AF = mybir.ActivationFunctionType
ALU = mybir.AluOpType
AX = mybir.AxisListType

T = 4096
D = 1024
NT = 32
NCH = 8
NH = 16
DH = 64
NG = 8
IA = 1024
IF = 2048
EA = 16
EF = 8

_DEV = False
_PHASES = (0, 1, 2, 4, 3, 5)
_LIM_P2 = NCH
_P4_NEXP = None
_P3_NEXP = None
_P2_PASSB = True
_P2_ROUTER = True
_P2_XASPILL = True
_P2_H2SPILL = True
_P2_BCAST = True
_LIM_P1G = NG
ACT_SET_LNEXP = 6

ENGINES = ("sync", "gpsimd", "vector", "scalar", "tensor")


class Sem:
    __slots__ = ("h", "n", "name")

    def __init__(self, h, name):
        self.h, self.n, self.name = h, 0, name


class Builder:
    def __init__(self, dev=False, phases=_PHASES):
        self.dev = dev
        self.phases = phases
        nc = self.nc = bass.Bass("TRN2", target_bir_lowering=False)
        self.es = ExitStack()
        self.ops = None

        def din(name, shape, dt=F32):
            return nc.dram_tensor(name, list(shape), dt, kind="ExternalInput").ap()

        self.x = din("x", (T, D))
        self.wqT = din("wqT", (D, D)); self.wkT = din("wkT", (D, D))
        self.wvT = din("wvT", (D, D)); self.woT = din("woT", (D, D))
        self.nw1 = din("nw1", (1, D)); self.nw2 = din("nw2", (1, D))
        self.router_wT = din("router_wT", (D, EA)); self.ebias = din("ebias", (1, EA))
        self.sgT = din("sgT", (D, IA)); self.suT = din("suT", (D, IA)); self.sdT = din("sdT", (IA, D))
        self.egT = din("egT", (EA, D, IA)); self.euT = din("euT", (EA, D, IA)); self.edT = din("edT", (EA, IA, D))
        self.gate_wT = din("gate_wT", (D, EF)); self.gbias = din("gbias", (1, EF))
        self.w1T = din("w1T", (EF, D, IF)); self.w3T = din("w3T", (EF, D, IF)); self.w2T = din("w2T", (EF, IF, D))
        self.sw1T = din("sw1T", (D, IF)); self.sw3T = din("sw3T", (D, IF)); self.sw2T = din("sw2T", (IF, D))
        self.cos_c = din("cos_c", (DH, T)); self.sin_c = din("sin_c", (DH, T))
        self.masks_c = din("masks_c", (4 * 128, 512))
        self.ones_f32 = din("ones_f32", (128, 128))
        self.ident_c = din("ident_c", (128, 128))

        self.out = nc.dram_tensor("out", [T, D], F32, kind="ExternalOutput").ap()
        sk = "ExternalOutput" if dev else "Internal"
        self.yz_d = nc.dram_tensor("yz_d", [128, NT * D], BF16, kind=sk).ap() if dev else None
        self.hb_d = nc.dram_tensor("hb_d", [128, 16 * 512], BF16, kind=sk).ap() if dev else None
        self.wa_d = nc.dram_tensor("wa_d", [128, 8 * 2048], BF16, kind=sk).ap() if dev else None
        self.xs_d = nc.dram_tensor("xs_d", [128, 8 * 512], BF16, kind=sk).ap() if dev else None
        self.gs_d = nc.dram_tensor("gs_d", [128, 512], BF16, kind=sk).ap() if dev else None
        self.sr_d = nc.dram_tensor("sr_d", [128, NT * D], BF16, kind=sk).ap() if dev else None
        self.xnT = nc.dram_tensor("xnT", [D, T], BF16, kind=sk).ap()
        self.xnrT = nc.dram_tensor("xnrT", [D, T], BF16, kind=sk).ap()
        self.ctxT = nc.dram_tensor("ctxT", [D, T], BF16, kind=sk).ap()
        self.h2nT = nc.dram_tensor("h2nT", [D, T], BF16, kind=sk).ap()
        self.xaT = nc.dram_tensor("xaT", [T, D], BF16, kind=sk).ap()

        self.sems = []
        for nm in ("SL", "SO", "SW", "SX", "ST", "SV", "SS", "SG"):
            s = Sem(self.es.enter_context(nc.semaphore(nm)), nm)
            setattr(self, nm, s)
            self.sems.append(s)

        sb = self.sb
        self.cols = sb("cols", (128, 144), F32)
        self.onesb = sb("onesb", (128, 128), BF16)
        self.onesf = sb("onesf", (128, 128), F32)
        self.identb = sb("identb", (128, 128), BF16)
        self.ebias_r = sb("ebias_r", (128, EA), F32)
        self.gbias_r = sb("gbias_r", (128, EF), F32)
        self.cwa = sb("cwa", (128, NT * EA), F32)
        self.cwf = sb("cwf", (128, NT * EF), F32)
        self.pA = self.ps("pA", (128, 1024))
        self.pB = self.ps("pB", (128, 1024))
        self.p0 = self.ps("p0", (128, 512))
        self.p1 = self.ps("p1", (128, 512))
        self.p2 = self.ps("p2", (128, 512))
        self.p3 = self.ps("p3", (128, 512))

        self.C_NW1, self.C_NW1Q, self.C_EPS6, self.C_EPS5 = 0, 8, 16, 17
        self.C_NW2 = 60

    def sb(self, name, shape, dt):
        es = self.phase_es if getattr(self, "phase_es", None) is not None else self.es
        return es.enter_context(self.nc.sbuf_tensor(name, list(shape), dt))

    def sb_persist(self, name, shape, dt):
        return self.es.enter_context(self.nc.sbuf_tensor(name, list(shape), dt))

    def ps(self, name, shape):
        return self.es.enter_context(self.nc.psum_tensor(name, list(shape), F32))

    def begin_phase(self):
        self.phase_es = ExitStack()
        self._hwm = {}
        self.ops = {e: [] for e in ENGINES}
        for eng in ENGINES:
            for s in self.sems:
                if s.n > 0:
                    self.wait(eng, s, s.n)

    def end_phase(self):
        ops = self.ops
        with self.nc.Block() as block:
            @block.sync
            def _(e):
                for fn in ops["sync"]:
                    fn(e)

            @block.gpsimd
            def _(e):
                for fn in ops["gpsimd"]:
                    fn(e)

            @block.vector
            def _(e):
                for fn in ops["vector"]:
                    fn(e)

            @block.scalar
            def _(e):
                for fn in ops["scalar"]:
                    fn(e)

            @block.tensor
            def _(e):
                for fn in ops["tensor"]:
                    fn(e)
        self.ops = None
        self.phase_es.close()
        self.phase_es = None

    def emit(self, eng, fn):
        self.ops[eng].append(fn)

    def wait(self, eng, sem, val):
        if not val or val <= 0:
            return
        key = (eng, sem.name)
        if self._hwm.get(key, -1) >= val:
            return
        self._hwm[key] = val
        self.emit(eng, lambda e, s=sem.h, v=val: e.wait_ge(s, v))

    def dma(self, eng, sem, dst, src, **kw):
        sem.n += 16
        v = sem.n
        self.emit(eng, lambda e, d=dst, s=src, sm=sem.h, kw=kw: e.dma_start(d, s, **kw).then_inc(sm, 16))
        return v

    def dmat(self, dst, src):
        self.SX.n += 16
        v = self.SX.n
        self.emit("scalar", lambda e, d=dst, s=src: e.dma_start(d, s, transpose=True).then_inc(self.SX.h, 16))
        return v

    def op(self, eng, sem, f):
        if sem is None:
            self.emit(eng, f)
            return None
        sem.n += 1
        v = sem.n
        self.emit(eng, lambda e, f=f, s=sem.h: f(e).then_inc(s, 1))
        return v

    def drain(self, eng):
        self.emit(eng, lambda e: e.drain())

    def preload(self, set_id):
        pass

    def chunk_load(self, dstbuf, src, c, half_elems=512):
        vr = None
        for h in range(2):
            vr = self.dma("sync", self.SL,
                          dstbuf[:, h * 4 * half_elems:(h + 1) * 4 * half_elems]
                          .rearrange("p (a n) -> p a n", n=half_elems),
                          src.rearrange("(a p) m -> p a m", p=128)[:, 4 * h:4 * h + 4,
                                                                  c * half_elems:(c + 1) * half_elems])
        return vr

    def build(self):
        self.startup()
        if 0 in self.phases:
            self.p0_layernorm()
        if 1 in self.phases:
            self.p1_attention()
        if 2 in self.phases:
            self.p2_wo_norm2()
        if 4 in self.phases:
            self.p4_ffn()
        if 3 in self.phases:
            self.p3_amoe()
        if 5 in self.phases:
            self.p5_combine()
        self.es.close()
        return self.nc

    def startup(self):
        self.begin_phase()
        b = self
        b.preload(ACT_SET_LNEXP)
        vof = b.dma("sync", b.SL, b.onesf[:, :], b.ones_f32)
        b.dma("sync", b.SL, b.cols[:, b.C_NW1:b.C_NW1 + 8], b.nw1.rearrange("o (a p) -> p (o a)", p=128),
              allow_slow_non_contiguous=True)
        b.dma("sync", b.SL, b.cols[:, b.C_NW2:b.C_NW2 + 8], b.nw2.rearrange("o (a p) -> p (o a)", p=128),
              allow_slow_non_contiguous=True)
        eb_row = b.sb("eb_row", (1, EA), F32)
        gb_row = b.sb("gb_row", (1, EF), F32)
        b.dma("sync", b.SL, eb_row[0:1, :], b.ebias)
        v = b.dma("sync", b.SL, gb_row[0:1, :], b.gbias)
        eb_hl = b.sb("eb_hl", (1, 2 * EA), BF16)
        gb_hl = b.sb("gb_hl", (1, 2 * EF), BF16)
        identf = b.sb("identf", (128, 128), F32)
        vid = b.dma("sync", b.SL, identf[:, :], b.ident_c)
        b.wait("vector", b.SL, v)
        b.wait("vector", b.SL, vof)
        b.wait("vector", b.SL, vid)
        b.emit("vector", lambda e: e.tensor_copy(b.onesb[:, :], b.onesf[:, :]))
        b.drain("vector")
        b.emit("vector", lambda e: e.tensor_copy(b.identb[:, :], identf[:, :]))
        b.drain("vector")
        b.emit("vector", lambda e: e.tensor_copy(eb_hl[0:1, 0:EA], eb_row[0:1, :]))
        b.emit("vector", lambda e: e.tensor_copy(gb_hl[0:1, 0:EF], gb_row[0:1, :]))
        b.drain("vector")
        b.emit("vector", lambda e: e.tensor_tensor(eb_hl[0:1, EA:2 * EA], eb_row[0:1, :], eb_hl[0:1, 0:EA], op=ALU.subtract))
        vbl = b.op("vector", b.SV, lambda e: e.tensor_tensor(gb_hl[0:1, EF:2 * EF], gb_row[0:1, :], gb_hl[0:1, 0:EF], op=ALU.subtract))
        b.drain("vector")
        b.wait("tensor", b.SV, vbl)
        b.emit("tensor", lambda e: e.matmul(b.pA[:, 0:EA], b.onesb[0:1, 0:128], eb_hl[0:1, 0:EA], start=True, stop=False))
        b.emit("tensor", lambda e: e.matmul(b.pA[:, 0:EA], b.onesb[0:1, 0:128], eb_hl[0:1, EA:2 * EA], start=False, stop=True))
        b.emit("tensor", lambda e: e.matmul(b.pB[:, 0:EF], b.onesb[0:1, 0:128], gb_hl[0:1, 0:EF], start=True, stop=False))
        vbb = b.op("tensor", b.ST, lambda e: e.matmul(b.pB[:, 0:EF], b.onesb[0:1, 0:128], gb_hl[0:1, EF:2 * EF], start=False, stop=True))
        b.wait("scalar", b.ST, vbb)
        b.emit("scalar", lambda e: e.copy(b.ebias_r[:, :], b.pA[:, 0:EA]))
        b.op("scalar", b.SS, lambda e: e.copy(b.gbias_r[:, :], b.pB[:, 0:EF]))
        b.emit("vector", lambda e: e.memset(b.cols[:, b.C_EPS6:b.C_EPS6 + 1], 1e-6))
        b.emit("vector", lambda e: e.memset(b.cols[:, b.C_EPS5:b.C_EPS5 + 1], 1e-5))
        b.drain("vector")
        b.op("vector", b.SV, lambda e: e.tensor_scalar(
            b.cols[:, b.C_NW1Q:b.C_NW1Q + 8], b.cols[:, b.C_NW1:b.C_NW1 + 8], 0.125, None, op0=ALU.mult))
        self.end_phase()

    def p0_layernorm(self):
        b = self
        self.begin_phase()
        BT = 8
        xin = [b.sb("p0_x0", (128, BT * D), F32), b.sb("p0_x1", (128, BT * D), F32)]
        sqo = b.sb("p0_sqo", (128, D), F32)
        xn_tm = [b.sb("p0_n0", (128, D), BF16), b.sb("p0_n1", (128, D), BF16)]
        xr_tm = [b.sb("p0_r0", (128, D), BF16), b.sb("p0_r1", (128, D), BF16)]
        stg = [b.sb("p0_s0", (128, 2 * D), BF16), b.sb("p0_s1", (128, 2 * D), BF16)]
        c = b.cols
        CO = {0: (20, 28, 36, 44, 52, 68, 76), 1: (84, 92, 100, 108, 116, 124, 132)}
        NB = NT // BT
        last = {}
        vxd = {}

        def load(blk):
            bi = blk % 2
            if ("blk", bi) in last:
                b.wait("sync", b.SV, last[("blk", bi)])
                b.wait("sync", b.SS, last[("sq", bi)])
            vxd[blk] = b.dma("sync", b.SL, xin[bi][:, :].rearrange("p (a n) -> p a n", n=D),
                             b.x.rearrange("(a p) n -> p a n", p=128)[:, blk * BT:blk * BT + BT, :])

        def stats(blk):
            bi = blk % 2
            CS, CQ, CM, CV, CR, CA, CX = CO[bi]
            vx = vxd[blk]
            b.wait("vector", b.SL, vx)
            b.wait("scalar", b.SL, vx)
            if ("chain", bi) in last:
                b.wait("vector", b.SS, last[("chain", bi)])
            for j in range(BT):
                b.emit("vector", lambda e, j=j, bi=bi: e.tensor_reduce(
                    c[:, CS + j:CS + j + 1], xin[bi][:, j * D:(j + 1) * D], AX.X, ALU.add))
            b.drain("vector")
            b.emit("vector", lambda e: e.tensor_scalar(c[:, CM:CM + BT], c[:, CS:CS + BT], 1.0 / D, None, op0=ALU.mult))
            b.drain("vector")
            vsq = None
            for j in range(BT):
                vsq = b.op("scalar", b.SS, lambda e, j=j, bi=bi: e.activation(
                    sqo[:, :], xin[bi][:, j * D:(j + 1) * D], AF.Square, accum_out=c[:, CQ + j:CQ + j + 1]))
            last[("sq", bi)] = vsq
            b.wait("vector", b.SS, vsq)
            b.emit("vector", lambda e: e.tensor_tensor(c[:, CA:CA + BT], c[:, CM:CM + BT], c[:, CM:CM + BT], op=ALU.mult))
            b.drain("vector")
            vvar = b.op("vector", b.SV, lambda e: e.scalar_tensor_tensor(
                c[:, CV:CV + BT], c[:, CQ:CQ + BT], 1.0 / D, c[:, CA:CA + BT], op0=ALU.mult, op1=ALU.subtract))
            b.drain("vector")
            b.wait("scalar", b.SV, vvar)
            b.emit("scalar", lambda e: e.activation(c[:, CA:CA + BT], c[:, CV:CV + BT], AF.Ln, bias=c[:, b.C_EPS5:b.C_EPS5 + 1], scale=1.0))
            b.drain("scalar")
            b.emit("scalar", lambda e: e.activation(c[:, CR:CR + BT], c[:, CA:CA + BT], AF.Exp, scale=-0.5))
            b.drain("scalar")
            vs2 = b.op("scalar", b.SS, lambda e: e.activation(c[:, CA:CA + BT], c[:, CR:CR + BT], AF.Square))
            b.wait("vector", b.SS, vs2)
            vv2 = b.op("vector", b.SV, lambda e: e.tensor_tensor(c[:, CA:CA + BT], c[:, CV:CV + BT], c[:, CA:CA + BT], op=ALU.mult))
            b.drain("vector")
            b.wait("scalar", b.SV, vv2)
            b.emit("scalar", lambda e: e.activation(c[:, CA:CA + BT], c[:, CA:CA + BT], AF.Ln, bias=c[:, b.C_EPS6:b.C_EPS6 + 1], scale=1.0))
            b.drain("scalar")
            vs3 = b.op("scalar", b.SS, lambda e: e.activation(c[:, CA:CA + BT], c[:, CA:CA + BT], AF.Exp, scale=-0.5))
            b.wait("vector", b.SS, vs3)
            b.op("vector", b.SV, lambda e: e.tensor_tensor(c[:, CX:CX + BT], c[:, CR:CR + BT], c[:, CA:CA + BT], op=ALU.mult))
            b.drain("vector")
            last[("chain", bi)] = vs3

        load(0)
        stats(0)
        for blk in range(NB):
            bi = blk % 2
            t0 = blk * BT
            CS, CQ, CM, CV, CR, CA, CX = CO[bi]
            if blk + 1 < NB:
                load(blk + 1)
            vxn_last = None
            for j in range(BT):
                if j == 4 and blk + 1 < NB:
                    stats(blk + 1)
                t = t0 + j
                pi = t % 2
                if ("tp", pi) in last:
                    b.wait("vector", b.ST, last[("tp", pi)])
                b.emit("vector", lambda e, j=j, bi=bi, pi=pi, CM=CM, CR=CR: e.tensor_scalar(
                    xn_tm[pi][:, :], xin[bi][:, j * D:(j + 1) * D],
                    c[:, CM + j:CM + j + 1], c[:, CR + j:CR + j + 1], op0=ALU.subtract, op1=ALU.mult))
                vxn = b.op("vector", b.SV, lambda e, j=j, bi=bi, pi=pi, CM=CM, CX=CX: e.tensor_scalar(
                    xr_tm[pi][:, :], xin[bi][:, j * D:(j + 1) * D],
                    c[:, CM + j:CM + j + 1], c[:, CX + j:CX + j + 1], op0=ALU.subtract, op1=ALU.mult))
                b.drain("vector")
                vxn_last = vxn
                if "ev" in last:
                    b.wait("tensor", b.SS, last["ev"])
                b.wait("tensor", b.SV, vxn)
                vtp = None
                for k in range(8):
                    pd = (b.p0, b.p1)[k // 4]
                    vtp = b.op("tensor", b.ST, lambda e, k=k, pd=pd, pi=pi: e.matmul(
                        pd[:, (k % 4) * 128:(k % 4 + 1) * 128], xn_tm[pi][:, k * 128:(k + 1) * 128],
                        b.identb[:, :], start=True, stop=True))
                for k in range(8):
                    pd = (b.p2, b.p3)[k // 4]
                    vtp = b.op("tensor", b.ST, lambda e, k=k, pd=pd, pi=pi: e.matmul(
                        pd[:, (k % 4) * 128:(k % 4 + 1) * 128], xr_tm[pi][:, k * 128:(k + 1) * 128],
                        b.identb[:, :], start=True, stop=True))
                last[("tp", pi)] = vtp
                if ("st", pi) in last:
                    b.wait("scalar", b.SO, last[("st", pi)])
                b.wait("scalar", b.ST, vtp)
                vev = None
                for q, pd in enumerate((b.p0, b.p1, b.p2, b.p3)):
                    vev = b.op("scalar", b.SS, lambda e, q=q, pd=pd, pi=pi: e.copy(
                        stg[pi][:, q * 512:(q + 1) * 512], pd[:, :]))
                last["ev"] = vev
                b.wait("sync", b.SS, vev)
                for (dst, off) in ((b.xnT, 0), (b.xnrT, D)):
                    for h in range(2):
                        vsp = b.dma("sync", b.SO,
                                    dst.rearrange("(a p) m -> p a m", p=128)[:, 4 * h:4 * h + 4, t * 128:(t + 1) * 128],
                                    stg[pi][:, off + h * 512:off + (h + 1) * 512].rearrange("p (a n) -> p a n", n=128))
                last[("st", pi)] = vsp
            last[("blk", bi)] = vxn_last
        self.end_phase()

    def p1_attention(self):
        b = self
        self.begin_phase()
        tabs = b.sb("p1_tabs", (DH, 2 * T), F32)
        masks = b.sb("p1_masks", (128, 4 * 512), BF16)
        masksf = b.sb("p1_masksf", (128, 4 * 512), F32)
        qg = b.sb("p1_qg", (128, T), BF16)
        kg = b.sb("p1_kg", (128, T), BF16)
        vg = b.sb("p1_vg", (128, NT * 130), BF16)
        vtg = b.sb("p1_vtg", (128, T), BF16)
        xs = [b.sb("p1_xs0", (128, 8 * 512), BF16), b.sb("p1_xs1", (128, 8 * 512), BF16)]
        wraw = b.sb("p1_wraw", (128, 3 * 1024), BF16)
        weff = b.sb("p1_weff", (128, 3 * 1024), BF16)
        pb = b.sb("p1_pb", (128, 4 * 512), BF16)
        t1s = b.sb("p1_t1", (128, 512), F32)
        t2s = b.sb("p1_t2", (128, 512), F32)
        rbs = b.sb("p1_rb", (64, 512), F32)
        rls = b.sb("p1_rl", (128, 2048), F32)
        rlhl = b.sb("p1_rlhl", (128, 2048), BF16)
        cst = b.sb("p1_cst", (128, 2 * 512), BF16)
        vstg = b.sb("p1_vstg", (128, 2 * 512), BF16)

        dma, emit, op, wait, drain = b.dma, b.emit, b.op, b.wait, b.drain
        b.preload(ACT_SET_LNEXP)
        dma("sync", b.SL, tabs[:, 0:T], b.cos_c)
        dma("sync", b.SL, tabs[:, T:2 * T], b.sin_c)
        v_tabs = dma("sync", b.SL, masksf[:, :].rearrange("p (a n) -> p a n", n=512),
                     b.masks_c.rearrange("(a p) n -> p a n", p=128))
        wait("vector", b.SL, v_tabs)
        emit("vector", lambda e: e.tensor_copy(masks[:, :], masksf[:, :]))
        drain("vector")
        emit("vector", lambda e: e.memset(
            vg[:, :].rearrange("p (t h n) -> p t h n", h=2, n=65)[:, :, :, 64:65], 1.0))
        drain("vector")

        prev = {}
        for g in range(_LIM_P1G):
            if g > 0:
                wait("gpsimd", b.ST, prev["qkv_done"])
            vw = None
            for j, W in enumerate((b.wqT, b.wkT, b.wvT)):
                for h in range(2):
                    vw = dma("gpsimd", b.SW,
                             wraw[:, j * 1024 + h * 512:j * 1024 + (h + 1) * 512].rearrange("p (a n) -> p a n", n=128),
                             W[:, g * 128:(g + 1) * 128].rearrange("(a p) n -> p a n", p=128)[:, 4 * h:4 * h + 4, :])
            wait("gpsimd", b.SW, vw)
            vfold = None
            for j in range(3):
                ccol = b.C_NW1Q if j == 0 else b.C_NW1
                for dt in range(8):
                    vfold = b.op("gpsimd", b.SG, lambda e, j=j, dt=dt, ccol=ccol: e.tensor_scalar(
                        weff[:, j * 1024 + dt * 128:j * 1024 + (dt + 1) * 128],
                        wraw[:, j * 1024 + dt * 128:j * 1024 + (dt + 1) * 128],
                        b.cols[:, ccol + dt:ccol + dt + 1], None, op0=ALU.mult))
            if "attn_done" in prev:
                wait("vector", b.ST, prev["attn_done"])
                wait("scalar", b.ST, prev["attn_done"])
            ropeq = {}; ropek = {}; vsec = {}; vcpc = {}
            for c in range(NCH):
                ci = c % 2
                if ("xs", ci) in prev:
                    wait("sync", b.ST, prev[("xs", ci)])
                vxs = b.chunk_load(xs[ci], b.xnrT, c)
                wait("tensor", b.SL, vxs)
                wait("tensor", b.SG, vfold)
                if ("q_ps", ci) in prev:
                    wait("tensor", b.SV, prev[("q_ps", ci)])
                if ("k_ps", ci) in prev:
                    wait("tensor", b.SV, prev[("k_ps", ci)])
                if ("v_ps", ci) in prev:
                    wait("tensor", b.SS, prev[("v_ps", ci)])
                if g > 0 and c == 0:
                    wait("tensor", b.SS, b.SS.n)
                    wait("tensor", b.SV, b.SV.n)
                q_ps = (b.p0, b.p1)[ci]; k_ps = (b.p2, b.p3)[ci]
                v_ps = b.pA[:, ci * 512:(ci + 1) * 512]
                for dt in range(8):
                    emit("tensor", lambda e, dt=dt, q_ps=q_ps, ci=ci: e.matmul(
                        q_ps[:, :], weff[:, dt * 128:(dt + 1) * 128],
                        xs[ci][:, dt * 512:(dt + 1) * 512], start=(dt == 0), stop=(dt == 7)))
                for dt in range(8):
                    emit("tensor", lambda e, dt=dt, k_ps=k_ps, ci=ci: e.matmul(
                        k_ps[:, :], weff[:, 1024 + dt * 128:1024 + (dt + 1) * 128],
                        xs[ci][:, dt * 512:(dt + 1) * 512], start=(dt == 0), stop=(dt == 7)))
                vmm = None
                for dt in range(8):
                    vmm = op("tensor", b.ST, lambda e, dt=dt, v_ps=v_ps, ci=ci: e.matmul(
                        v_ps, weff[:, 2048 + dt * 128:2048 + (dt + 1) * 128],
                        xs[ci][:, dt * 512:(dt + 1) * 512], start=(dt == 0), stop=(dt == 7)))
                prev[("xs", ci)] = vmm
                prev["qkv_done"] = vmm
                wait("vector", b.ST, vmm)
                for which, sp, dst in (("q", q_ps, qg), ("k", k_ps, kg)):
                    emit("vector", lambda e, sp=sp, c=c: e.tensor_tensor(
                        t1s[0:64, :], sp[0:64, :], tabs[:, c * 512:(c + 1) * 512], op=ALU.mult))
                    emit("vector", lambda e, sp=sp, c=c: e.tensor_tensor(
                        t1s[64:128, :], sp[64:128, :], tabs[:, c * 512:(c + 1) * 512], op=ALU.mult))
                    emit("vector", lambda e, sp=sp, c=c: e.tensor_tensor(
                        t2s[0:32, :], sp[32:64, :], tabs[0:32, T + c * 512:T + c * 512 + 512], op=ALU.mult))
                    emit("vector", lambda e, sp=sp, c=c: e.tensor_tensor(
                        t2s[32:64, :], sp[0:32, :], tabs[32:64, T + c * 512:T + c * 512 + 512], op=ALU.mult))
                    emit("vector", lambda e, sp=sp, c=c: e.tensor_tensor(
                        t2s[64:96, :], sp[96:128, :], tabs[0:32, T + c * 512:T + c * 512 + 512], op=ALU.mult))
                    emit("vector", lambda e, sp=sp, c=c: e.tensor_tensor(
                        t2s[96:128, :], sp[64:96, :], tabs[32:64, T + c * 512:T + c * 512 + 512], op=ALU.mult))
                    drain("vector")
                    emit("vector", lambda e, d=dst, c=c: e.tensor_tensor(
                        d[0:32, c * 512:(c + 1) * 512], t1s[0:32, :], t2s[0:32, :], op=ALU.subtract))
                    emit("vector", lambda e, d=dst, c=c: e.tensor_tensor(
                        d[32:64, c * 512:(c + 1) * 512], t1s[32:64, :], t2s[32:64, :], op=ALU.add))
                    emit("vector", lambda e, d=dst, c=c: e.tensor_tensor(
                        d[64:96, c * 512:(c + 1) * 512], t1s[64:96, :], t2s[64:96, :], op=ALU.subtract))
                    vlast = op("vector", b.SV, lambda e, d=dst, c=c: e.tensor_tensor(
                        d[96:128, c * 512:(c + 1) * 512], t1s[96:128, :], t2s[96:128, :], op=ALU.add))
                    drain("vector")
                    prev[("q_ps" if which == "q" else "k_ps", ci)] = vlast
                    if which == "q":
                        ropeq[c] = vlast
                    else:
                        ropek[c] = vlast
                wait("scalar", b.ST, vmm)
                vse = op("scalar", b.SS, lambda e, v_ps=v_ps, c=c: e.copy(
                    vtg[:, c * 512:(c + 1) * 512], v_ps))
                prev[("v_ps", ci)] = vse
                vsec[c] = vse

                def v_transpose(vc):
                    si = vc % 2
                    drain("scalar")
                    if ("vstg", si) in prev:
                        wait("scalar", b.SV, prev[("vstg", si)])
                    vtp = b.dmat(vstg[:, si * 512:(si + 1) * 512].rearrange("p (a n) -> p a n", n=128),
                                 vtg[:, vc * 512:(vc + 1) * 512])
                    wait("vector", b.SX, vtp)
                    vcp = None
                    for hh in range(2):
                        vcp = op("vector", b.SV, lambda e, si=si, vc=vc, hh=hh: e.tensor_copy(
                            vg[:, :].rearrange("p (t h n) -> p t h n", h=2, n=65)[:, 4 * vc:4 * vc + 4, hh, 0:64],
                            vstg[:, si * 512:(si + 1) * 512].rearrange("p (a h n) -> p a h n", h=2, n=64)[:, :, hh, :]))
                    drain("vector")
                    prev[("vstg", si)] = vcp
                    vcpc[vc] = vcp

                if c > 0:
                    v_transpose(c - 1)
                if c == NCH - 1:
                    v_transpose(c)
            prev[("sbank", 0)] = (b.SV, ropeq[6])
            prev[("sbank", 1)] = (b.SV, ropeq[7])
            prev[("sbank", 2)] = (b.SV, ropek[6])
            prev[("ctxv", 0, 0)] = (b.SS, vsec[6])
            prev[("ctxv", 0, 1)] = (b.SS, vsec[7])
            sstep = prev.get("sstep", 0)
            pend = None

            def waitv(eng, entry):
                if isinstance(entry, tuple):
                    wait(eng, entry[0], entry[1])
                else:
                    wait(eng, b.SS, entry)

            def plan_evac_part2(pd):
                vctx = None
                for hh in range(2):
                    wait("tensor", b.SV, pd["vhl"][hh])
                    if "rb" in prev:
                        wait("tensor", b.SS, prev["rb"])
                    emit("tensor", lambda e, hh=hh: e.matmul(
                        b.p3[0:64, :], b.onesb[64:65, 0:64], rlhl[64:65, hh * 1024:hh * 1024 + 512],
                        start=True, stop=False))
                    vbc = op("tensor", b.ST, lambda e, hh=hh: e.matmul(
                        b.p3[0:64, :], b.onesb[64:65, 0:64], rlhl[64:65, hh * 1024 + 512:hh * 1024 + 1024],
                        start=False, stop=True))
                    wait("scalar", b.ST, vbc)
                    vrb = op("scalar", b.SS, lambda e: e.copy(rbs[:, :], b.p3[0:64, :]))
                    prev["rb"] = vrb
                    wait("vector", b.SS, vrb)
                    if ("cst", pd["par"]) in prev:
                        wait("vector", b.SO, prev[("cst", pd["par"])])
                    vctx = op("vector", b.SV, lambda e, hh=hh, pd=pd: e.tensor_tensor(
                        cst[hh * 64:(hh + 1) * 64, pd["par"] * 512:pd["par"] * 512 + 512],
                        pd["ctxb"][hh][0:64, :], rbs[:, :], op=ALU.mult))
                    drain("vector")
                    prev[("ctxv", hh, pd["par"])] = vctx
                wait("sync", b.SV, vctx)
                vsp = b.dma("sync", b.SO,
                            b.ctxT[g * 128:(g + 1) * 128, pd["qc"] * 512:(pd["qc"] + 1) * 512],
                            cst[:, pd["par"] * 512:pd["par"] * 512 + 512])
                prev[("cst", pd["par"])] = vsp

            for qc in range(NCH):
                nkt = 4 * qc + 4
                par = qc % 2
                ctxb = {0: b.pA[:, par * 512:(par + 1) * 512],
                        1: b.pB[:, par * 512:(par + 1) * 512]}
                vexp_q = [[None] * nkt for _ in range(2)]
                vpv_last = [None, None]
                for kt in range(nkt + 1):
                    if kt < nkt:
                        for hh in range(2):
                            j = sstep % 3
                            s_ps = (b.p0, b.p1, b.p2)[j]
                            if ("sbank", j) in prev:
                                waitv("tensor", prev[("sbank", j)])
                            wait("tensor", b.SV, ropeq[qc])
                            wait("tensor", b.SV, ropek[kt // 4])
                            vs_mm = op("tensor", b.ST, lambda e, hh=hh, kt=kt, qc=qc, s_ps=s_ps: e.matmul(
                                s_ps[:, :], kg[hh * 64:(hh + 1) * 64, kt * 128:(kt + 1) * 128],
                                qg[hh * 64:(hh + 1) * 64, qc * 512:(qc + 1) * 512],
                                start=True, stop=True, tile_position=(hh * 64, 0)))
                            a = kt - 4 * qc
                            pslot = sstep % 4
                            pbs = pb[:, pslot * 512:(pslot + 1) * 512]
                            if a >= 0:
                                wait("vector", b.ST, vs_mm)
                                vmask = op("vector", b.SV, lambda e, s_ps=s_ps, a=a: e.tensor_tensor(
                                    s_ps[:, :], s_ps[:, :], masks[:, a * 512:(a + 1) * 512], op=ALU.add))
                                wait("scalar", b.SV, vmask)
                            else:
                                wait("scalar", b.ST, vs_mm)
                            if ("pb", pslot) in prev:
                                wait("scalar", b.ST, prev[("pb", pslot)])
                            vexp = op("scalar", b.SS, lambda e, s_ps=s_ps, pbs=pbs: e.activation(
                                pbs, s_ps[:, :], AF.Exp))
                            prev[("sbank", j)] = vexp
                            vexp_q[hh][kt] = (vexp, pslot)
                            sstep += 1
                    if kt == 1 and pend is not None:
                        plan_evac_part2(pend)
                        pend = None
                    if kt >= 1:
                        ktp = kt - 1
                        for hh in range(2):
                            vexp, pslot = vexp_q[hh][ktp]
                            wait("tensor", b.SS, vexp)
                            wait("tensor", b.SV, vcpc[ktp // 4])
                            if ktp == 0 and ("ctxv", hh, par) in prev:
                                ent = prev[("ctxv", hh, par)]
                                if isinstance(ent, tuple):
                                    wait("tensor", ent[0], ent[1])
                                else:
                                    wait("tensor", b.SV, ent)
                            vpv = op("tensor", b.ST, lambda e, ktp=ktp, hh=hh, nkt=nkt, ctxb=ctxb, pslot=pslot: e.matmul(
                                ctxb[hh][0:65, :], vg[:, ktp * 130 + hh * 65:ktp * 130 + hh * 65 + 65],
                                pb[:, pslot * 512:(pslot + 1) * 512],
                                start=(ktp == 0), stop=(ktp == nkt - 1)))
                            prev[("pb", pslot)] = vpv
                            if ktp == nkt - 1:
                                vpv_last[hh] = vpv
                vhls = [None, None]
                for hh in range(2):
                    wait("scalar", b.ST, vpv_last[hh])
                    emit("scalar", lambda e, hh=hh, ctxb=ctxb: e.activation(
                        rls[64:65, hh * 1024:hh * 1024 + 512], ctxb[hh][64:65, :], AF.Ln))
                    drain("scalar")
                    vrl = op("scalar", b.SS, lambda e, hh=hh: e.activation(
                        rls[64:65, hh * 1024 + 512:hh * 1024 + 1024],
                        rls[64:65, hh * 1024:hh * 1024 + 512], AF.Exp, scale=-1.0))
                    wait("vector", b.SS, vrl)
                    emit("vector", lambda e, hh=hh: e.tensor_copy(
                        rlhl[64:65, hh * 1024:hh * 1024 + 512],
                        rls[64:65, hh * 1024 + 512:hh * 1024 + 1024]))
                    drain("vector")
                    vhl = op("vector", b.SV, lambda e, hh=hh: e.tensor_tensor(
                        rlhl[64:65, hh * 1024 + 512:hh * 1024 + 1024],
                        rls[64:65, hh * 1024 + 512:hh * 1024 + 1024],
                        rlhl[64:65, hh * 1024:hh * 1024 + 512], op=ALU.subtract))
                    drain("vector")
                    vhls[hh] = vhl
                pend = dict(qc=qc, par=par, ctxb=ctxb, vhl=vhls)
            plan_evac_part2(pend)
            pend = None
            prev["sstep"] = sstep
            prev["attn_done"] = b.ST.n
        self.end_phase()

    def p2_wo_norm2(self):
        b = self
        self.begin_phase()
        wo_sb = b.sb("p2_wo", (128, 8 * 1024), BF16)
        rw_sb = b.sb("p2_rw", (128, 8 * EA), BF16)
        ctx_s = [b.sb("p2_c0", (128, 8 * 512), BF16), b.sb("p2_c1", (128, 8 * 512), BF16)]
        xn_s = [b.sb("p2_x0", (128, 8 * 512), BF16), b.sb("p2_x1", (128, 8 * 512), BF16)]
        xa_scr = [b.sb("p2_xa0", (128, 8 * 512), BF16), b.sb("p2_xa1", (128, 8 * 512), BF16)]
        h2_scr = [b.sb("p2_h20", (128, 8 * 512), BF16), b.sb("p2_h21", (128, 8 * 512), BF16)]
        sqt = [b.sb("p2_sq0", (128, 512), BF16), b.sb("p2_sq1", (128, 512), BF16)]
        r2row = b.sb("p2_r2", (1, T), F32)
        r2hl = b.sb("p2_r2hl", (1, 1024), BF16)
        tstg = b.sb("p2_ts", (128, 2 * 512), BF16)
        lgt = b.sb("p2_lg", (128, 4 * EA), F32)
        gts = b.sb("p2_gt", (128, 4 * EA), F32)
        mk1 = b.sb("p2_mk1", (128, 4 * EA), F32)
        mk2 = b.sb("p2_mk2", (128, 4 * EA), F32)
        mskd = b.sb("p2_msk", (128, 4 * EA), F32)
        c = b.cols

        dma, emit, op, wait, drain = b.dma, b.emit, b.op, b.wait, b.drain
        for h in range(2):
            dma("gpsimd", b.SW, wo_sb[:, h * 4096:(h + 1) * 4096].rearrange("p (a n) -> p a n", n=1024),
                b.woT.rearrange("(a p) n -> p a n", p=128)[:, 4 * h:4 * h + 4, :])
        vw = dma("gpsimd", b.SW, rw_sb[:, :].rearrange("p (a n) -> p a n", n=EA),
                 b.router_wT.rearrange("(a p) n -> p a n", p=128))
        prev = {}

        def plan_passB(cc):
            ci = cc % 2
            vr2 = prev[("r2", cc)]
            wait("vector", b.SS, vr2)
            emit("vector", lambda e, cc=cc: e.tensor_copy(
                r2hl[0:1, 0:512], r2row[0:1, cc * 512:(cc + 1) * 512]))
            drain("vector")
            vhl = op("vector", b.SV, lambda e, cc=cc: e.tensor_tensor(
                r2hl[0:1, 512:1024], r2row[0:1, cc * 512:(cc + 1) * 512], r2hl[0:1, 0:512], op=ALU.subtract))
            drain("vector")
            wait("tensor", b.SV, vhl)
            if "r2b" in prev:
                wait("tensor", b.SV, prev["r2b"])
            emit("tensor", lambda e: e.matmul(
                b.p3[:, :], b.onesb[0:1, 0:128], r2hl[0:1, 0:512], start=True, stop=False))
            vbc = op("tensor", b.ST, lambda e: e.matmul(
                b.p3[:, :], b.onesb[0:1, 0:128], r2hl[0:1, 512:1024], start=False, stop=True))
            wait("vector", b.ST, vbc)
            vh2_last = None
            vtp = None
            for et in range(8):
                vh2 = op("vector", b.SV, lambda e, et=et, ci=ci: e.scalar_tensor_tensor(
                    h2_scr[ci][:, et * 512:(et + 1) * 512], xa_scr[ci][:, et * 512:(et + 1) * 512],
                    b.cols[:, b.C_NW2 + et:b.C_NW2 + et + 1], b.p3[:, :],
                    op0=ALU.mult, op1=ALU.mult))
                drain("vector")
                vh2_last = vh2
                wait("gpsimd", b.SV, vh2)
                dma("gpsimd", b.SW, b.h2nT[et * 128:(et + 1) * 128, cc * 512:(cc + 1) * 512],
                    h2_scr[ci][:, et * 512:(et + 1) * 512])
                si = et % 2
                if ("tstg", si) in prev:
                    wait("scalar", b.SO, prev[("tstg", si)])
                vtp = b.dmat(tstg[:, si * 512:(si + 1) * 512].rearrange("p (a n) -> p a n", n=128),
                             xa_scr[ci][:, et * 512:(et + 1) * 512])
                wait("sync", b.SX, vtp)
                xa_ap = bass.AP(b.xaT.tensor, (cc * 512) * D + et * 128,
                                [[D, 128], [128 * D, 4], [1, 128]])
                vso = b.dma("sync", b.SO, xa_ap,
                            tstg[:, si * 512:(si + 1) * 512].rearrange("p (a n) -> p a n", n=128))
                prev[("tstg", si)] = vso
            prev[("xa_done", ci)] = (vh2_last, vtp)
            prev["r2b"] = vh2_last
            wait("tensor", b.SV, vh2_last)
            M1, M2, NR, IV = 20, 24, 28, 32
            vlgs = []
            for tt in range(4):
                rps = b.pB[:, 0:EA] if tt % 2 == 0 else b.pB[:, 512:512 + EA]
                if ("rtA", tt % 2) in prev:
                    wait("tensor", b.SV, prev[("rtA", tt % 2)])
                vrt = None
                for dt in range(8):
                    vrt = op("tensor", b.ST, lambda e, dt=dt, tt=tt, ci=ci, rps=rps: e.matmul(
                        rps, h2_scr[ci][:, dt * 512 + tt * 128:dt * 512 + tt * 128 + 128],
                        rw_sb[:, dt * EA:(dt + 1) * EA], start=(dt == 0), stop=(dt == 7)))
                wait("vector", b.ST, vrt)
                if "sigA" in prev:
                    wait("vector", b.SS, prev["sigA"])
                vlg = op("vector", b.SV, lambda e, tt=tt, rps=rps: e.tensor_tensor(
                    lgt[:, tt * EA:(tt + 1) * EA], rps, b.ebias_r[:, :], op=ALU.add))
                prev[("rtA", tt % 2)] = vlg
                vlgs.append(vlg)
            drain("vector")
            wait("scalar", b.SV, vlgs[-1])
            vsg = op("scalar", b.SS, lambda e: e.activation(gts[:, :], lgt[:, :], AF.Sigmoid))
            prev["sigA"] = vsg
            wait("vector", b.SS, vsg)
            for tt in range(4):
                emit("vector", lambda e, tt=tt: e.tensor_reduce(
                    c[:, M1 + tt:M1 + tt + 1], gts[:, tt * EA:(tt + 1) * EA], AX.X, ALU.max))
            drain("vector")
            for tt in range(4):
                emit("vector", lambda e, tt=tt: e.tensor_scalar(
                    mk1[:, tt * EA:(tt + 1) * EA], gts[:, tt * EA:(tt + 1) * EA],
                    c[:, M1 + tt:M1 + tt + 1], None, op0=ALU.is_equal))
            drain("vector")
            for tt in range(4):
                emit("vector", lambda e, tt=tt: e.scalar_tensor_tensor(
                    mskd[:, tt * EA:(tt + 1) * EA], mk1[:, tt * EA:(tt + 1) * EA], -1e30,
                    gts[:, tt * EA:(tt + 1) * EA], op0=ALU.mult, op1=ALU.add))
            drain("vector")
            for tt in range(4):
                emit("vector", lambda e, tt=tt: e.tensor_reduce(
                    c[:, M2 + tt:M2 + tt + 1], mskd[:, tt * EA:(tt + 1) * EA], AX.X, ALU.max))
            drain("vector")
            for tt in range(4):
                emit("vector", lambda e, tt=tt: e.tensor_scalar(
                    mk2[:, tt * EA:(tt + 1) * EA], mskd[:, tt * EA:(tt + 1) * EA],
                    c[:, M2 + tt:M2 + tt + 1], None, op0=ALU.is_equal))
            emit("vector", lambda e: e.tensor_tensor(c[:, NR:NR + 4], c[:, M1:M1 + 4], c[:, M2:M2 + 4], op=ALU.add))
            drain("vector")
            emit("vector", lambda e: e.tensor_tensor(mk1[:, :], mk1[:, :], mk2[:, :], op=ALU.add))
            emit("vector", lambda e: e.tensor_scalar(c[:, NR:NR + 4], c[:, NR:NR + 4], 1e-9, None, op0=ALU.add))
            drain("vector")
            emit("vector", lambda e: e.tensor_tensor(mk1[:, :], mk1[:, :], gts[:, :], op=ALU.mult))
            emit("vector", lambda e: e.reciprocal(c[:, IV:IV + 4], c[:, NR:NR + 4]))
            drain("vector")
            for tt in range(4):
                ttg = cc * 4 + tt
                op("vector", b.SV, lambda e, tt=tt, ttg=ttg: e.tensor_scalar(
                    b.cwa[:, ttg * EA:(ttg + 1) * EA], mk1[:, tt * EA:(tt + 1) * EA],
                    c[:, IV + tt:IV + tt + 1], None, op0=ALU.mult))
            drain("vector")
            prev[("routers_done", ci)] = b.ST.n

        for cc in range(NCH):
            ci = cc % 2
            if ("ld", ci) in prev:
                wait("sync", b.ST, prev[("ld", ci)][0])
                wait("sync", b.SV, prev[("ld", ci)][1])
            b.chunk_load(ctx_s[ci], b.ctxT, cc)
            vld = b.chunk_load(xn_s[ci], b.xnT, cc)
            wait("tensor", b.SL, vld)
            wait("tensor", b.SW, vw)
            if ("xa_done", ci) in prev:
                wait("vector", b.ST, prev[("routers_done", ci)])
                wait("vector", b.SX, prev[("xa_done", ci)][1])
                wait("vector", b.SW, b.SW.n)
            vss = None
            vxa_last = None
            sq_pend = None
            for et in range(8):
                d_ps = (b.p0, b.p1)[et % 2]
                if ("d_ps", et % 2) in prev:
                    wait("tensor", b.SV, prev[("d_ps", et % 2)])
                vdm = None
                for dt in range(8):
                    vdm = op("tensor", b.ST, lambda e, dt=dt, et=et, d_ps=d_ps, ci=ci: e.matmul(
                        d_ps[:, :], wo_sb[:, dt * 1024 + et * 128:dt * 1024 + (et + 1) * 128],
                        ctx_s[ci][:, dt * 512:(dt + 1) * 512], start=(dt == 0), stop=(dt == 7)))
                if sq_pend is not None:
                    pet, pv = sq_pend
                    wait("tensor", b.SV, pv)
                    if pet == 0 and ("r2", cc - 1) in prev:
                        wait("tensor", b.SS, prev[("r2", cc - 1)])
                    vss = op("tensor", b.ST, lambda e, pet=pet: e.matmul(
                        b.p2[0:1, :], b.onesb[:, 0:1], sqt[pet % 2][:, :], start=(pet == 0), stop=False))
                    prev[("ssq", pet % 2)] = vss
                wait("vector", b.ST, vdm)
                vxa = op("vector", b.SV, lambda e, et=et, d_ps=d_ps, ci=ci: e.tensor_tensor(
                    xa_scr[ci][:, et * 512:(et + 1) * 512], d_ps[:, :],
                    xn_s[ci][:, et * 512:(et + 1) * 512], op=ALU.add))
                prev[("d_ps", et % 2)] = vxa
                vxa_last = vxa
                drain("vector")
                if ("ssq", et % 2) in prev:
                    wait("vector", b.ST, prev[("ssq", et % 2)])
                vsq = op("vector", b.SV, lambda e, et=et, ci=ci: e.tensor_tensor(
                    sqt[et % 2][:, :], xa_scr[ci][:, et * 512:(et + 1) * 512],
                    xa_scr[ci][:, et * 512:(et + 1) * 512], op=ALU.mult))
                drain("vector")
                sq_pend = (et, vsq)
            pet, pv = sq_pend
            wait("tensor", b.SV, pv)
            vss = op("tensor", b.ST, lambda e, pet=pet: e.matmul(
                b.p2[0:1, :], b.onesb[:, 0:1], sqt[pet % 2][:, :], start=False, stop=True))
            prev[("ssq", pet % 2)] = vss
            wait("scalar", b.ST, vss)
            emit("scalar", lambda e, cc=cc: e.activation(
                r2row[0:1, cc * 512:(cc + 1) * 512], b.p2[0:1, :], AF.Ln,
                bias=c[0:1, b.C_EPS6:b.C_EPS6 + 1], scale=1.0 / D))
            drain("scalar")
            vr2 = op("scalar", b.SS, lambda e, cc=cc: e.activation(
                r2row[0:1, cc * 512:(cc + 1) * 512], r2row[0:1, cc * 512:(cc + 1) * 512], AF.Exp, scale=-0.5))
            prev[("r2", cc)] = vr2
            prev[("ld", ci)] = (b.ST.n, vxa_last)
            if cc > 0:
                plan_passB(cc - 1)
        plan_passB(NCH - 1)
        self.end_phase()

    def _moe(self, tag, n_exp, inner, stream_dram, wu1, wu3, wdn, su1, su3, sdn,
             acc, cwbuf, cw_w, overwrite_first):
        b = self
        nit = inner // 128
        wA, wB, wC = self.wsA, self.wsB, self.wsC
        xs, hb, gsc = self.ms_xs, self.ms_hb, self.ms_gs
        dma, emit, op, wait, drain = b.dma, b.emit, b.op, b.wait, b.drain
        prev = self.ms_prev

        def load_w(which, e_idx):
            if e_idx == -1:
                up1, up3, dn = su1, su3, sdn
            else:
                up1, up3, dn = wu1[e_idx], wu3[e_idx], wdn[e_idx]
            v = None
            if which == "A" or which == "B":
                W = up1 if which == "A" else up3
                slot = wA if which == "A" else wB
                for h in range(2):
                    v = dma("gpsimd", b.SW,
                            slot[:, h * 4 * inner:(h + 1) * 4 * inner].rearrange("p (a n) -> p a n", n=inner),
                            W.rearrange("(a p) n -> p a n", p=128)[:, 4 * h:4 * h + 4, :])
            else:
                nh = max(1, nit // 4)
                step = nit // nh
                for h in range(nh):
                    v = dma("gpsimd", b.SW,
                            wC[:, h * step * 1024:(h + 1) * step * 1024].rearrange("p (a n) -> p a n", n=1024),
                            dn.rearrange("(a p) n -> p a n", p=128)[:, step * h:step * (h + 1), :])
            return v

        exps = [-1] + list(range(n_exp))
        lim = _P4_NEXP if tag == "p4" else _P3_NEXP
        if lim is not None:
            exps = exps[:lim]
        wait("tensor", b.SV, b.SV.n)
        wait("tensor", b.SS, b.SS.n)
        vA = load_w("A", exps[0]); vB = load_w("B", exps[0]); vC = load_w("C", exps[0])
        for xi, eidx in enumerate(exps):
            for tc in range(NCH):
                ci = tc % 2
                if (tag, "xs", ci) in prev:
                    wait("sync", b.ST, prev[(tag, "xs", ci)])
                vxs = b.chunk_load(xs[ci], stream_dram, tc)
                wait("tensor", b.SL, vxs)
                if tc == 0:
                    wait("tensor", b.SW, vA)
                    wait("tensor", b.SW, vB)
                if (tag, "down_st") in prev:
                    wait("vector", b.ST, prev[(tag, "down_st")])
                for it in range(nit):
                    g_ps = (b.p0, b.p1)[it % 2]
                    u_ps = (b.p2, b.p3)[it % 2]
                    if (tag, "g_ps", it % 2) in prev:
                        wait("tensor", b.SS, prev[(tag, "g_ps", it % 2)])
                    vgm = None
                    for dt in range(8):
                        vgm = op("tensor", b.ST, lambda e, dt=dt, it=it, g_ps=g_ps, ci=ci: e.matmul(
                            g_ps[:, :], wA[:, dt * inner + it * 128:dt * inner + (it + 1) * 128],
                            xs[ci][:, dt * 512:(dt + 1) * 512], start=(dt == 0), stop=(dt == 7)))
                    if (tag, "u_ps", it % 2) in prev:
                        wait("tensor", b.SV, prev[(tag, "u_ps", it % 2)])
                    vum = None
                    for dt in range(8):
                        vum = op("tensor", b.ST, lambda e, dt=dt, it=it, u_ps=u_ps, ci=ci: e.matmul(
                            u_ps[:, :], wB[:, dt * inner + it * 128:dt * inner + (it + 1) * 128],
                            xs[ci][:, dt * 512:(dt + 1) * 512], start=(dt == 0), stop=(dt == 7)))
                    wait("scalar", b.ST, vgm)
                    gs = gsc[it % 2]
                    if (tag, "gs", it % 2) in prev:
                        wait("scalar", b.SV, prev[(tag, "gs", it % 2)])
                    vsi = op("scalar", b.SS, lambda e, g_ps=g_ps, gs=gs: e.activation(
                        gs[:, :], g_ps[:, :], AF.Silu))
                    prev[(tag, "g_ps", it % 2)] = vsi
                    wait("vector", b.ST, vum)
                    wait("vector", b.SS, vsi)
                    vh = op("vector", b.SV, lambda e, it=it, u_ps=u_ps, gs=gs: e.tensor_tensor(
                        hb[:, it * 512:(it + 1) * 512], gs[:, :], u_ps[:, :], op=ALU.mult))
                    drain("vector")
                    prev[(tag, "u_ps", it % 2)] = vh
                    prev[(tag, "gs", it % 2)] = vh
                prev[(tag, "xs", ci)] = b.ST.n
                if self.dev and tag == "p4" and xi == 0 and tc == 0:
                    vtap = b.SV.n
                    b.wait("sync", b.SV, vtap)
                    b.dma("sync", b.SO, self.hb_d, hb[:, 0:16 * 512])
                    b.dma("sync", b.SO, self.wa_d, wA[:, 0:8 * inner])
                    b.dma("sync", b.SO, self.xs_d, xs[ci][:, :])
                    b.dma("sync", b.SO, self.gs_d, gsc[0][:, :])
                if tc == NCH - 1 and xi + 1 < len(exps):
                    wait("gpsimd", b.ST, b.ST.n)
                    vA = load_w("A", exps[xi + 1])
                    vB = load_w("B", exps[xi + 1])
                if tc == 0:
                    wait("tensor", b.SW, vC)
                wait("tensor", b.SV, b.SV.n)
                for tt in range(4):
                    ttg = tc * 4 + tt
                    d_ps = (b.pA, b.pB)[tt % 2]
                    if (tag, "d_ps", tt % 2) in prev:
                        wait("tensor", b.SV, prev[(tag, "d_ps", tt % 2)])
                    vdm = None
                    for dch in range(2):
                        for it in range(nit):
                            vdm = op("tensor", b.ST, lambda e, tt=tt, dch=dch, it=it, d_ps=d_ps: e.matmul(
                                d_ps[:, dch * 512:(dch + 1) * 512],
                                hb[:, it * 512 + tt * 128:it * 512 + tt * 128 + 128],
                                wC[:, it * 1024 + dch * 512:it * 1024 + dch * 512 + 512],
                                start=(it == 0), stop=(it == nit - 1)))
                    wait("vector", b.ST, vdm)
                    accs = acc[:, ttg * D:(ttg + 1) * D]
                    if overwrite_first and xi == 0:
                        vac = op("vector", b.SV, lambda e, d_ps=d_ps, accs=accs: e.tensor_scalar(
                            accs, d_ps[:, :], 1.0, None, op0=ALU.mult))
                    elif eidx == -1:
                        vac = op("vector", b.SV, lambda e, d_ps=d_ps, accs=accs: e.scalar_tensor_tensor(
                            accs, d_ps[:, :], 1.0, accs, op0=ALU.mult, op1=ALU.add))
                    else:
                        cwcol = cwbuf[:, ttg * cw_w + eidx:ttg * cw_w + eidx + 1]
                        vac = op("vector", b.SV, lambda e, d_ps=d_ps, accs=accs, cwcol=cwcol: e.scalar_tensor_tensor(
                            accs, d_ps[:, :], cwcol, accs, op0=ALU.mult, op1=ALU.add))
                    drain("vector")
                    prev[(tag, "d_ps", tt % 2)] = vac
                prev[(tag, "down_st")] = b.ST.n
            if xi + 1 < len(exps):
                wait("gpsimd", b.ST, b.ST.n)
                vC = load_w("C", exps[xi + 1])

    def p4_ffn(self):
        b = self
        self.begin_phase()
        self.yz = b.sb_persist("yz", (128, NT * D), BF16)
        self.wsA = b.sb("p4_wA", (128, 8 * IF), BF16)
        self.wsB = b.sb("p4_wB", (128, 8 * IF), BF16)
        self.wsC = b.sb("p4_wC", (128, 16 * D), BF16)
        self.ms_xs = [b.sb("p4_xs0", (128, 8 * 512), BF16), b.sb("p4_xs1", (128, 8 * 512), BF16)]
        self.ms_hb = b.sb("p4_hb", (128, 16 * 512), BF16)
        self.ms_gs = [b.sb("p4_g0", (128, 512), BF16), b.sb("p4_g1", (128, 512), BF16)]
        self.ms_prev = {}
        gw_sb = b.sb("p4_gw", (128, 8 * EF), BF16)
        exs = b.sb("p4_ex", (128, EF), F32)
        sco = b.sb("p4_sc", (128, EF), F32)
        bsd = b.sb("p4_bs", (128, EF), F32)
        mk1 = b.sb("p4_m1", (128, EF), F32)
        mk2 = b.sb("p4_m2", (128, EF), F32)
        mskd = b.sb("p4_mk", (128, EF), F32)
        c = b.cols
        mxc = c[:, 50:51]; nmx = c[:, 51:52]; smc = c[:, 52:53]; rsc = c[:, 53:54]
        m1c = c[:, 54:55]; m2c = c[:, 55:56]

        dma, emit, op, wait, drain = b.dma, b.emit, b.op, b.wait, b.drain
        vgw = dma("gpsimd", b.SW, gw_sb[:, :].rearrange("p (a n) -> p a n", n=EF),
                  b.gate_wT.rearrange("(a p) n -> p a n", p=128))
        xs = self.ms_xs
        prevR = {}
        for cc in range(NCH):
            ci = cc % 2
            if ("xs", ci) in prevR:
                wait("sync", b.ST, prevR[("xs", ci)])
            vxs = b.chunk_load(xs[ci], b.xnT, cc)
            wait("tensor", b.SL, vxs)
            wait("tensor", b.SW, vgw)
            for tt in range(4):
                ttg = cc * 4 + tt
                if "pB" in prevR:
                    wait("tensor", b.SV, prevR["pBv"])
                    wait("tensor", b.SS, prevR["pBs"])
                vrt = None
                for dt in range(8):
                    vrt = op("tensor", b.ST, lambda e, dt=dt, tt=tt, ci=ci: e.matmul(
                        b.pB[:, 0:EF], xs[ci][:, dt * 512 + tt * 128:dt * 512 + tt * 128 + 128],
                        gw_sb[:, dt * EF:(dt + 1) * EF], start=(dt == 0), stop=(dt == 7)))
                wait("vector", b.ST, vrt)
                emit("vector", lambda e: e.tensor_reduce(mxc, b.pB[:, 0:EF], AX.X, ALU.max))
                drain("vector")
                vnm = op("vector", b.SV, lambda e: e.tensor_scalar(nmx, mxc, -1.0, None, op0=ALU.mult))
                prevR["pB"] = True
                prevR["pBv"] = vnm
                wait("scalar", b.SV, vnm)
                vex = op("scalar", b.SS, lambda e: e.activation(
                    exs[:, :], b.pB[:, 0:EF], AF.Exp, bias=nmx, scale=1.0))
                prevR["pBs"] = vex
                wait("vector", b.SS, vex)
                emit("vector", lambda e: e.tensor_reduce(smc, exs[:, :], AX.X, ALU.add))
                drain("vector")
                emit("vector", lambda e: e.reciprocal(rsc, smc))
                drain("vector")
                emit("vector", lambda e: e.tensor_scalar(sco[:, :], exs[:, :], rsc, None, op0=ALU.mult))
                drain("vector")
                emit("vector", lambda e: e.tensor_tensor(bsd[:, :], sco[:, :], b.gbias_r[:, :], op=ALU.add))
                drain("vector")
                emit("vector", lambda e: e.tensor_reduce(m1c, bsd[:, :], AX.X, ALU.max))
                drain("vector")
                emit("vector", lambda e: e.tensor_scalar(mk1[:, :], bsd[:, :], m1c, None, op0=ALU.is_equal))
                drain("vector")
                emit("vector", lambda e: e.scalar_tensor_tensor(
                    mskd[:, :], mk1[:, :], -1e30, bsd[:, :], op0=ALU.mult, op1=ALU.add))
                drain("vector")
                emit("vector", lambda e: e.tensor_reduce(m2c, mskd[:, :], AX.X, ALU.max))
                drain("vector")
                emit("vector", lambda e: e.tensor_scalar(mk2[:, :], mskd[:, :], m2c, None, op0=ALU.is_equal))
                drain("vector")
                emit("vector", lambda e: e.tensor_tensor(mk1[:, :], mk1[:, :], mk2[:, :], op=ALU.add))
                drain("vector")
                op("vector", b.SV, lambda e, ttg=ttg: e.tensor_tensor(
                    b.cwf[:, ttg * EF:(ttg + 1) * EF], mk1[:, :], sco[:, :], op=ALU.mult))
                drain("vector")
            prevR[("xs", ci)] = b.ST.n
        self.ms_prev[("p4", "xs", 0)] = b.ST.n
        self.ms_prev[("p4", "xs", 1)] = b.ST.n
        self._moe("p4", EF, IF, b.xnT,
                  [self.w1T[e] for e in range(EF)], [self.w3T[e] for e in range(EF)],
                  [self.w2T[e] for e in range(EF)],
                  self.sw1T, self.sw3T, self.sw2T, self.yz, b.cwf, EF, True)
        if self.dev:
            b.wait("sync", b.SV, b.SV.n)
            b.dma("sync", b.SO, self.yz_d, self.yz[:, :])
        self.end_phase()

    def p3_amoe(self):
        b = self
        self.begin_phase()
        self.sr = b.sb_persist("sr", (128, NT * D), BF16)
        self.wsA = b.sb("p3_wA", (128, 8 * IA), BF16)
        self.wsB = b.sb("p3_wB", (128, 8 * IA), BF16)
        self.wsC = b.sb("p3_wC", (128, 8 * D), BF16)
        self.ms_xs = [b.sb("p3_xs0", (128, 8 * 512), BF16), b.sb("p3_xs1", (128, 8 * 512), BF16)]
        self.ms_hb = b.sb("p3_hb", (128, 8 * 512), BF16)
        self.ms_gs = [b.sb("p3_g0", (128, 512), BF16), b.sb("p3_g1", (128, 512), BF16)]
        self.ms_prev = {}
        vin = None
        for q in range(8):
            vin = b.dma("sync", b.SL, self.sr[:, q * 4 * D:(q + 1) * 4 * D].rearrange("p (t n) -> p t n", n=D),
                        b.xaT.rearrange("(t p) n -> p t n", p=128)[:, 4 * q:4 * q + 4, :])
        b.wait("vector", b.SL, vin)
        self._moe("p3", EA, IA, b.h2nT,
                  [self.egT[e] for e in range(EA)], [self.euT[e] for e in range(EA)],
                  [self.edT[e] for e in range(EA)],
                  self.sgT, self.suT, self.sdT, self.sr, b.cwa, EA, False)
        if self.dev:
            b.wait("sync", b.SV, b.SV.n)
            b.dma("sync", b.SO, self.sr_d, self.sr[:, :])
        self.end_phase()

    def p5_combine(self):
        b = self
        self.begin_phase()
        xin = [b.sb("p5_x0", (128, D), F32), b.sb("p5_x1", (128, D), F32)]
        gsc = [b.sb("p5_g0", (128, D), F32), b.sb("p5_g1", (128, D), F32)]
        tsc = [b.sb("p5_t0", (128, D), F32), b.sb("p5_t1", (128, D), F32)]
        osb = [b.sb("p5_o0", (128, D), F32), b.sb("p5_o1", (128, D), F32)]
        dma, emit, op, wait, drain = b.dma, b.emit, b.op, b.wait, b.drain
        prev = {}
        for t in range(NT):
            pi = t % 2
            if ("vo", pi) in prev:
                wait("sync", b.SV, prev[("vo", pi)])
            vx = dma("sync", b.SL, xin[pi][:, :], b.x[t * 128:(t + 1) * 128, :])
            if ("vt", pi) in prev:
                wait("scalar", b.SV, prev[("vt", pi)])
            vg = op("scalar", b.SS, lambda e, t=t, pi=pi: e.activation(
                gsc[pi][:, :], self.yz[:, t * D:(t + 1) * D], AF.Silu))
            wait("vector", b.SS, vg)
            vt = op("vector", b.SV, lambda e, t=t, pi=pi: e.tensor_tensor(
                tsc[pi][:, :], self.sr[:, t * D:(t + 1) * D], gsc[pi][:, :], op=ALU.mult))
            drain("vector")
            prev[("vt", pi)] = vt
            wait("vector", b.SL, vx)
            if ("st", pi) in prev:
                wait("vector", b.SO, prev[("st", pi)])
            vo = op("vector", b.SV, lambda e, pi=pi: e.tensor_tensor(
                osb[pi][:, :], tsc[pi][:, :], xin[pi][:, :], op=ALU.add))
            drain("vector")
            prev[("vo", pi)] = vo
            wait("sync", b.SV, vo)
            vst = dma("sync", b.SO, b.out[t * 128:(t + 1) * 128, :], osb[pi][:, :])
            prev[("st", pi)] = vst
        b.wait("sync", b.SO, b.SO.n)
        self.end_phase()


def build_core(dev=False, phases=_PHASES):
    return Builder(dev=dev, phases=phases).build()



def _constants():
    pos = np.arange(T, dtype=np.float32)
    inv_freq = 1.0 / (10000.0 ** (np.arange(0, DH, 2, dtype=np.float32) / DH))
    freqs = np.concatenate([inv_freq, inv_freq])[:, None] * pos[None, :]
    cos_c = np.cos(freqs).astype(np.float32)
    sin_c = np.sin(freqs).astype(np.float32)
    masks = np.zeros((4, 128, 512), dtype=np.float32)
    for a in range(4):
        kk = a * 128 + np.arange(128)[:, None]
        qq = np.arange(512)[None, :]
        masks[a] = np.where(kk <= qq, 0.0, -1e9)
    masks_c = np.ascontiguousarray(masks.reshape(4 * 128, 512))
    ones = np.ones((128, 128), dtype=np.float32)
    return dict(cos_c=cos_c, sin_c=sin_c, masks_c=masks_c, ones_f32=ones,
                ident_c=np.eye(128, dtype=np.float32))


def _tr(a):
    return np.ascontiguousarray(np.swapaxes(np.asarray(a), -1, -2))


def kernel(**inp):
    consts = _constants()
    w = dict(
        wqT=_tr(inp["L0_attn_wq"]), wkT=_tr(inp["L0_attn_wk"]),
        wvT=_tr(inp["L0_attn_wv"]), woT=_tr(inp["L0_attn_wo"]),
        nw1=np.ascontiguousarray(np.asarray(inp["L0_attn_norm1_w"])[None, :]),
        nw2=np.ascontiguousarray(np.asarray(inp["L0_attn_norm2_w"])[None, :]),
        router_wT=_tr(inp["L0_attn_router_w"]),
        ebias=np.ascontiguousarray(np.asarray(inp["L0_attn_expert_bias"])[None, :]),
        sgT=_tr(inp["L0_attn_sg"]), suT=_tr(inp["L0_attn_su"]), sdT=_tr(inp["L0_attn_sd"]),
        egT=_tr(inp["L0_attn_eg"]), euT=_tr(inp["L0_attn_eu"]), edT=_tr(inp["L0_attn_ed"]),
        gate_wT=_tr(inp["L0_ffn_gate_w"]),
        gbias=np.ascontiguousarray(np.asarray(inp["L0_ffn_gate_bias"])[None, :]),
        w1T=_tr(inp["L0_ffn_w1"]), w3T=_tr(inp["L0_ffn_w3"]), w2T=_tr(inp["L0_ffn_w2"]),
        sw1T=_tr(inp["L0_ffn_sw1"]), sw3T=_tr(inp["L0_ffn_sw3"]), sw2T=_tr(inp["L0_ffn_sw2"]),
    )
    xfull = np.asarray(inp["L0_attn_x"])
    nc = build_core(dev=_DEV)
    in_maps = []
    for cc in range(8):
        m = dict(x=np.ascontiguousarray(xfull[cc]))
        m.update(w)
        m.update(consts)
        in_maps.append(m)
    res = run_bass_kernel_spmd(nc, in_maps, core_ids=list(range(8)))
    return np.stack([np.asarray(r["out"]) for r in res.results], axis=0)


# version 26
# speedup vs baseline: 1.1853x; 1.0034x over previous
import numpy as np
import ml_dtypes
from contextlib import ExitStack

import concourse.bass as bass
import concourse.mybir as mybir
from concourse.bass_utils import run_bass_kernel_spmd

F32 = mybir.dt.float32
BF16 = mybir.dt.bfloat16
AF = mybir.ActivationFunctionType
ALU = mybir.AluOpType
AX = mybir.AxisListType

T = 4096
D = 1024
NT = 32
NCH = 8
NH = 16
DH = 64
NG = 8
IA = 1024
IF = 2048
EA = 16
EF = 8

_DEV = False
_PHASES = (0, 1, 2, 4, 3, 5)
_LIM_P2 = NCH
_P4_NEXP = None
_P3_NEXP = None
_P2_PASSB = True
_P2_ROUTER = True
_P2_XASPILL = True
_P2_H2SPILL = True
_P2_BCAST = True
_LIM_P1G = NG
ACT_SET_LNEXP = 6

ENGINES = ("sync", "gpsimd", "vector", "scalar", "tensor")


class Sem:
    __slots__ = ("h", "n", "name")

    def __init__(self, h, name):
        self.h, self.n, self.name = h, 0, name


class Builder:
    def __init__(self, dev=False, phases=_PHASES):
        self.dev = dev
        self.phases = phases
        nc = self.nc = bass.Bass("TRN2", target_bir_lowering=False)
        self.es = ExitStack()
        self.ops = None

        def din(name, shape, dt=F32):
            return nc.dram_tensor(name, list(shape), dt, kind="ExternalInput").ap()

        self.x = din("x", (T, D))
        self.wqT = din("wqT", (D, D)); self.wkT = din("wkT", (D, D))
        self.wvT = din("wvT", (D, D)); self.woT = din("woT", (D, D))
        self.nw1 = din("nw1", (1, D)); self.nw2 = din("nw2", (1, D))
        self.router_wT = din("router_wT", (D, EA)); self.ebias = din("ebias", (1, EA))
        self.sgT = din("sgT", (D, IA)); self.suT = din("suT", (D, IA)); self.sdT = din("sdT", (IA, D))
        self.egT = din("egT", (EA, D, IA)); self.euT = din("euT", (EA, D, IA)); self.edT = din("edT", (EA, IA, D))
        self.gate_wT = din("gate_wT", (D, EF)); self.gbias = din("gbias", (1, EF))
        self.w1T = din("w1T", (EF, D, IF)); self.w3T = din("w3T", (EF, D, IF)); self.w2T = din("w2T", (EF, IF, D))
        self.sw1T = din("sw1T", (D, IF)); self.sw3T = din("sw3T", (D, IF)); self.sw2T = din("sw2T", (IF, D))
        self.cos_c = din("cos_c", (DH, T)); self.sin_c = din("sin_c", (DH, T))
        self.masks_c = din("masks_c", (4 * 128, 512))
        self.ones_f32 = din("ones_f32", (128, 128))
        self.ident_c = din("ident_c", (128, 128))

        self.out = nc.dram_tensor("out", [T, D], F32, kind="ExternalOutput").ap()
        sk = "ExternalOutput" if dev else "Internal"
        self.yz_d = nc.dram_tensor("yz_d", [128, NT * D], BF16, kind=sk).ap() if dev else None
        self.hb_d = nc.dram_tensor("hb_d", [128, 16 * 512], BF16, kind=sk).ap() if dev else None
        self.wa_d = nc.dram_tensor("wa_d", [128, 8 * 2048], BF16, kind=sk).ap() if dev else None
        self.xs_d = nc.dram_tensor("xs_d", [128, 8 * 512], BF16, kind=sk).ap() if dev else None
        self.gs_d = nc.dram_tensor("gs_d", [128, 512], BF16, kind=sk).ap() if dev else None
        self.sr_d = nc.dram_tensor("sr_d", [128, NT * D], BF16, kind=sk).ap() if dev else None
        self.xnT = nc.dram_tensor("xnT", [D, T], BF16, kind=sk).ap()
        self.xnrT = nc.dram_tensor("xnrT", [D, T], BF16, kind=sk).ap()
        self.ctxT = nc.dram_tensor("ctxT", [D, T], BF16, kind=sk).ap()
        self.h2nT = nc.dram_tensor("h2nT", [D, T], BF16, kind=sk).ap()
        self.xaT = nc.dram_tensor("xaT", [T, D], BF16, kind=sk).ap()

        self.sems = []
        for nm in ("SL", "SO", "SW", "SX", "ST", "SV", "SS", "SG"):
            s = Sem(self.es.enter_context(nc.semaphore(nm)), nm)
            setattr(self, nm, s)
            self.sems.append(s)

        sb = self.sb
        self.cols = sb("cols", (128, 144), F32)
        self.onesb = sb("onesb", (128, 128), BF16)
        self.onesf = sb("onesf", (128, 128), F32)
        self.identb = sb("identb", (128, 128), BF16)
        self.ebias_r = sb("ebias_r", (128, EA), F32)
        self.gbias_r = sb("gbias_r", (128, EF), F32)
        self.cwa = sb("cwa", (128, NT * EA), F32)
        self.cwf = sb("cwf", (128, NT * EF), F32)
        self.pA = self.ps("pA", (128, 1024))
        self.pB = self.ps("pB", (128, 1024))
        self.p0 = self.ps("p0", (128, 512))
        self.p1 = self.ps("p1", (128, 512))
        self.p2 = self.ps("p2", (128, 512))
        self.p3 = self.ps("p3", (128, 512))

        self.C_NW1, self.C_NW1Q, self.C_EPS6, self.C_EPS5 = 0, 8, 16, 17
        self.C_NW2 = 60

    def sb(self, name, shape, dt):
        es = self.phase_es if getattr(self, "phase_es", None) is not None else self.es
        return es.enter_context(self.nc.sbuf_tensor(name, list(shape), dt))

    def sb_persist(self, name, shape, dt):
        return self.es.enter_context(self.nc.sbuf_tensor(name, list(shape), dt))

    def ps(self, name, shape):
        return self.es.enter_context(self.nc.psum_tensor(name, list(shape), F32))

    def begin_phase(self):
        self.phase_es = ExitStack()
        self._hwm = {}
        self.ops = {e: [] for e in ENGINES}
        for eng in ENGINES:
            for s in self.sems:
                if s.n > 0:
                    self.wait(eng, s, s.n)

    def end_phase(self):
        ops = self.ops
        with self.nc.Block() as block:
            @block.sync
            def _(e):
                for fn in ops["sync"]:
                    fn(e)

            @block.gpsimd
            def _(e):
                for fn in ops["gpsimd"]:
                    fn(e)

            @block.vector
            def _(e):
                for fn in ops["vector"]:
                    fn(e)

            @block.scalar
            def _(e):
                for fn in ops["scalar"]:
                    fn(e)

            @block.tensor
            def _(e):
                for fn in ops["tensor"]:
                    fn(e)
        self.ops = None
        self.phase_es.close()
        self.phase_es = None

    def emit(self, eng, fn):
        self.ops[eng].append(fn)

    def wait(self, eng, sem, val):
        if not val or val <= 0:
            return
        key = (eng, sem.name)
        if self._hwm.get(key, -1) >= val:
            return
        self._hwm[key] = val
        self.emit(eng, lambda e, s=sem.h, v=val: e.wait_ge(s, v))

    def dma(self, eng, sem, dst, src, **kw):
        sem.n += 16
        v = sem.n
        self.emit(eng, lambda e, d=dst, s=src, sm=sem.h, kw=kw: e.dma_start(d, s, **kw).then_inc(sm, 16))
        return v

    def dmat(self, dst, src):
        self.SX.n += 16
        v = self.SX.n
        self.emit("scalar", lambda e, d=dst, s=src: e.dma_start(d, s, transpose=True).then_inc(self.SX.h, 16))
        return v

    def op(self, eng, sem, f):
        if sem is None:
            self.emit(eng, f)
            return None
        sem.n += 1
        v = sem.n
        self.emit(eng, lambda e, f=f, s=sem.h: f(e).then_inc(s, 1))
        return v

    def drain(self, eng):
        self.emit(eng, lambda e: e.drain())

    def preload(self, set_id):
        pass

    def chunk_load(self, dstbuf, src, c, half_elems=512):
        vr = None
        for h in range(2):
            vr = self.dma("sync", self.SL,
                          dstbuf[:, h * 4 * half_elems:(h + 1) * 4 * half_elems]
                          .rearrange("p (a n) -> p a n", n=half_elems),
                          src.rearrange("(a p) m -> p a m", p=128)[:, 4 * h:4 * h + 4,
                                                                  c * half_elems:(c + 1) * half_elems])
        return vr

    def build(self):
        self.startup()
        if 0 in self.phases:
            self.p0_layernorm()
        if 1 in self.phases:
            self.p1_attention()
        if 2 in self.phases:
            self.p2_wo_norm2()
        if 4 in self.phases:
            self.p4_ffn()
        if 3 in self.phases:
            self.p3_amoe()
        if 5 in self.phases:
            self.p5_combine()
        self.es.close()
        return self.nc

    def startup(self):
        self.begin_phase()
        b = self
        b.preload(ACT_SET_LNEXP)
        vof = b.dma("sync", b.SL, b.onesf[:, :], b.ones_f32)
        b.dma("sync", b.SL, b.cols[:, b.C_NW1:b.C_NW1 + 8], b.nw1.rearrange("o (a p) -> p (o a)", p=128),
              allow_slow_non_contiguous=True)
        b.dma("sync", b.SL, b.cols[:, b.C_NW2:b.C_NW2 + 8], b.nw2.rearrange("o (a p) -> p (o a)", p=128),
              allow_slow_non_contiguous=True)
        eb_row = b.sb("eb_row", (1, EA), F32)
        gb_row = b.sb("gb_row", (1, EF), F32)
        b.dma("sync", b.SL, eb_row[0:1, :], b.ebias)
        v = b.dma("sync", b.SL, gb_row[0:1, :], b.gbias)
        eb_hl = b.sb("eb_hl", (1, 2 * EA), BF16)
        gb_hl = b.sb("gb_hl", (1, 2 * EF), BF16)
        identf = b.sb("identf", (128, 128), F32)
        vid = b.dma("sync", b.SL, identf[:, :], b.ident_c)
        b.wait("vector", b.SL, v)
        b.wait("vector", b.SL, vof)
        b.wait("vector", b.SL, vid)
        b.emit("vector", lambda e: e.tensor_copy(b.onesb[:, :], b.onesf[:, :]))
        b.drain("vector")
        b.emit("vector", lambda e: e.tensor_copy(b.identb[:, :], identf[:, :]))
        b.drain("vector")
        b.emit("vector", lambda e: e.tensor_copy(eb_hl[0:1, 0:EA], eb_row[0:1, :]))
        b.emit("vector", lambda e: e.tensor_copy(gb_hl[0:1, 0:EF], gb_row[0:1, :]))
        b.drain("vector")
        b.emit("vector", lambda e: e.tensor_tensor(eb_hl[0:1, EA:2 * EA], eb_row[0:1, :], eb_hl[0:1, 0:EA], op=ALU.subtract))
        vbl = b.op("vector", b.SV, lambda e: e.tensor_tensor(gb_hl[0:1, EF:2 * EF], gb_row[0:1, :], gb_hl[0:1, 0:EF], op=ALU.subtract))
        b.drain("vector")
        b.wait("tensor", b.SV, vbl)
        b.emit("tensor", lambda e: e.matmul(b.pA[:, 0:EA], b.onesb[0:1, 0:128], eb_hl[0:1, 0:EA], start=True, stop=False))
        b.emit("tensor", lambda e: e.matmul(b.pA[:, 0:EA], b.onesb[0:1, 0:128], eb_hl[0:1, EA:2 * EA], start=False, stop=True))
        b.emit("tensor", lambda e: e.matmul(b.pB[:, 0:EF], b.onesb[0:1, 0:128], gb_hl[0:1, 0:EF], start=True, stop=False))
        vbb = b.op("tensor", b.ST, lambda e: e.matmul(b.pB[:, 0:EF], b.onesb[0:1, 0:128], gb_hl[0:1, EF:2 * EF], start=False, stop=True))
        b.wait("scalar", b.ST, vbb)
        b.emit("scalar", lambda e: e.copy(b.ebias_r[:, :], b.pA[:, 0:EA]))
        b.op("scalar", b.SS, lambda e: e.copy(b.gbias_r[:, :], b.pB[:, 0:EF]))
        b.emit("vector", lambda e: e.memset(b.cols[:, b.C_EPS6:b.C_EPS6 + 1], 1e-6))
        b.emit("vector", lambda e: e.memset(b.cols[:, b.C_EPS5:b.C_EPS5 + 1], 1e-5))
        b.drain("vector")
        b.op("vector", b.SV, lambda e: e.tensor_scalar(
            b.cols[:, b.C_NW1Q:b.C_NW1Q + 8], b.cols[:, b.C_NW1:b.C_NW1 + 8], 0.125, None, op0=ALU.mult))
        self.end_phase()

    def p0_layernorm(self):
        b = self
        self.begin_phase()
        BT = 8
        xin = [b.sb("p0_x0", (128, BT * D), F32), b.sb("p0_x1", (128, BT * D), F32)]
        sqo = b.sb("p0_sqo", (128, D), F32)
        xn_tm = [b.sb("p0_n0", (128, D), BF16), b.sb("p0_n1", (128, D), BF16)]
        xr_tm = [b.sb("p0_r0", (128, D), BF16), b.sb("p0_r1", (128, D), BF16)]
        stg = [b.sb("p0_s0", (128, 2 * D), BF16), b.sb("p0_s1", (128, 2 * D), BF16)]
        c = b.cols
        CO = {0: (20, 28, 36, 44, 52, 68, 76), 1: (84, 92, 100, 108, 116, 124, 132)}
        NB = NT // BT
        last = {}
        vxd = {}

        def load(blk):
            bi = blk % 2
            if ("blk", bi) in last:
                b.wait("sync", b.SV, last[("blk", bi)])
                b.wait("sync", b.SS, last[("sq", bi)])
            vxd[blk] = b.dma("sync", b.SL, xin[bi][:, :].rearrange("p (a n) -> p a n", n=D),
                             b.x.rearrange("(a p) n -> p a n", p=128)[:, blk * BT:blk * BT + BT, :])

        def stats(blk):
            bi = blk % 2
            CS, CQ, CM, CV, CR, CA, CX = CO[bi]
            vx = vxd[blk]
            b.wait("vector", b.SL, vx)
            b.wait("scalar", b.SL, vx)
            if ("chain", bi) in last:
                b.wait("vector", b.SS, last[("chain", bi)])
            for j in range(BT):
                b.emit("vector", lambda e, j=j, bi=bi: e.tensor_reduce(
                    c[:, CS + j:CS + j + 1], xin[bi][:, j * D:(j + 1) * D], AX.X, ALU.add))
            b.drain("vector")
            b.emit("vector", lambda e: e.tensor_scalar(c[:, CM:CM + BT], c[:, CS:CS + BT], 1.0 / D, None, op0=ALU.mult))
            b.drain("vector")
            vsq = None
            for j in range(BT):
                vsq = b.op("scalar", b.SS, lambda e, j=j, bi=bi: e.activation(
                    sqo[:, :], xin[bi][:, j * D:(j + 1) * D], AF.Square, accum_out=c[:, CQ + j:CQ + j + 1]))
            last[("sq", bi)] = vsq
            b.wait("vector", b.SS, vsq)
            b.emit("vector", lambda e: e.tensor_tensor(c[:, CA:CA + BT], c[:, CM:CM + BT], c[:, CM:CM + BT], op=ALU.mult))
            b.drain("vector")
            vvar = b.op("vector", b.SV, lambda e: e.scalar_tensor_tensor(
                c[:, CV:CV + BT], c[:, CQ:CQ + BT], 1.0 / D, c[:, CA:CA + BT], op0=ALU.mult, op1=ALU.subtract))
            b.drain("vector")
            b.wait("scalar", b.SV, vvar)
            b.emit("scalar", lambda e: e.activation(c[:, CA:CA + BT], c[:, CV:CV + BT], AF.Ln, bias=c[:, b.C_EPS5:b.C_EPS5 + 1], scale=1.0))
            b.drain("scalar")
            b.emit("scalar", lambda e: e.activation(c[:, CR:CR + BT], c[:, CA:CA + BT], AF.Exp, scale=-0.5))
            b.drain("scalar")
            vs2 = b.op("scalar", b.SS, lambda e: e.activation(c[:, CA:CA + BT], c[:, CR:CR + BT], AF.Square))
            b.wait("vector", b.SS, vs2)
            vv2 = b.op("vector", b.SV, lambda e: e.tensor_tensor(c[:, CA:CA + BT], c[:, CV:CV + BT], c[:, CA:CA + BT], op=ALU.mult))
            b.drain("vector")
            b.wait("scalar", b.SV, vv2)
            b.emit("scalar", lambda e: e.activation(c[:, CA:CA + BT], c[:, CA:CA + BT], AF.Ln, bias=c[:, b.C_EPS6:b.C_EPS6 + 1], scale=1.0))
            b.drain("scalar")
            vs3 = b.op("scalar", b.SS, lambda e: e.activation(c[:, CA:CA + BT], c[:, CA:CA + BT], AF.Exp, scale=-0.5))
            b.wait("vector", b.SS, vs3)
            b.op("vector", b.SV, lambda e: e.tensor_tensor(c[:, CX:CX + BT], c[:, CR:CR + BT], c[:, CA:CA + BT], op=ALU.mult))
            b.drain("vector")
            last[("chain", bi)] = vs3

        load(0)
        stats(0)
        for blk in range(NB):
            bi = blk % 2
            t0 = blk * BT
            CS, CQ, CM, CV, CR, CA, CX = CO[bi]
            if blk + 1 < NB:
                load(blk + 1)
            vxn_last = None
            for j in range(BT):
                if j == 4 and blk + 1 < NB:
                    stats(blk + 1)
                t = t0 + j
                pi = t % 2
                if ("tp", pi) in last:
                    b.wait("vector", b.ST, last[("tp", pi)])
                b.emit("vector", lambda e, j=j, bi=bi, pi=pi, CM=CM, CR=CR: e.tensor_scalar(
                    xn_tm[pi][:, :], xin[bi][:, j * D:(j + 1) * D],
                    c[:, CM + j:CM + j + 1], c[:, CR + j:CR + j + 1], op0=ALU.subtract, op1=ALU.mult))
                vxn = b.op("vector", b.SV, lambda e, j=j, bi=bi, pi=pi, CM=CM, CX=CX: e.tensor_scalar(
                    xr_tm[pi][:, :], xin[bi][:, j * D:(j + 1) * D],
                    c[:, CM + j:CM + j + 1], c[:, CX + j:CX + j + 1], op0=ALU.subtract, op1=ALU.mult))
                b.drain("vector")
                vxn_last = vxn
                if "ev" in last:
                    b.wait("tensor", b.SS, last["ev"])
                b.wait("tensor", b.SV, vxn)
                vtp = None
                for k in range(8):
                    pd = (b.p0, b.p1)[k // 4]
                    vtp = b.op("tensor", b.ST, lambda e, k=k, pd=pd, pi=pi: e.matmul(
                        pd[:, (k % 4) * 128:(k % 4 + 1) * 128], xn_tm[pi][:, k * 128:(k + 1) * 128],
                        b.identb[:, :], start=True, stop=True))
                for k in range(8):
                    pd = (b.p2, b.p3)[k // 4]
                    vtp = b.op("tensor", b.ST, lambda e, k=k, pd=pd, pi=pi: e.matmul(
                        pd[:, (k % 4) * 128:(k % 4 + 1) * 128], xr_tm[pi][:, k * 128:(k + 1) * 128],
                        b.identb[:, :], start=True, stop=True))
                last[("tp", pi)] = vtp
                if ("st", pi) in last:
                    b.wait("scalar", b.SO, last[("st", pi)])
                b.wait("scalar", b.ST, vtp)
                vev = None
                for q, pd in enumerate((b.p0, b.p1, b.p2, b.p3)):
                    vev = b.op("scalar", b.SS, lambda e, q=q, pd=pd, pi=pi: e.copy(
                        stg[pi][:, q * 512:(q + 1) * 512], pd[:, :]))
                last["ev"] = vev
                b.wait("sync", b.SS, vev)
                for (dst, off) in ((b.xnT, 0), (b.xnrT, D)):
                    for h in range(2):
                        vsp = b.dma("sync", b.SO,
                                    dst.rearrange("(a p) m -> p a m", p=128)[:, 4 * h:4 * h + 4, t * 128:(t + 1) * 128],
                                    stg[pi][:, off + h * 512:off + (h + 1) * 512].rearrange("p (a n) -> p a n", n=128))
                last[("st", pi)] = vsp
            last[("blk", bi)] = vxn_last
        self.end_phase()

    def p1_attention(self):
        b = self
        self.begin_phase()
        tabs = b.sb("p1_tabs", (DH, 2 * T), F32)
        masks = b.sb("p1_masks", (128, 4 * 512), BF16)
        masksf = b.sb("p1_masksf", (128, 4 * 512), F32)
        qg = b.sb("p1_qg", (128, T), BF16)
        kg = b.sb("p1_kg", (128, T), BF16)
        vg = b.sb("p1_vg", (128, NT * 130), BF16)
        vtg = b.sb("p1_vtg", (128, T), BF16)
        xs = [b.sb("p1_xs0", (128, 8 * 512), BF16), b.sb("p1_xs1", (128, 8 * 512), BF16)]
        wraw = b.sb("p1_wraw", (128, 3 * 1024), BF16)
        weff = b.sb("p1_weff", (128, 3 * 1024), BF16)
        pb = b.sb("p1_pb", (128, 4 * 512), BF16)
        t1s = b.sb("p1_t1", (128, 512), F32)
        t2s = b.sb("p1_t2", (128, 512), F32)
        rbs = b.sb("p1_rb", (64, 512), F32)
        rls = b.sb("p1_rl", (128, 2048), F32)
        rlhl = b.sb("p1_rlhl", (128, 2048), BF16)
        cst = b.sb("p1_cst", (128, 2 * 512), BF16)
        vstg = b.sb("p1_vstg", (128, 2 * 512), BF16)

        dma, emit, op, wait, drain = b.dma, b.emit, b.op, b.wait, b.drain
        b.preload(ACT_SET_LNEXP)
        dma("sync", b.SL, tabs[:, 0:T], b.cos_c)
        dma("sync", b.SL, tabs[:, T:2 * T], b.sin_c)
        v_tabs = dma("sync", b.SL, masksf[:, :].rearrange("p (a n) -> p a n", n=512),
                     b.masks_c.rearrange("(a p) n -> p a n", p=128))
        wait("vector", b.SL, v_tabs)
        emit("vector", lambda e: e.tensor_copy(masks[:, :], masksf[:, :]))
        drain("vector")
        emit("vector", lambda e: e.memset(
            vg[:, :].rearrange("p (t h n) -> p t h n", h=2, n=65)[:, :, :, 64:65], 1.0))
        drain("vector")

        prev = {}
        for g in range(_LIM_P1G):
            if g > 0:
                wait("gpsimd", b.ST, prev["qkv_done"])
            vw = None
            for j, W in enumerate((b.wqT, b.wkT, b.wvT)):
                for h in range(2):
                    vw = dma("gpsimd", b.SW,
                             wraw[:, j * 1024 + h * 512:j * 1024 + (h + 1) * 512].rearrange("p (a n) -> p a n", n=128),
                             W[:, g * 128:(g + 1) * 128].rearrange("(a p) n -> p a n", p=128)[:, 4 * h:4 * h + 4, :])
            wait("gpsimd", b.SW, vw)
            vfold = None
            for j in range(3):
                ccol = b.C_NW1Q if j == 0 else b.C_NW1
                for dt in range(8):
                    vfold = b.op("gpsimd", b.SG, lambda e, j=j, dt=dt, ccol=ccol: e.tensor_scalar(
                        weff[:, j * 1024 + dt * 128:j * 1024 + (dt + 1) * 128],
                        wraw[:, j * 1024 + dt * 128:j * 1024 + (dt + 1) * 128],
                        b.cols[:, ccol + dt:ccol + dt + 1], None, op0=ALU.mult))
            if "attn_done" in prev:
                wait("vector", b.ST, prev["attn_done"])
                wait("scalar", b.ST, prev["attn_done"])
            ropeq = {}; ropek = {}; vsec = {}; vcpc = {}
            for c in range(NCH):
                ci = c % 2
                if ("xs", ci) in prev:
                    wait("sync", b.ST, prev[("xs", ci)])
                vxs = b.chunk_load(xs[ci], b.xnrT, c)
                wait("tensor", b.SL, vxs)
                wait("tensor", b.SG, vfold)
                if ("q_ps", ci) in prev:
                    wait("tensor", b.SV, prev[("q_ps", ci)])
                if ("k_ps", ci) in prev:
                    wait("tensor", b.SV, prev[("k_ps", ci)])
                if ("v_ps", ci) in prev:
                    wait("tensor", b.SS, prev[("v_ps", ci)])
                if g > 0 and c == 0:
                    wait("tensor", b.SS, b.SS.n)
                    wait("tensor", b.SV, b.SV.n)
                q_ps = (b.p0, b.p1)[ci]; k_ps = (b.p2, b.p3)[ci]
                v_ps = b.pA[:, ci * 512:(ci + 1) * 512]
                for dt in range(8):
                    emit("tensor", lambda e, dt=dt, q_ps=q_ps, ci=ci: e.matmul(
                        q_ps[:, :], weff[:, dt * 128:(dt + 1) * 128],
                        xs[ci][:, dt * 512:(dt + 1) * 512], start=(dt == 0), stop=(dt == 7)))
                for dt in range(8):
                    emit("tensor", lambda e, dt=dt, k_ps=k_ps, ci=ci: e.matmul(
                        k_ps[:, :], weff[:, 1024 + dt * 128:1024 + (dt + 1) * 128],
                        xs[ci][:, dt * 512:(dt + 1) * 512], start=(dt == 0), stop=(dt == 7)))
                vmm = None
                for dt in range(8):
                    vmm = op("tensor", b.ST, lambda e, dt=dt, v_ps=v_ps, ci=ci: e.matmul(
                        v_ps, weff[:, 2048 + dt * 128:2048 + (dt + 1) * 128],
                        xs[ci][:, dt * 512:(dt + 1) * 512], start=(dt == 0), stop=(dt == 7)))
                prev[("xs", ci)] = vmm
                prev["qkv_done"] = vmm
                wait("vector", b.ST, vmm)
                for which, sp, dst in (("q", q_ps, qg), ("k", k_ps, kg)):
                    emit("vector", lambda e, sp=sp, c=c: e.tensor_tensor(
                        t1s[0:64, :], sp[0:64, :], tabs[:, c * 512:(c + 1) * 512], op=ALU.mult))
                    emit("vector", lambda e, sp=sp, c=c: e.tensor_tensor(
                        t1s[64:128, :], sp[64:128, :], tabs[:, c * 512:(c + 1) * 512], op=ALU.mult))
                    emit("vector", lambda e, sp=sp, c=c: e.tensor_tensor(
                        t2s[0:32, :], sp[32:64, :], tabs[0:32, T + c * 512:T + c * 512 + 512], op=ALU.mult))
                    emit("vector", lambda e, sp=sp, c=c: e.tensor_tensor(
                        t2s[32:64, :], sp[0:32, :], tabs[32:64, T + c * 512:T + c * 512 + 512], op=ALU.mult))
                    emit("vector", lambda e, sp=sp, c=c: e.tensor_tensor(
                        t2s[64:96, :], sp[96:128, :], tabs[0:32, T + c * 512:T + c * 512 + 512], op=ALU.mult))
                    emit("vector", lambda e, sp=sp, c=c: e.tensor_tensor(
                        t2s[96:128, :], sp[64:96, :], tabs[32:64, T + c * 512:T + c * 512 + 512], op=ALU.mult))
                    drain("vector")
                    emit("vector", lambda e, d=dst, c=c: e.tensor_tensor(
                        d[0:32, c * 512:(c + 1) * 512], t1s[0:32, :], t2s[0:32, :], op=ALU.subtract))
                    emit("vector", lambda e, d=dst, c=c: e.tensor_tensor(
                        d[32:64, c * 512:(c + 1) * 512], t1s[32:64, :], t2s[32:64, :], op=ALU.add))
                    emit("vector", lambda e, d=dst, c=c: e.tensor_tensor(
                        d[64:96, c * 512:(c + 1) * 512], t1s[64:96, :], t2s[64:96, :], op=ALU.subtract))
                    vlast = op("vector", b.SV, lambda e, d=dst, c=c: e.tensor_tensor(
                        d[96:128, c * 512:(c + 1) * 512], t1s[96:128, :], t2s[96:128, :], op=ALU.add))
                    drain("vector")
                    prev[("q_ps" if which == "q" else "k_ps", ci)] = vlast
                    if which == "q":
                        ropeq[c] = vlast
                    else:
                        ropek[c] = vlast
                wait("scalar", b.ST, vmm)
                vse = op("scalar", b.SS, lambda e, v_ps=v_ps, c=c: e.copy(
                    vtg[:, c * 512:(c + 1) * 512], v_ps))
                prev[("v_ps", ci)] = vse
                vsec[c] = vse

                def v_transpose(vc):
                    si = vc % 2
                    drain("scalar")
                    if ("vstg", si) in prev:
                        wait("scalar", b.SV, prev[("vstg", si)])
                    vtp = b.dmat(vstg[:, si * 512:(si + 1) * 512].rearrange("p (a n) -> p a n", n=128),
                                 vtg[:, vc * 512:(vc + 1) * 512])
                    wait("vector", b.SX, vtp)
                    vcp = None
                    for hh in range(2):
                        vcp = op("vector", b.SV, lambda e, si=si, vc=vc, hh=hh: e.tensor_copy(
                            vg[:, :].rearrange("p (t h n) -> p t h n", h=2, n=65)[:, 4 * vc:4 * vc + 4, hh, 0:64],
                            vstg[:, si * 512:(si + 1) * 512].rearrange("p (a h n) -> p a h n", h=2, n=64)[:, :, hh, :]))
                    drain("vector")
                    prev[("vstg", si)] = vcp
                    vcpc[vc] = vcp

                if c > 0:
                    v_transpose(c - 1)
                if c == NCH - 1:
                    v_transpose(c)
            prev[("sbank", 0)] = (b.SV, ropeq[6])
            prev[("sbank", 1)] = (b.SV, ropeq[7])
            prev[("sbank", 2)] = (b.SV, ropek[6])
            prev[("ctxv", 0, 0)] = (b.SS, vsec[6])
            prev[("ctxv", 0, 1)] = (b.SS, vsec[7])
            sstep = prev.get("sstep", 0)
            pend = None

            def waitv(eng, entry):
                if isinstance(entry, tuple):
                    wait(eng, entry[0], entry[1])
                else:
                    wait(eng, b.SS, entry)

            def plan_evac_part2(pd):
                vctx = None
                for hh in range(2):
                    wait("tensor", b.SV, pd["vhl"][hh])
                    if "rb" in prev:
                        wait("tensor", b.SS, prev["rb"])
                    emit("tensor", lambda e, hh=hh: e.matmul(
                        b.p3[0:64, :], b.onesb[64:65, 0:64], rlhl[64:65, hh * 1024:hh * 1024 + 512],
                        start=True, stop=False))
                    vbc = op("tensor", b.ST, lambda e, hh=hh: e.matmul(
                        b.p3[0:64, :], b.onesb[64:65, 0:64], rlhl[64:65, hh * 1024 + 512:hh * 1024 + 1024],
                        start=False, stop=True))
                    wait("scalar", b.ST, vbc)
                    vrb = op("scalar", b.SS, lambda e: e.copy(rbs[:, :], b.p3[0:64, :]))
                    prev["rb"] = vrb
                    wait("vector", b.SS, vrb)
                    if ("cst", pd["par"]) in prev:
                        wait("vector", b.SO, prev[("cst", pd["par"])])
                    vctx = op("vector", b.SV, lambda e, hh=hh, pd=pd: e.tensor_tensor(
                        cst[hh * 64:(hh + 1) * 64, pd["par"] * 512:pd["par"] * 512 + 512],
                        pd["ctxb"][hh][0:64, :], rbs[:, :], op=ALU.mult))
                    drain("vector")
                    prev[("ctxv", hh, pd["par"])] = vctx
                wait("sync", b.SV, vctx)
                vsp = b.dma("sync", b.SO,
                            b.ctxT[g * 128:(g + 1) * 128, pd["qc"] * 512:(pd["qc"] + 1) * 512],
                            cst[:, pd["par"] * 512:pd["par"] * 512 + 512])
                prev[("cst", pd["par"])] = vsp

            for qc in range(NCH):
                nkt = 4 * qc + 4
                par = qc % 2
                ctxb = {0: b.pA[:, par * 512:(par + 1) * 512],
                        1: b.pB[:, par * 512:(par + 1) * 512]}
                vexp_q = [[None] * nkt for _ in range(2)]
                vpv_last = [None, None]
                for kt in range(nkt + 1):
                    if kt < nkt:
                        for hh in range(2):
                            j = sstep % 3
                            s_ps = (b.p0, b.p1, b.p2)[j]
                            if ("sbank", j) in prev:
                                waitv("tensor", prev[("sbank", j)])
                            wait("tensor", b.SV, ropeq[qc])
                            wait("tensor", b.SV, ropek[kt // 4])
                            vs_mm = op("tensor", b.ST, lambda e, hh=hh, kt=kt, qc=qc, s_ps=s_ps: e.matmul(
                                s_ps[:, :], kg[hh * 64:(hh + 1) * 64, kt * 128:(kt + 1) * 128],
                                qg[hh * 64:(hh + 1) * 64, qc * 512:(qc + 1) * 512],
                                start=True, stop=True, tile_position=(hh * 64, 0)))
                            a = kt - 4 * qc
                            pslot = sstep % 4
                            pbs = pb[:, pslot * 512:(pslot + 1) * 512]
                            if a >= 0:
                                wait("vector", b.ST, vs_mm)
                                vmask = op("vector", b.SV, lambda e, s_ps=s_ps, a=a: e.tensor_tensor(
                                    s_ps[:, :], s_ps[:, :], masks[:, a * 512:(a + 1) * 512], op=ALU.add))
                                wait("scalar", b.SV, vmask)
                            else:
                                wait("scalar", b.ST, vs_mm)
                            if ("pb", pslot) in prev:
                                wait("scalar", b.ST, prev[("pb", pslot)])
                            vexp = op("scalar", b.SS, lambda e, s_ps=s_ps, pbs=pbs: e.activation(
                                pbs, s_ps[:, :], AF.Exp))
                            prev[("sbank", j)] = vexp
                            vexp_q[hh][kt] = (vexp, pslot)
                            sstep += 1
                    if kt == 1 and pend is not None:
                        plan_evac_part2(pend)
                        pend = None
                    if kt >= 1:
                        ktp = kt - 1
                        for hh in range(2):
                            vexp, pslot = vexp_q[hh][ktp]
                            wait("tensor", b.SS, vexp)
                            wait("tensor", b.SV, vcpc[ktp // 4])
                            if ktp == 0 and ("ctxv", hh, par) in prev:
                                ent = prev[("ctxv", hh, par)]
                                if isinstance(ent, tuple):
                                    wait("tensor", ent[0], ent[1])
                                else:
                                    wait("tensor", b.SV, ent)
                            vpv = op("tensor", b.ST, lambda e, ktp=ktp, hh=hh, nkt=nkt, ctxb=ctxb, pslot=pslot: e.matmul(
                                ctxb[hh][0:65, :], vg[:, ktp * 130 + hh * 65:ktp * 130 + hh * 65 + 65],
                                pb[:, pslot * 512:(pslot + 1) * 512],
                                start=(ktp == 0), stop=(ktp == nkt - 1)))
                            prev[("pb", pslot)] = vpv
                            if ktp == nkt - 1:
                                vpv_last[hh] = vpv
                vhls = [None, None]
                for hh in range(2):
                    wait("scalar", b.ST, vpv_last[hh])
                    emit("scalar", lambda e, hh=hh, ctxb=ctxb: e.activation(
                        rls[64:65, hh * 1024:hh * 1024 + 512], ctxb[hh][64:65, :], AF.Ln))
                    drain("scalar")
                    vrl = op("scalar", b.SS, lambda e, hh=hh: e.activation(
                        rls[64:65, hh * 1024 + 512:hh * 1024 + 1024],
                        rls[64:65, hh * 1024:hh * 1024 + 512], AF.Exp, scale=-1.0))
                    wait("vector", b.SS, vrl)
                    emit("vector", lambda e, hh=hh: e.tensor_copy(
                        rlhl[64:65, hh * 1024:hh * 1024 + 512],
                        rls[64:65, hh * 1024 + 512:hh * 1024 + 1024]))
                    drain("vector")
                    vhl = op("vector", b.SV, lambda e, hh=hh: e.tensor_tensor(
                        rlhl[64:65, hh * 1024 + 512:hh * 1024 + 1024],
                        rls[64:65, hh * 1024 + 512:hh * 1024 + 1024],
                        rlhl[64:65, hh * 1024:hh * 1024 + 512], op=ALU.subtract))
                    drain("vector")
                    vhls[hh] = vhl
                pend = dict(qc=qc, par=par, ctxb=ctxb, vhl=vhls)
            plan_evac_part2(pend)
            pend = None
            prev["sstep"] = sstep
            prev["attn_done"] = b.ST.n
        self.end_phase()

    def p2_wo_norm2(self):
        b = self
        self.begin_phase()
        wo_sb = b.sb("p2_wo", (128, 8 * 1024), BF16)
        rw_sb = b.sb("p2_rw", (128, 8 * EA), BF16)
        ctx_s = [b.sb("p2_c0", (128, 8 * 512), BF16), b.sb("p2_c1", (128, 8 * 512), BF16)]
        xn_s = [b.sb("p2_x0", (128, 8 * 512), BF16), b.sb("p2_x1", (128, 8 * 512), BF16)]
        xa_scr = [b.sb("p2_xa0", (128, 8 * 512), BF16), b.sb("p2_xa1", (128, 8 * 512), BF16)]
        h2_scr = [b.sb("p2_h20", (128, 8 * 512), BF16), b.sb("p2_h21", (128, 8 * 512), BF16)]
        sqt = [b.sb("p2_sq0", (128, 512), BF16), b.sb("p2_sq1", (128, 512), BF16)]
        r2row = b.sb("p2_r2", (1, T), F32)
        r2hl = b.sb("p2_r2hl", (1, 1024), BF16)
        tstg = b.sb("p2_ts", (128, 2 * 512), BF16)
        lgt = b.sb("p2_lg", (128, 4 * EA), F32)
        gts = b.sb("p2_gt", (128, 4 * EA), F32)
        mk1 = b.sb("p2_mk1", (128, 4 * EA), F32)
        mk2 = b.sb("p2_mk2", (128, 4 * EA), F32)
        mskd = b.sb("p2_msk", (128, 4 * EA), F32)
        c = b.cols

        dma, emit, op, wait, drain = b.dma, b.emit, b.op, b.wait, b.drain
        for h in range(2):
            dma("gpsimd", b.SW, wo_sb[:, h * 4096:(h + 1) * 4096].rearrange("p (a n) -> p a n", n=1024),
                b.woT.rearrange("(a p) n -> p a n", p=128)[:, 4 * h:4 * h + 4, :])
        vw = dma("gpsimd", b.SW, rw_sb[:, :].rearrange("p (a n) -> p a n", n=EA),
                 b.router_wT.rearrange("(a p) n -> p a n", p=128))
        prev = {}

        def plan_passB(cc):
            ci = cc % 2
            vr2 = prev[("r2", cc)]
            wait("vector", b.SS, vr2)
            emit("vector", lambda e, cc=cc: e.tensor_copy(
                r2hl[0:1, 0:512], r2row[0:1, cc * 512:(cc + 1) * 512]))
            drain("vector")
            vhl = op("vector", b.SV, lambda e, cc=cc: e.tensor_tensor(
                r2hl[0:1, 512:1024], r2row[0:1, cc * 512:(cc + 1) * 512], r2hl[0:1, 0:512], op=ALU.subtract))
            drain("vector")
            wait("tensor", b.SV, vhl)
            if "r2b" in prev:
                wait("tensor", b.SV, prev["r2b"])
            emit("tensor", lambda e: e.matmul(
                b.p3[:, :], b.onesb[0:1, 0:128], r2hl[0:1, 0:512], start=True, stop=False))
            vbc = op("tensor", b.ST, lambda e: e.matmul(
                b.p3[:, :], b.onesb[0:1, 0:128], r2hl[0:1, 512:1024], start=False, stop=True))
            wait("vector", b.ST, vbc)
            vh2_last = None
            vtp = None
            for et in range(8):
                vh2 = op("vector", b.SV, lambda e, et=et, ci=ci: e.scalar_tensor_tensor(
                    h2_scr[ci][:, et * 512:(et + 1) * 512], xa_scr[ci][:, et * 512:(et + 1) * 512],
                    b.cols[:, b.C_NW2 + et:b.C_NW2 + et + 1], b.p3[:, :],
                    op0=ALU.mult, op1=ALU.mult))
                drain("vector")
                vh2_last = vh2
                wait("gpsimd", b.SV, vh2)
                dma("gpsimd", b.SW, b.h2nT[et * 128:(et + 1) * 128, cc * 512:(cc + 1) * 512],
                    h2_scr[ci][:, et * 512:(et + 1) * 512])
                si = et % 2
                if ("tstg", si) in prev:
                    wait("scalar", b.SO, prev[("tstg", si)])
                vtp = b.dmat(tstg[:, si * 512:(si + 1) * 512].rearrange("p (a n) -> p a n", n=128),
                             xa_scr[ci][:, et * 512:(et + 1) * 512])
                wait("sync", b.SX, vtp)
                xa_ap = bass.AP(b.xaT.tensor, (cc * 512) * D + et * 128,
                                [[D, 128], [128 * D, 4], [1, 128]])
                vso = b.dma("sync", b.SO, xa_ap,
                            tstg[:, si * 512:(si + 1) * 512].rearrange("p (a n) -> p a n", n=128))
                prev[("tstg", si)] = vso
            prev[("xa_done", ci)] = (vh2_last, vtp)
            prev["r2b"] = vh2_last
            wait("tensor", b.SV, vh2_last)
            M1, M2, NR, IV = 20, 24, 28, 32
            vlgs = []
            for tt in range(4):
                rps = b.pB[:, 0:EA] if tt % 2 == 0 else b.pB[:, 512:512 + EA]
                if ("rtA", tt % 2) in prev:
                    wait("tensor", b.SV, prev[("rtA", tt % 2)])
                vrt = None
                for dt in range(8):
                    vrt = op("tensor", b.ST, lambda e, dt=dt, tt=tt, ci=ci, rps=rps: e.matmul(
                        rps, h2_scr[ci][:, dt * 512 + tt * 128:dt * 512 + tt * 128 + 128],
                        rw_sb[:, dt * EA:(dt + 1) * EA], start=(dt == 0), stop=(dt == 7)))
                wait("vector", b.ST, vrt)
                if "sigA" in prev:
                    wait("vector", b.SS, prev["sigA"])
                vlg = op("vector", b.SV, lambda e, tt=tt, rps=rps: e.tensor_tensor(
                    lgt[:, tt * EA:(tt + 1) * EA], rps, b.ebias_r[:, :], op=ALU.add))
                prev[("rtA", tt % 2)] = vlg
                vlgs.append(vlg)
            drain("vector")
            wait("scalar", b.SV, vlgs[-1])
            vsg = op("scalar", b.SS, lambda e: e.activation(gts[:, :], lgt[:, :], AF.Sigmoid))
            prev["sigA"] = vsg
            wait("vector", b.SS, vsg)
            for tt in range(4):
                emit("vector", lambda e, tt=tt: e.tensor_reduce(
                    c[:, M1 + tt:M1 + tt + 1], gts[:, tt * EA:(tt + 1) * EA], AX.X, ALU.max))
            drain("vector")
            for tt in range(4):
                emit("vector", lambda e, tt=tt: e.tensor_scalar(
                    mk1[:, tt * EA:(tt + 1) * EA], gts[:, tt * EA:(tt + 1) * EA],
                    c[:, M1 + tt:M1 + tt + 1], None, op0=ALU.is_equal))
            drain("vector")
            for tt in range(4):
                emit("vector", lambda e, tt=tt: e.scalar_tensor_tensor(
                    mskd[:, tt * EA:(tt + 1) * EA], mk1[:, tt * EA:(tt + 1) * EA], -1e30,
                    gts[:, tt * EA:(tt + 1) * EA], op0=ALU.mult, op1=ALU.add))
            drain("vector")
            for tt in range(4):
                emit("vector", lambda e, tt=tt: e.tensor_reduce(
                    c[:, M2 + tt:M2 + tt + 1], mskd[:, tt * EA:(tt + 1) * EA], AX.X, ALU.max))
            drain("vector")
            for tt in range(4):
                emit("vector", lambda e, tt=tt: e.tensor_scalar(
                    mk2[:, tt * EA:(tt + 1) * EA], mskd[:, tt * EA:(tt + 1) * EA],
                    c[:, M2 + tt:M2 + tt + 1], None, op0=ALU.is_equal))
            emit("vector", lambda e: e.tensor_tensor(c[:, NR:NR + 4], c[:, M1:M1 + 4], c[:, M2:M2 + 4], op=ALU.add))
            drain("vector")
            emit("vector", lambda e: e.tensor_tensor(mk1[:, :], mk1[:, :], mk2[:, :], op=ALU.add))
            emit("vector", lambda e: e.tensor_scalar(c[:, NR:NR + 4], c[:, NR:NR + 4], 1e-9, None, op0=ALU.add))
            drain("vector")
            emit("vector", lambda e: e.tensor_tensor(mk1[:, :], mk1[:, :], gts[:, :], op=ALU.mult))
            emit("vector", lambda e: e.reciprocal(c[:, IV:IV + 4], c[:, NR:NR + 4]))
            drain("vector")
            for tt in range(4):
                ttg = cc * 4 + tt
                op("vector", b.SV, lambda e, tt=tt, ttg=ttg: e.tensor_scalar(
                    b.cwa[:, ttg * EA:(ttg + 1) * EA], mk1[:, tt * EA:(tt + 1) * EA],
                    c[:, IV + tt:IV + tt + 1], None, op0=ALU.mult))
            drain("vector")
            prev[("routers_done", ci)] = b.ST.n

        for cc in range(NCH):
            ci = cc % 2
            if ("ld", ci) in prev:
                wait("sync", b.ST, prev[("ld", ci)][0])
                wait("sync", b.SV, prev[("ld", ci)][1])
            b.chunk_load(ctx_s[ci], b.ctxT, cc)
            vld = b.chunk_load(xn_s[ci], b.xnT, cc)
            wait("tensor", b.SL, vld)
            wait("tensor", b.SW, vw)
            if ("xa_done", ci) in prev:
                wait("vector", b.ST, prev[("routers_done", ci)])
                wait("vector", b.SX, prev[("xa_done", ci)][1])
                wait("vector", b.SW, b.SW.n)
            vss = None
            vxa_last = None
            sq_pend = None
            for et in range(8):
                d_ps = (b.p0, b.p1)[et % 2]
                if ("d_ps", et % 2) in prev:
                    wait("tensor", b.SV, prev[("d_ps", et % 2)])
                vdm = None
                for dt in range(8):
                    vdm = op("tensor", b.ST, lambda e, dt=dt, et=et, d_ps=d_ps, ci=ci: e.matmul(
                        d_ps[:, :], wo_sb[:, dt * 1024 + et * 128:dt * 1024 + (et + 1) * 128],
                        ctx_s[ci][:, dt * 512:(dt + 1) * 512], start=(dt == 0), stop=(dt == 7)))
                if sq_pend is not None:
                    pet, pv = sq_pend
                    wait("tensor", b.SV, pv)
                    if pet == 0 and ("r2", cc - 1) in prev:
                        wait("tensor", b.SS, prev[("r2", cc - 1)])
                    vss = op("tensor", b.ST, lambda e, pet=pet: e.matmul(
                        b.p2[0:1, :], b.onesb[:, 0:1], sqt[pet % 2][:, :], start=(pet == 0), stop=False))
                    prev[("ssq", pet % 2)] = vss
                wait("vector", b.ST, vdm)
                vxa = op("vector", b.SV, lambda e, et=et, d_ps=d_ps, ci=ci: e.tensor_tensor(
                    xa_scr[ci][:, et * 512:(et + 1) * 512], d_ps[:, :],
                    xn_s[ci][:, et * 512:(et + 1) * 512], op=ALU.add))
                prev[("d_ps", et % 2)] = vxa
                vxa_last = vxa
                drain("vector")
                if ("ssq", et % 2) in prev:
                    wait("vector", b.ST, prev[("ssq", et % 2)])
                vsq = op("vector", b.SV, lambda e, et=et, ci=ci: e.tensor_tensor(
                    sqt[et % 2][:, :], xa_scr[ci][:, et * 512:(et + 1) * 512],
                    xa_scr[ci][:, et * 512:(et + 1) * 512], op=ALU.mult))
                drain("vector")
                sq_pend = (et, vsq)
            pet, pv = sq_pend
            wait("tensor", b.SV, pv)
            vss = op("tensor", b.ST, lambda e, pet=pet: e.matmul(
                b.p2[0:1, :], b.onesb[:, 0:1], sqt[pet % 2][:, :], start=False, stop=True))
            prev[("ssq", pet % 2)] = vss
            wait("scalar", b.ST, vss)
            emit("scalar", lambda e, cc=cc: e.activation(
                r2row[0:1, cc * 512:(cc + 1) * 512], b.p2[0:1, :], AF.Ln,
                bias=c[0:1, b.C_EPS6:b.C_EPS6 + 1], scale=1.0 / D))
            drain("scalar")
            vr2 = op("scalar", b.SS, lambda e, cc=cc: e.activation(
                r2row[0:1, cc * 512:(cc + 1) * 512], r2row[0:1, cc * 512:(cc + 1) * 512], AF.Exp, scale=-0.5))
            prev[("r2", cc)] = vr2
            prev[("ld", ci)] = (b.ST.n, vxa_last)
            if cc > 0:
                plan_passB(cc - 1)
        plan_passB(NCH - 1)
        self.end_phase()

    def _moe(self, tag, n_exp, inner, stream_dram, wu1, wu3, wdn, su1, su3, sdn,
             acc, cwbuf, cw_w, overwrite_first):
        b = self
        nit = inner // 128
        wA, wB, wC = self.wsA, self.wsB, self.wsC
        xs, hb, gsc = self.ms_xs, self.ms_hb, self.ms_gs
        dma, emit, op, wait, drain = b.dma, b.emit, b.op, b.wait, b.drain
        prev = self.ms_prev

        def load_w(which, e_idx):
            if e_idx == -1:
                up1, up3, dn = su1, su3, sdn
            else:
                up1, up3, dn = wu1[e_idx], wu3[e_idx], wdn[e_idx]
            v = None
            if which == "A" or which == "B":
                W = up1 if which == "A" else up3
                slot = wA if which == "A" else wB
                for h in range(2):
                    v = dma("gpsimd", b.SW,
                            slot[:, h * 4 * inner:(h + 1) * 4 * inner].rearrange("p (a n) -> p a n", n=inner),
                            W.rearrange("(a p) n -> p a n", p=128)[:, 4 * h:4 * h + 4, :])
            else:
                nh = max(1, nit // 4)
                step = nit // nh
                for h in range(nh):
                    v = dma("gpsimd", b.SW,
                            wC[:, h * step * 1024:(h + 1) * step * 1024].rearrange("p (a n) -> p a n", n=1024),
                            dn.rearrange("(a p) n -> p a n", p=128)[:, step * h:step * (h + 1), :])
            return v

        exps = [-1] + list(range(n_exp))
        lim = _P4_NEXP if tag == "p4" else _P3_NEXP
        if lim is not None:
            exps = exps[:lim]
        wait("tensor", b.SV, b.SV.n)
        wait("tensor", b.SS, b.SS.n)
        vA = load_w("A", exps[0]); vB = load_w("B", exps[0]); vC = load_w("C", exps[0])
        for xi, eidx in enumerate(exps):
            for tc in range(NCH):
                ci = tc % 2
                if (tag, "xs", ci) in prev:
                    wait("sync", b.ST, prev[(tag, "xs", ci)])
                vxs = b.chunk_load(xs[ci], stream_dram, tc)
                wait("tensor", b.SL, vxs)
                if tc == 0:
                    wait("tensor", b.SW, vA)
                if (tag, "down_st") in prev:
                    wait("vector", b.ST, prev[(tag, "down_st")])
                for it in range(nit):
                    g_ps = (b.p0, b.p1)[it % 2]
                    u_ps = (b.p2, b.p3)[it % 2]
                    if (tag, "g_ps", it % 2) in prev:
                        wait("tensor", b.SS, prev[(tag, "g_ps", it % 2)])
                    vgm = None
                    for dt in range(8):
                        vgm = op("tensor", b.ST, lambda e, dt=dt, it=it, g_ps=g_ps, ci=ci: e.matmul(
                            g_ps[:, :], wA[:, dt * inner + it * 128:dt * inner + (it + 1) * 128],
                            xs[ci][:, dt * 512:(dt + 1) * 512], start=(dt == 0), stop=(dt == 7)))
                    if tc == 0 and it == 0:
                        wait("tensor", b.SW, vB)
                    if (tag, "u_ps", it % 2) in prev:
                        wait("tensor", b.SV, prev[(tag, "u_ps", it % 2)])
                    vum = None
                    for dt in range(8):
                        vum = op("tensor", b.ST, lambda e, dt=dt, it=it, u_ps=u_ps, ci=ci: e.matmul(
                            u_ps[:, :], wB[:, dt * inner + it * 128:dt * inner + (it + 1) * 128],
                            xs[ci][:, dt * 512:(dt + 1) * 512], start=(dt == 0), stop=(dt == 7)))
                    wait("scalar", b.ST, vgm)
                    gs = gsc[it % 2]
                    if (tag, "gs", it % 2) in prev:
                        wait("scalar", b.SV, prev[(tag, "gs", it % 2)])
                    vsi = op("scalar", b.SS, lambda e, g_ps=g_ps, gs=gs: e.activation(
                        gs[:, :], g_ps[:, :], AF.Silu))
                    prev[(tag, "g_ps", it % 2)] = vsi
                    wait("vector", b.ST, vum)
                    wait("vector", b.SS, vsi)
                    vh = op("vector", b.SV, lambda e, it=it, u_ps=u_ps, gs=gs: e.tensor_tensor(
                        hb[:, it * 512:(it + 1) * 512], gs[:, :], u_ps[:, :], op=ALU.mult))
                    drain("vector")
                    prev[(tag, "u_ps", it % 2)] = vh
                    prev[(tag, "gs", it % 2)] = vh
                prev[(tag, "xs", ci)] = b.ST.n
                if self.dev and tag == "p4" and xi == 0 and tc == 0:
                    vtap = b.SV.n
                    b.wait("sync", b.SV, vtap)
                    b.dma("sync", b.SO, self.hb_d, hb[:, 0:16 * 512])
                    b.dma("sync", b.SO, self.wa_d, wA[:, 0:8 * inner])
                    b.dma("sync", b.SO, self.xs_d, xs[ci][:, :])
                    b.dma("sync", b.SO, self.gs_d, gsc[0][:, :])
                if tc == NCH - 1 and xi + 1 < len(exps):
                    wait("gpsimd", b.ST, b.ST.n)
                    vA = load_w("A", exps[xi + 1])
                    vB = load_w("B", exps[xi + 1])
                if tc == 0:
                    wait("tensor", b.SW, vC)
                wait("tensor", b.SV, b.SV.n)
                for tt in range(4):
                    ttg = tc * 4 + tt
                    d_ps = (b.pA, b.pB)[tt % 2]
                    if (tag, "d_ps", tt % 2) in prev:
                        wait("tensor", b.SV, prev[(tag, "d_ps", tt % 2)])
                    vdm = None
                    for dch in range(2):
                        for it in range(nit):
                            vdm = op("tensor", b.ST, lambda e, tt=tt, dch=dch, it=it, d_ps=d_ps: e.matmul(
                                d_ps[:, dch * 512:(dch + 1) * 512],
                                hb[:, it * 512 + tt * 128:it * 512 + tt * 128 + 128],
                                wC[:, it * 1024 + dch * 512:it * 1024 + dch * 512 + 512],
                                start=(it == 0), stop=(it == nit - 1)))
                    wait("vector", b.ST, vdm)
                    accs = acc[:, ttg * D:(ttg + 1) * D]
                    if overwrite_first and xi == 0:
                        vac = op("vector", b.SV, lambda e, d_ps=d_ps, accs=accs: e.tensor_scalar(
                            accs, d_ps[:, :], 1.0, None, op0=ALU.mult))
                    elif eidx == -1:
                        vac = op("vector", b.SV, lambda e, d_ps=d_ps, accs=accs: e.scalar_tensor_tensor(
                            accs, d_ps[:, :], 1.0, accs, op0=ALU.mult, op1=ALU.add))
                    else:
                        cwcol = cwbuf[:, ttg * cw_w + eidx:ttg * cw_w + eidx + 1]
                        vac = op("vector", b.SV, lambda e, d_ps=d_ps, accs=accs, cwcol=cwcol: e.scalar_tensor_tensor(
                            accs, d_ps[:, :], cwcol, accs, op0=ALU.mult, op1=ALU.add))
                    drain("vector")
                    prev[(tag, "d_ps", tt % 2)] = vac
                prev[(tag, "down_st")] = b.ST.n
            if xi + 1 < len(exps):
                wait("gpsimd", b.ST, b.ST.n)
                vC = load_w("C", exps[xi + 1])

    def p4_ffn(self):
        b = self
        self.begin_phase()
        self.yz = b.sb_persist("yz", (128, NT * D), BF16)
        self.wsA = b.sb("p4_wA", (128, 8 * IF), BF16)
        self.wsB = b.sb("p4_wB", (128, 8 * IF), BF16)
        self.wsC = b.sb("p4_wC", (128, 16 * D), BF16)
        self.ms_xs = [b.sb("p4_xs0", (128, 8 * 512), BF16), b.sb("p4_xs1", (128, 8 * 512), BF16)]
        self.ms_hb = b.sb("p4_hb", (128, 16 * 512), BF16)
        self.ms_gs = [b.sb("p4_g0", (128, 512), BF16), b.sb("p4_g1", (128, 512), BF16)]
        self.ms_prev = {}
        gw_sb = b.sb("p4_gw", (128, 8 * EF), BF16)
        exs = b.sb("p4_ex", (128, EF), F32)
        sco = b.sb("p4_sc", (128, EF), F32)
        bsd = b.sb("p4_bs", (128, EF), F32)
        mk1 = b.sb("p4_m1", (128, EF), F32)
        mk2 = b.sb("p4_m2", (128, EF), F32)
        mskd = b.sb("p4_mk", (128, EF), F32)
        c = b.cols
        mxc = c[:, 50:51]; nmx = c[:, 51:52]; smc = c[:, 52:53]; rsc = c[:, 53:54]
        m1c = c[:, 54:55]; m2c = c[:, 55:56]

        dma, emit, op, wait, drain = b.dma, b.emit, b.op, b.wait, b.drain
        vgw = dma("gpsimd", b.SW, gw_sb[:, :].rearrange("p (a n) -> p a n", n=EF),
                  b.gate_wT.rearrange("(a p) n -> p a n", p=128))
        xs = self.ms_xs
        prevR = {}
        for cc in range(NCH):
            ci = cc % 2
            if ("xs", ci) in prevR:
                wait("sync", b.ST, prevR[("xs", ci)])
            vxs = b.chunk_load(xs[ci], b.xnT, cc)
            wait("tensor", b.SL, vxs)
            wait("tensor", b.SW, vgw)
            for tt in range(4):
                ttg = cc * 4 + tt
                if "pB" in prevR:
                    wait("tensor", b.SV, prevR["pBv"])
                    wait("tensor", b.SS, prevR["pBs"])
                vrt = None
                for dt in range(8):
                    vrt = op("tensor", b.ST, lambda e, dt=dt, tt=tt, ci=ci: e.matmul(
                        b.pB[:, 0:EF], xs[ci][:, dt * 512 + tt * 128:dt * 512 + tt * 128 + 128],
                        gw_sb[:, dt * EF:(dt + 1) * EF], start=(dt == 0), stop=(dt == 7)))
                wait("vector", b.ST, vrt)
                emit("vector", lambda e: e.tensor_reduce(mxc, b.pB[:, 0:EF], AX.X, ALU.max))
                drain("vector")
                vnm = op("vector", b.SV, lambda e: e.tensor_scalar(nmx, mxc, -1.0, None, op0=ALU.mult))
                prevR["pB"] = True
                prevR["pBv"] = vnm
                wait("scalar", b.SV, vnm)
                vex = op("scalar", b.SS, lambda e: e.activation(
                    exs[:, :], b.pB[:, 0:EF], AF.Exp, bias=nmx, scale=1.0))
                prevR["pBs"] = vex
                wait("vector", b.SS, vex)
                emit("vector", lambda e: e.tensor_reduce(smc, exs[:, :], AX.X, ALU.add))
                drain("vector")
                emit("vector", lambda e: e.reciprocal(rsc, smc))
                drain("vector")
                emit("vector", lambda e: e.tensor_scalar(sco[:, :], exs[:, :], rsc, None, op0=ALU.mult))
                drain("vector")
                emit("vector", lambda e: e.tensor_tensor(bsd[:, :], sco[:, :], b.gbias_r[:, :], op=ALU.add))
                drain("vector")
                emit("vector", lambda e: e.tensor_reduce(m1c, bsd[:, :], AX.X, ALU.max))
                drain("vector")
                emit("vector", lambda e: e.tensor_scalar(mk1[:, :], bsd[:, :], m1c, None, op0=ALU.is_equal))
                drain("vector")
                emit("vector", lambda e: e.scalar_tensor_tensor(
                    mskd[:, :], mk1[:, :], -1e30, bsd[:, :], op0=ALU.mult, op1=ALU.add))
                drain("vector")
                emit("vector", lambda e: e.tensor_reduce(m2c, mskd[:, :], AX.X, ALU.max))
                drain("vector")
                emit("vector", lambda e: e.tensor_scalar(mk2[:, :], mskd[:, :], m2c, None, op0=ALU.is_equal))
                drain("vector")
                emit("vector", lambda e: e.tensor_tensor(mk1[:, :], mk1[:, :], mk2[:, :], op=ALU.add))
                drain("vector")
                op("vector", b.SV, lambda e, ttg=ttg: e.tensor_tensor(
                    b.cwf[:, ttg * EF:(ttg + 1) * EF], mk1[:, :], sco[:, :], op=ALU.mult))
                drain("vector")
            prevR[("xs", ci)] = b.ST.n
        self.ms_prev[("p4", "xs", 0)] = b.ST.n
        self.ms_prev[("p4", "xs", 1)] = b.ST.n
        self._moe("p4", EF, IF, b.xnT,
                  [self.w1T[e] for e in range(EF)], [self.w3T[e] for e in range(EF)],
                  [self.w2T[e] for e in range(EF)],
                  self.sw1T, self.sw3T, self.sw2T, self.yz, b.cwf, EF, True)
        if self.dev:
            b.wait("sync", b.SV, b.SV.n)
            b.dma("sync", b.SO, self.yz_d, self.yz[:, :])
        self.end_phase()

    def p3_amoe(self):
        b = self
        self.begin_phase()
        self.sr = b.sb_persist("sr", (128, NT * D), BF16)
        self.wsA = b.sb("p3_wA", (128, 8 * IA), BF16)
        self.wsB = b.sb("p3_wB", (128, 8 * IA), BF16)
        self.wsC = b.sb("p3_wC", (128, 8 * D), BF16)
        self.ms_xs = [b.sb("p3_xs0", (128, 8 * 512), BF16), b.sb("p3_xs1", (128, 8 * 512), BF16)]
        self.ms_hb = b.sb("p3_hb", (128, 8 * 512), BF16)
        self.ms_gs = [b.sb("p3_g0", (128, 512), BF16), b.sb("p3_g1", (128, 512), BF16)]
        self.ms_prev = {}
        vin = None
        for q in range(8):
            vin = b.dma("sync", b.SL, self.sr[:, q * 4 * D:(q + 1) * 4 * D].rearrange("p (t n) -> p t n", n=D),
                        b.xaT.rearrange("(t p) n -> p t n", p=128)[:, 4 * q:4 * q + 4, :])
        b.wait("vector", b.SL, vin)
        self._moe("p3", EA, IA, b.h2nT,
                  [self.egT[e] for e in range(EA)], [self.euT[e] for e in range(EA)],
                  [self.edT[e] for e in range(EA)],
                  self.sgT, self.suT, self.sdT, self.sr, b.cwa, EA, False)
        if self.dev:
            b.wait("sync", b.SV, b.SV.n)
            b.dma("sync", b.SO, self.sr_d, self.sr[:, :])
        self.end_phase()

    def p5_combine(self):
        b = self
        self.begin_phase()
        xin = [b.sb("p5_x0", (128, D), F32), b.sb("p5_x1", (128, D), F32)]
        gsc = [b.sb("p5_g0", (128, D), F32), b.sb("p5_g1", (128, D), F32)]
        tsc = [b.sb("p5_t0", (128, D), F32), b.sb("p5_t1", (128, D), F32)]
        osb = [b.sb("p5_o0", (128, D), F32), b.sb("p5_o1", (128, D), F32)]
        dma, emit, op, wait, drain = b.dma, b.emit, b.op, b.wait, b.drain
        prev = {}
        for t in range(NT):
            pi = t % 2
            if ("vo", pi) in prev:
                wait("sync", b.SV, prev[("vo", pi)])
            vx = dma("sync", b.SL, xin[pi][:, :], b.x[t * 128:(t + 1) * 128, :])
            if ("vt", pi) in prev:
                wait("scalar", b.SV, prev[("vt", pi)])
            vg = op("scalar", b.SS, lambda e, t=t, pi=pi: e.activation(
                gsc[pi][:, :], self.yz[:, t * D:(t + 1) * D], AF.Silu))
            wait("vector", b.SS, vg)
            vt = op("vector", b.SV, lambda e, t=t, pi=pi: e.tensor_tensor(
                tsc[pi][:, :], self.sr[:, t * D:(t + 1) * D], gsc[pi][:, :], op=ALU.mult))
            drain("vector")
            prev[("vt", pi)] = vt
            wait("vector", b.SL, vx)
            if ("st", pi) in prev:
                wait("vector", b.SO, prev[("st", pi)])
            vo = op("vector", b.SV, lambda e, pi=pi: e.tensor_tensor(
                osb[pi][:, :], tsc[pi][:, :], xin[pi][:, :], op=ALU.add))
            drain("vector")
            prev[("vo", pi)] = vo
            wait("sync", b.SV, vo)
            vst = dma("sync", b.SO, b.out[t * 128:(t + 1) * 128, :], osb[pi][:, :])
            prev[("st", pi)] = vst
        b.wait("sync", b.SO, b.SO.n)
        self.end_phase()


def build_core(dev=False, phases=_PHASES):
    return Builder(dev=dev, phases=phases).build()



def _constants():
    pos = np.arange(T, dtype=np.float32)
    inv_freq = 1.0 / (10000.0 ** (np.arange(0, DH, 2, dtype=np.float32) / DH))
    freqs = np.concatenate([inv_freq, inv_freq])[:, None] * pos[None, :]
    cos_c = np.cos(freqs).astype(np.float32)
    sin_c = np.sin(freqs).astype(np.float32)
    masks = np.zeros((4, 128, 512), dtype=np.float32)
    for a in range(4):
        kk = a * 128 + np.arange(128)[:, None]
        qq = np.arange(512)[None, :]
        masks[a] = np.where(kk <= qq, 0.0, -1e9)
    masks_c = np.ascontiguousarray(masks.reshape(4 * 128, 512))
    ones = np.ones((128, 128), dtype=np.float32)
    return dict(cos_c=cos_c, sin_c=sin_c, masks_c=masks_c, ones_f32=ones,
                ident_c=np.eye(128, dtype=np.float32))


def _tr(a):
    return np.ascontiguousarray(np.swapaxes(np.asarray(a), -1, -2))


def kernel(**inp):
    consts = _constants()
    w = dict(
        wqT=_tr(inp["L0_attn_wq"]), wkT=_tr(inp["L0_attn_wk"]),
        wvT=_tr(inp["L0_attn_wv"]), woT=_tr(inp["L0_attn_wo"]),
        nw1=np.ascontiguousarray(np.asarray(inp["L0_attn_norm1_w"])[None, :]),
        nw2=np.ascontiguousarray(np.asarray(inp["L0_attn_norm2_w"])[None, :]),
        router_wT=_tr(inp["L0_attn_router_w"]),
        ebias=np.ascontiguousarray(np.asarray(inp["L0_attn_expert_bias"])[None, :]),
        sgT=_tr(inp["L0_attn_sg"]), suT=_tr(inp["L0_attn_su"]), sdT=_tr(inp["L0_attn_sd"]),
        egT=_tr(inp["L0_attn_eg"]), euT=_tr(inp["L0_attn_eu"]), edT=_tr(inp["L0_attn_ed"]),
        gate_wT=_tr(inp["L0_ffn_gate_w"]),
        gbias=np.ascontiguousarray(np.asarray(inp["L0_ffn_gate_bias"])[None, :]),
        w1T=_tr(inp["L0_ffn_w1"]), w3T=_tr(inp["L0_ffn_w3"]), w2T=_tr(inp["L0_ffn_w2"]),
        sw1T=_tr(inp["L0_ffn_sw1"]), sw3T=_tr(inp["L0_ffn_sw3"]), sw2T=_tr(inp["L0_ffn_sw2"]),
    )
    xfull = np.asarray(inp["L0_attn_x"])
    nc = build_core(dev=_DEV)
    in_maps = []
    for cc in range(8):
        m = dict(x=np.ascontiguousarray(xfull[cc]))
        m.update(w)
        m.update(consts)
        in_maps.append(m)
    res = run_bass_kernel_spmd(nc, in_maps, core_ids=list(range(8)))
    return np.stack([np.asarray(r["out"]) for r in res.results], axis=0)
